# version 08 (final)
# speedup vs baseline: 1.0619x; 1.0619x over previous
import contextlib
import math

import numpy as np

import concourse.bass as bass
import concourse.mybir as mybir
from concourse.bass_utils import run_bass_kernel_spmd

F32 = mybir.dt.float32
BF16 = mybir.dt.bfloat16
U8 = mybir.dt.uint8
I32 = mybir.dt.int32
AF = mybir.ActivationFunctionType
ALU = mybir.AluOpType
AX = mybir.AxisListType

NCORES = 8
D = 2048
L = 1024
CD = D // 128
LN_EPS = 1e-5

_TRACE_COUNTS = False
ENGINE_KEYS = ["sync", "gp", "v", "s", "t", "tA", "tB", "dgp", "ds", "cst"]


class _Counts(dict):

    def __init__(self, prog):
        super().__init__()
        self.prog = prog

    def __missing__(self, key):
        pk = self.prog.pref + key
        if pk in self:
            return self[pk]
        raise KeyError(key)


class Prog:

    def __init__(self, nc, es):
        self.nc = nc
        self.es = es
        self.alloc_es = es
        self.pref = ""
        self.cur = None
        self.cur_eng = None
        self.sems = {}
        self.counts = _Counts(self)
        for k in ENGINE_KEYS:
            self.sems[k] = es.enter_context(nc.semaphore("sem_" + k))
        self.all_keys = list(ENGINE_KEYS)
        self._names = set()
        self._alloc_cache = {}
        self.stateful = []

    @contextlib.contextmanager
    def stage_scope(self, prefix):
        old_pref, old_es = self.pref, self.alloc_es
        self.pref = prefix + "_"
        with contextlib.ExitStack() as ses:
            self.alloc_es = ses
            try:
                yield
            finally:
                self.pref, self.alloc_es = old_pref, old_es

    def stage_barrier(self):
        snap = dict(self.counts)
        for e in ("sync", "gp", "v", "s", "t"):
            for k, v in snap.items():
                if v and k != e:
                    self.wait(e, (k, v))

    def sb(self, name, shape, dtype=F32):
        name = self.pref + name
        assert name not in self._names, name
        self._names.add(name)
        return self.alloc_es.enter_context(self.nc.sbuf_tensor(name, shape,
                                                               dtype))

    def ps(self, name, shape, dtype=F32):
        name = self.pref + name
        assert name not in self._names, name
        self._names.add(name)
        return self.alloc_es.enter_context(self.nc.psum_tensor(name, shape,
                                                               dtype))

    def fm(self, name, nchunk=CD, l=L, dtype=F32):
        return [self.sb(f"{name}_{i}", [128, l], dtype) for i in range(nchunk)]

    def new_key(self, name):
        name = self.pref + name
        self.sems[name] = self.es.enter_context(self.nc.semaphore("sem_" + name))
        self.all_keys.append(name)
        self.counts[name] = 0
        return name

    def _reset_counts(self):
        for k in self.all_keys:
            self.counts[k] = 0
        self._tflip = False

    def engine_for(self, key):
        return {"sync": self.nc.sync, "gp": self.nc.gpsimd, "v": self.nc.vector,
                "s": self.nc.scalar, "t": self.nc.tensor}[key]

    def _k(self, name):
        pn = self.pref + name
        return pn if pn in self.counts else name

    AUTO_CHAIN = ("v", "s", "gp")

    def op(self, key, fn):
        if key in self.AUTO_CHAIN:
            return self.opi(key, fn)
        if self.cur == key:
            return fn(self.cur_eng)
        return None

    def opi(self, key, fn, n=1):
        key = self._k(key)
        ins = None
        if self.cur == key:
            if key in self.AUTO_CHAIN and self.counts[key]:
                self.cur_eng.wait_ge(self.sems[key], self.counts[key])
            ins = fn(self.cur_eng)
            ins.then_inc(self.sems[key], n)
        self.counts[key] += n
        return self.counts[key]

    def opk(self, eng_key, sem_key, fn, n=1):
        if eng_key == "t" and sem_key == "t":
            sem_key = "tA" if self._tflip else "tB"
            self._tflip = not self._tflip
            if self.cur == eng_key:
                ins = fn(self.cur_eng)
                ins.then_inc(self.sems[sem_key], n)
            self.counts[sem_key] += n
            return (sem_key, self.counts[sem_key])
        sem_key = self._k(sem_key)
        if self.cur == eng_key:
            ins = fn(self.cur_eng)
            ins.then_inc(self.sems[sem_key], n)
        self.counts[sem_key] += n
        return self.counts[sem_key]

    def dma(self, key, out, in_, **kw):
        qkey = key
        semkey = self._k(kw.pop("sem_key", None) or
                         {"sync": "sync", "gp": "dgp", "s": "ds"}[key])
        if self.cur == qkey:
            i = self.cur_eng.dma_start(out=out, in_=in_, **kw)
            i.then_inc(self.sems[semkey], 16)
        self.counts[semkey] += 16
        return self.counts[semkey]

    def wait(self, waiter, key, val=None):
        if isinstance(val, tuple):
            key, val = val
        elif isinstance(key, tuple):
            key, val = key
        key = self._k(key)
        if self.cur == waiter:
            self.cur_eng.wait_ge(self.sems[key],
                                 self.counts[key] if val is None else val)

    def wait_t(self, waiter):
        self.wait(waiter, "tA")
        self.wait(waiter, "tB")

    def wait_all(self, waiter, keys=None):
        for k in keys or ENGINE_KEYS:
            k = self._k(k)
            if k != waiter and self.counts[k]:
                self.wait(waiter, k)

    def barrier(self, keys=("v", "s", "t", "gp")):
        for k in keys:
            self.opi(k, lambda e: e.nop())
        snap = dict(self.counts)
        for k in keys:
            for k2 in keys:
                if k2 != k:
                    self.wait(k, k2, snap[k2])

    def build(self, body):
        with self.nc.Block() as block:
            for key, setter in [("sync", block.sync), ("gp", block.gpsimd),
                                ("v", block.vector), ("s", block.scalar),
                                ("t", block.tensor)]:
                def section(eng, key=key):
                    self.cur, self.cur_eng = key, eng
                    self._reset_counts()
                    for o in self.stateful:
                        o.reset()
                    body()
                    if _TRACE_COUNTS:
                        print("SECTION", key, dict(self.counts))
                setter(section)
        self.cur = self.cur_eng = None



def ln_common(p, l=L, row_w=None):
    return dict(
        ones2=p.sb("ones2c", [128, 2]),
        ones_row=p.sb("ones_row", [1, 128]),
        sq=p.sb("ln_sq", [128, l]),
        row_ps=p.ps("ln_row_ps", [1, row_w or l]),
        bc_ps=p.ps("ln_bc_ps", [128, 512]),
        mb=p.sb("ln_mb", [128, l]),
        rb=p.sb("ln_rb", [128, l]),
        mrow=p.sb("ln_mrow", [1, l]),
        vrow=p.sb("ln_vrow", [1, l]),
    )


def emit_ln_setup(p, cm):
    p.opi("v", lambda v: v.memset(cm["ones2"][:], 1.0))
    p.opi("v", lambda v: v.memset(cm["ones_row"][:], 1.0))


def emit_layernorm(p, src, out, cm, rms_weight_cols=None, rms_only=False,
                   nfeat=None):
    n = len(src)
    l = src[0].shape[1]
    inv_n = 1.0 / (nfeat or (n * 128))
    ones2, sq = cm["ones2"], cm["sq"]
    row_ps, bc_ps = cm["row_ps"], cm["bc_ps"]
    mb, rb, mrow, vrow = cm["mb"], cm["rb"], cm["mrow"], cm["vrow"]

    p.barrier()
    for i, x in enumerate(src):
        p.wait_t("s")
        sd = p.opi("s", lambda s, x=x: s.activation(out=sq[:, :l], in_=x[:, :l],
                                                    func=AF.Square))
        p.wait("t", "s", sd)
        for n0 in range(0, l, 512):
            fn = (lambda t, i=i, n0=n0, n=n: t.matmul(
                row_ps[0:1, n0:n0 + 512], ones2[:, 0:1], sq[:, n0:n0 + 512],
                start=(i == 0), stop=(i == n - 1)))
            if n0 + 512 >= l:
                p.opk("t", "t", fn)
            else:
                p.op("t", fn)
    p.wait_t("s")
    p.opi("s", lambda s: s.activation(out=vrow[0:1, :l], in_=row_ps[0:1, :l],
                                      func=AF.Copy, scale=inv_n))
    if not rms_only:
        p.wait("t", "s")
        for i, x in enumerate(src):
            for n0 in range(0, l, 512):
                fn = (lambda t, i=i, n0=n0, x=x, n=n: t.matmul(
                    row_ps[0:1, n0:n0 + 512], ones2[:, 0:1], x[:, n0:n0 + 512],
                    start=(i == 0), stop=(i == n - 1)))
                if i == n - 1:
                    p.opk("t", "t", fn)
                else:
                    p.op("t", fn)
        p.wait_t("s")
        p.opi("s", lambda s: s.activation(out=mrow[0:1, :l],
                                          in_=row_ps[0:1, :l],
                                          func=AF.Copy, scale=inv_n))
        p.opi("s", lambda s: s.activation(out=sq[0:1, :l], in_=mrow[0:1, :l],
                                          func=AF.Square))
        p.wait("v", "s")
        p.opi("v", lambda v: v.tensor_sub(vrow[0:1, :l], vrow[0:1, :l],
                                          sq[0:1, :l]))
        p.wait("s", "v")
    p.wait("v", "s")
    p.opi("v", lambda v: v.tensor_scalar(vrow[0:1, :l], vrow[0:1, :l], LN_EPS,
                                         None, ALU.add))
    p.wait("s", "v")
    p.opi("s", lambda s: s.activation(out=vrow[0:1, :l], in_=vrow[0:1, :l],
                                      func=AF.Ln))
    sd = p.opi("s", lambda s: s.activation(out=vrow[0:1, :l], in_=vrow[0:1, :l],
                                           func=AF.Exp, scale=-0.5))
    p.wait("t", "s", sd)
    for n0 in range(0, l, 512):
        td = p.opk("t", "t", lambda t, n0=n0: t.matmul(
            bc_ps[:, 0:512], cm["ones_row"][0:1, :], vrow[0:1, n0:n0 + 512],
            start=True, stop=True))
        p.wait("v", td)
        vd = p.opi("v", lambda v, n0=n0: v.tensor_copy(rb[:, n0:n0 + 512],
                                                       bc_ps[:, 0:512]))
        p.wait("t", "v", vd)
    if not rms_only:
        for n0 in range(0, l, 512):
            td = p.opk("t", "t", lambda t, n0=n0: t.matmul(
                bc_ps[:, 0:512], cm["ones_row"][0:1, :], mrow[0:1, n0:n0 + 512],
                start=True, stop=True))
            p.wait("v", td)
            vd = p.opi("v", lambda v, n0=n0: v.tensor_copy(mb[:, n0:n0 + 512],
                                                           bc_ps[:, 0:512]))
            p.wait("t", "v", vd)
    p.barrier()
    for i, (x, o) in enumerate(zip(src, out)):
        if rms_only:
            vd = p.opi("v", lambda v, x=x, i=i: v.tensor_mul(sq[:, :l],
                                                             x[:, :l],
                                                             rb[:, :l]))
            p.wait("gp", "v", vd)
            if rms_weight_cols is not None:
                pd = p.opi("gp", lambda g, o=o, i=i: g.tensor_scalar(
                    o[:, :l], sq[:, :l], rms_weight_cols[i], None, ALU.mult))
            else:
                pd = p.opi("gp", lambda g, o=o: g.tensor_copy(o[:, :l],
                                                              sq[:, :l]))
            p.wait("v", "gp", pd)
        else:
            vd = p.opi("v", lambda v, x=x: v.tensor_sub(sq[:, :l], x[:, :l],
                                                        mb[:, :l]))
            p.wait("gp", "v", vd)
            pd = p.opi("gp", lambda g, o=o: g.tensor_mul(o[:, :l], sq[:, :l],
                                                         rb[:, :l]))
            p.wait("v", "gp", pd)
    p.barrier()


def emit_proj(p, w_sb, x_bf, out_cb, psums, n_tile=512, k_chunks=None):
    l = x_bf[0].shape[1]
    nk = k_chunks or len(w_sb)
    m_chunks = w_sb[0].shape[1] // 128
    slot = [0]
    for m in range(m_chunks):
        for n0 in range(0, l, n_tile):
            ps = psums[slot[0] % len(psums)]
            slot[0] += 1
            p.wait_all("t", ("v", "s", "gp"))
            for k in range(nk):
                mm_fn = (lambda t, k=k, m=m, n0=n0: t.matmul(
                    ps[:, 0:n_tile], w_sb[k][:, m * 128:(m + 1) * 128],
                    x_bf[k][:, n0:n0 + n_tile], start=(k == 0),
                    stop=(k == nk - 1)))
                if k == nk - 1:
                    td = p.opk("t", "t", mm_fn)
                else:
                    p.op("t", mm_fn)
            out_cb(m, n0, ps, td)


class WStream:

    def __init__(self, p, name, w_hbm, d_in, d_out, mg=512, dtype=BF16,
                 bufs=None):
        self.p = p
        self.w = w_hbm
        self.d_in, self.d_out, self.mg = d_in, d_out, mg
        self.nk = d_in // 128
        self.ng = d_out // mg
        if bufs is None:
            self.buf = [p.sb(f"{name}_wb{i}", [128, self.nk * mg], dtype).ap()
                        for i in range(2)]
        else:
            self.buf = list(bufs)
        self.ldkey = [p.new_key(f"{name}_ld0"), p.new_key(f"{name}_ld1")]
        self.reset()
        p.stateful.append(self)

    def reset(self):
        self.loaded = [None, None]
        self.last_td = [None, None]
        self.next_g = 0

    def _issue(self, g):
        par = g % 2
        if getattr(self, "last_td", None) and self.last_td[par]:
            for k_, v_ in self.last_td[par]:
                self.p.wait("gp", k_, v_)
        src = self.w[0:self.d_in, g * self.mg:(g + 1) * self.mg].rearrange(
            "(k p) m -> p k m", p=128)
        dst = self.buf[par].rearrange("p (k m) -> p k m", m=self.mg)
        self.loaded[par] = self.p.dma("gp", dst, src, sem_key=self.ldkey[par])

    def prefetch(self):
        if self.next_g < self.ng:
            self._issue(self.next_g)
            self.next_g += 1

    def group(self, g):
        par = g % 2
        assert self.loaded[par] is not None
        self.p.wait("t", self.ldkey[par], self.loaded[par])
        return self.buf[par]

    def chunk(self, buf, k, mo, width=128):
        return buf[:, k * self.mg + mo:k * self.mg + mo + width]


class PsumPool:

    def __init__(self, p, name, n=4, shape=(128, 512), dtype=F32):
        self.p = p
        self.tiles = [p.ps(f"{name}{i}", list(shape), dtype) for i in range(n)]
        self.keys = [p.new_key(f"{name}k{i}") for i in range(n)]
        self.consumed = [None] * n
        p.stateful.append(self)
        self.i = 0

    def reset(self):
        self.consumed = [None] * len(self.tiles)
        self.i = 0

    def next(self):
        j = self.i % len(self.tiles)
        self.i += 1
        if self.consumed[j] is not None:
            ck, cv = self.consumed[j]
            self.p.wait("t", ck, cv)
        return j, self.tiles[j], self.keys[j]

    def done(self, j, engine_key, count):
        self.consumed[j] = (engine_key, count)


def emit_proj_stream(p, ws, x_bf, out_cb, pool, n_tile=512):
    l = x_bf[0].shape[1]
    ws.prefetch()
    ws.prefetch()
    for g in range(ws.ng):
        buf = ws.group(g)
        for mo in range(0, ws.mg, 128):
            for n0 in range(0, l, n_tile):
                j, ps, tkey = pool.next()
                for k in range(ws.nk):
                    mm_fn = (lambda t, k=k, mo=mo, n0=n0, buf=buf, ps=ps: t.matmul(
                        ps[:, 0:n_tile], ws.chunk(buf, k, mo),
                        x_bf[k][:, n0:n0 + n_tile], start=(k == 0),
                        stop=(k == ws.nk - 1)))
                    if k == ws.nk - 1:
                        td = p.opk("t", tkey, mm_fn)
                    else:
                        p.op("t", mm_fn)
                ck, cv = out_cb(g * (ws.mg // 128) + mo // 128, n0, ps, tkey, td)
                pool.done(j, ck, cv)
        ws.last_td[g % 2] = [(k_, p.counts[k_]) for k_ in pool.keys]
        ws.prefetch()


def evac_simple(p, dst_tiles, eng="v", scale=None, bias_cols=None, func=None):
    def cb(m, n0, ps, tdone, _eng=eng):
        nt = ps.shape[1]
        p.wait(_eng, "t", tdone)
        if func is not None or scale is not None or bias_cols is not None:
            kw = {}
            if scale is not None:
                kw["scale"] = scale
            if bias_cols is not None:
                kw["bias"] = bias_cols[m]
            p.opi("s", lambda s, m=m, n0=n0: s.activation(
                out=dst_tiles[m][:, n0:n0 + nt], in_=ps[:, 0:nt],
                func=(func or AF.Copy), **kw))
        else:
            p.opi(_eng, lambda e, m=m, n0=n0: e.tensor_copy(
                dst_tiles[m][:, n0:n0 + nt], ps[:, 0:nt]))
    return cb


def stage_s1(p, io):
    x_in = io["x"]
    w_in = io["w_in_t"]
    u_out = io["u_pre"]
    z_out = io["z"]

    if True:
        x = p.fm("x")
        xn = p.fm("xn", dtype=BF16)
        outp = [p.sb(f"outp_{i}", [128, L]) for i in range(4)]
        cm = ln_common(p)
        pool = PsumPool(p, "mmp", 4)
        ws = WStream(p, "win", w_in, D, 2 * D, mg=512)

        def body():
            for i in range(CD):
                p.dma("sync", x[i][:], x_in[i * 128:(i + 1) * 128, :])
            p.wait_all("v", ("sync",))
            p.wait_all("s", ("sync",))
            p.wait_all("t", ("sync",))
            p.wait_all("gp", ("sync",))
            emit_ln_setup(p, cm)
            emit_layernorm(p, x, xn, cm)

            def out_cb(m, n0, ps, tkey, tdone):
                ot = outp[m % 4]
                p.wait("v", "sync")
                p.wait("v", tkey, tdone)
                vd = p.opi("v", lambda v: v.tensor_copy(ot[:, n0:n0 + 512],
                                                        ps[:, 0:512]))
                if n0 + 512 >= L:
                    p.wait("sync", "v", vd)
                    dst = u_out if m < CD else z_out
                    r = (m % CD) * 128
                    p.dma("sync", dst[r:r + 128, :], ot[:])
                return ("v", vd)

            emit_proj_stream(p, ws, xn, out_cb, pool)
            p.barrier()
            p.wait("sync", "sync")
        return body



D_STATE = 16
DT_RANK = 128


def emit_conv4_silu(p, u_ext, u_out, u_bf, cw, cb, scratch, nch=CD, K=4):
    for c in range(nch):
        ue = u_ext[c]
        p.op("v", lambda v, c=c, ue=ue: v.tensor_scalar(
            scratch[:, :L], ue[:, 0:L], cw[:, K * c:K * c + 1], None, ALU.mult))
        for k in range(1, K - 1):
            p.op("v", lambda v, c=c, k=k, ue=ue: v.scalar_tensor_tensor(
                scratch[:, :L], ue[:, k:k + L], cw[:, K * c + k:K * c + k + 1],
                scratch[:, :L], ALU.mult, ALU.add))
        p.op("v", lambda v, c=c, ue=ue: v.scalar_tensor_tensor(
            scratch[:, :L], ue[:, K - 1:K - 1 + L],
            cw[:, K * c + K - 1:K * c + K], scratch[:, :L], ALU.mult, ALU.add))
        vd = p.op("v", lambda v, c=c: v.tensor_scalar(
            scratch[:, :L], scratch[:, :L], cb[:, c:c + 1], None, ALU.add))
        p.wait("s", "v", vd)
        sd = p.op("s", lambda s, c=c: s.activation(out=u_out[c][:, :L],
                                                   in_=scratch[:, :L],
                                                   func=AF.Silu))
        if u_bf is not None:
            p.wait("gp", "s", sd)
            p.op("gp", lambda g, c=c: g.tensor_copy(u_bf[c][:, :L],
                                                    u_out[c][:, :L]))
        p.wait("v", "s", sd)


def emit_softplus(p, dst, src, scr1):
    l = src.shape[1]
    p.wait("s", "v")
    p.op("s", lambda s: s.activation(out=scr1[:, :l], in_=src[:, :l],
                                     func=AF.Abs))
    sd = p.op("s", lambda s: s.activation(out=scr1[:, :l], in_=scr1[:, :l],
                                          func=AF.Exp, scale=-1.0))
    p.wait("v", "s", sd)
    vd = p.op("v", lambda v: v.tensor_scalar(scr1[:, :l], scr1[:, :l], 1.0,
                                             None, ALU.add))
    p.wait("s", "v", vd)
    sd = p.op("s", lambda s: s.activation(out=scr1[:, :l], in_=scr1[:, :l],
                                          func=AF.Ln))
    sd2 = p.op("s", lambda s: s.activation(out=dst[:, :l], in_=src[:, :l],
                                           func=AF.Relu))
    p.wait("v", "s", sd2)
    vd = p.op("v", lambda v: v.tensor_add(dst[:, :l], dst[:, :l], scr1[:, :l]))
    return vd


def emit_bcast_row(p, src_bf, onehot, row, out_sb, ps, kdim=32):
    l = src_bf.shape[1]
    for n0 in range(0, l, 512):
        td = p.opk("t", "t", lambda t, n0=n0: t.matmul(
            ps[:, 0:512], onehot[:, row * 128:(row + 1) * 128],
            src_bf[0:kdim, n0:n0 + 512], start=True, stop=True))
        p.wait("v", td)
        vd = p.op("v", lambda v, n0=n0: v.tensor_copy(out_sb[:, n0:n0 + 512],
                                                      ps[:, 0:512]))
        p.wait("t", "v", vd)
    return vd


def emit_mamba1_scan(p, dt, u_f32, bc_bf, acol, h0, yacc, sc, onehot, bc_ps):
    for c in range(CD):
        p.op("gp", lambda g, c=c: g.tensor_mul(sc["wdu"][c][:, :L],
                                               dt[c][:, :L], u_f32[c][:, :L]))
    p.barrier()
    for n in range(D_STATE):
        par = n % 2
        bbc, cbc = sc["bbc"][par], sc["cbc"][par]
        p.wait("v", "gp")
        emit_bcast_row(p, bc_bf, onehot, n, bbc, bc_ps)
        vd = emit_bcast_row(p, bc_bf, onehot, 16 + n, cbc, bc_ps)
        p.wait("gp", "v", vd)
        p.wait("s", "v", vd)
        for c in range(CD):
            cpar = c % 2
            da, dbu, hsc, ym = (sc["da"][cpar], sc["dbu"][cpar],
                                sc["hs"][cpar], sc["ym"][cpar])
            sd = p.op("s", lambda s, c=c, n=n, da=da: s.activation(
                out=da[:, :L], in_=dt[c][:, :L], func=AF.Exp,
                scale=acol[:, c * D_STATE + n:c * D_STATE + n + 1]))
            pd = p.op("gp", lambda g, c=c, dbu=dbu, bbc=bbc: g.tensor_mul(
                dbu[:, :L], sc["wdu"][c][:, :L], bbc[:, :L]))
            p.wait("v", "s", sd)
            p.wait("v", "gp", pd)
            p.op("v", lambda v, da=da, dbu=dbu, hsc=hsc, c=c, n=n:
                 v.tensor_tensor_scan(hsc[:, :L], da[:, :L], dbu[:, :L],
                                      h0[c][:, n:n + 1], ALU.mult, ALU.add))
            p.op("v", lambda v, hsc=hsc, c=c, n=n: v.tensor_copy(
                sc["hend"][c][:, n:n + 1], hsc[:, L - 1:L]))
            vd_scan = p.counts["v"]
            p.wait("gp", "v", vd_scan)
            pd2 = p.op("gp", lambda g, hsc=hsc, ym=ym, cbc=cbc: g.tensor_mul(
                ym[:, :L], hsc[:, :L], cbc[:, :L]))
            p.wait("v", "gp", pd2)
            if n == 0:
                p.op("v", lambda v, c=c, ym=ym: v.tensor_copy(yacc[c][:, :L],
                                                              ym[:, :L]))
            else:
                p.op("v", lambda v, c=c, ym=ym: v.tensor_add(yacc[c][:, :L],
                                                             yacc[c][:, :L],
                                                             ym[:, :L]))
            p.wait("s", "v", vd_scan)
            p.wait("gp", "v", vd_scan)
    p.barrier()




def colpack(a, k=None):
    import numpy as np
    a = np.asarray(a)
    if a.ndim == 1:
        a = a[:, None]
    k = a.shape[1]
    return np.ascontiguousarray(a.reshape(-1, 128, k).transpose(1, 0, 2).reshape(128, -1))

def dma_colpack(p, queue, sb_tile, hbm, k):
    src = hbm.rearrange("(c p) k -> p c k", p=128)
    dst = sb_tile.ap().rearrange("p (c k) -> p c k", k=k)
    return p.dma(queue, dst, src)

def stage_s2(p, io):
    TL = 512
    u_in = io["u_pre"]
    halo = io["halo"]
    conv_w = io["conv_w"]
    conv_b = io["conv_b"]
    w_xproj = io["w_xproj_t"]
    w_dt = io["w_dt_t"]
    b_dt = io["b_dt"]
    a_log = io["a_log"]
    onehot_in = io["onehot32"]
    h0_in = io["h0"]

    u_out = io["u_silu"]
    dt_out = io["dt"]
    bc_out = io["bc"]
    y_out = io["y_loc"]
    hend_out = io["h_end"]

    if True:
        u_ext = [p.sb(f"uext_{i}", [128, 3 + TL]) for i in range(CD)]
        dt_t = p.fm("dt", l=TL)
        wdu = p.fm("wdu", l=TL)
        yacc = p.fm("yacc", l=TL)
        cw = p.sb("cw", [128, CD * 4])
        cb = p.sb("cb", [128, CD])
        bdt = p.sb("bdt", [128, CD])
        acol = p.sb("acol", [128, CD * D_STATE])
        onehot = p.sb("onehot", [32, 32 * 128], BF16)
        wxp = [p.sb(f"wxp_{k}", [128, 160]) for k in range(CD)]
        wdt = p.sb("wdt", [128, D])
        dtr = p.sb("dtr", [128, TL])
        bc_f = p.sb("bc_f", [32, L])
        bc_bf = p.sb("bc_bf", [32, L], BF16)
        h00 = [p.sb(f"h00_{i}", [128, D_STATE]) for i in range(CD)]
        scr = p.sb("scr", [128, TL])
        scr2 = p.sb("scr2", [128, TL])
        sc = dict(
            bbc=[p.sb(f"bbc{i}", [128, TL]) for i in range(2)],
            cbc=[p.sb(f"cbc{i}", [128, TL]) for i in range(2)],
            da=[p.sb(f"da{i}", [128, TL]) for i in range(2)],
            dbu=[p.sb(f"dbu{i}", [128, TL]) for i in range(2)],
            hs=[p.sb(f"hs{i}", [128, TL]) for i in range(2)],
            ym=[p.sb(f"ym{i}", [128, TL]) for i in range(2)],
            hend=[[p.sb(f"hend{s_}_{i}", [128, D_STATE]) for i in range(CD)]
                  for s_ in range(2)],
        )
        pool = PsumPool(p, "mmp", 4)
        bc_ps = p.ps("bc_ps", [128, 512])

        def body():
            p.dma("sync", cw[:], conv_w[:, :])
            p.dma("sync", cb[:], conv_b[:, :])
            p.dma("sync", bdt[:], b_dt[:, :])
            p.dma("sync", acol[:], a_log[:, :])
            for i in range(CD):
                p.dma("sync", h00[i][:], h0_in[i * 128:(i + 1) * 128, :])
            p.dma("gp", onehot[:], onehot_in[:, :])
            for k in range(CD):
                p.dma("s", wxp[k][:], w_xproj[k * 128:(k + 1) * 128, :])
            p.dma("s", wdt[:], w_dt[:, :])
            for e in ("v", "s", "t", "gp"):
                p.wait_all(e, ("sync", "dgp", "ds"))
            p.op("s", lambda s: s.activation(out=acol[:], in_=acol[:],
                                             func=AF.Exp))
            sd = p.op("s", lambda s: s.activation(out=acol[:], in_=acol[:],
                                                  func=AF.Copy, scale=-1.0))
            p.wait("v", "s", sd)
            p.wait("gp", "s", sd)

            for seg in range(2):
                s0 = seg * TL
                p.barrier()
                p.wait_all("sync", ("v", "s", "gp"))
                p.wait("sync", "sync")
                for i in range(CD):
                    p.dma("sync", u_ext[i][:, 3:3 + TL],
                          u_in[i * 128:(i + 1) * 128, s0:s0 + TL])
                    if seg == 0:
                        p.dma("sync", u_ext[i][:, 0:3],
                              halo[i * 128:(i + 1) * 128, 0:3])
                    else:
                        p.dma("sync", u_ext[i][:, 0:3],
                              u_in[i * 128:(i + 1) * 128, s0 - 3:s0])
                for e in ("v", "s", "t", "gp"):
                    p.wait_all(e, ("sync",))
                u_view = [u_ext[i].ap()[:, 3:3 + TL] for i in range(CD)]
                emit_conv4_silu_inplace(p, u_ext, cw, cb, scr, TL)
                p.barrier()
                for (mo, mw) in [(0, 128), (128, 32)]:
                    j, ps, tkey = pool.next()
                    for k in range(CD):
                        fn = (lambda t, k=k, mo=mo, mw=mw, ps=ps: t.matmul(
                            ps[0:mw, 0:TL], wxp[k][:, mo:mo + mw],
                            u_view[k], start=(k == 0), stop=(k == CD - 1)))
                        if k == CD - 1:
                            td = p.opk("t", tkey, fn)
                        else:
                            p.op("t", fn)
                    p.wait("v", tkey, td)
                    if mo == 0:
                        vd = p.op("v", lambda v, ps=ps: v.tensor_copy(
                            dtr[:, 0:TL], ps[:, 0:TL]))
                    else:
                        vd = p.op("v", lambda v, ps=ps, s0=s0: v.tensor_copy(
                            bc_f[0:32, s0:s0 + TL], ps[0:32, 0:TL]))
                    pool.done(j, "v", vd)
                vd = p.op("v", lambda v, s0=s0: v.tensor_copy(
                    bc_bf[:, s0:s0 + TL], bc_f[:, s0:s0 + TL]))
                p.barrier()
                for m in range(CD):
                    j, ps, tkey = pool.next()
                    td = p.opk("t", tkey, lambda t, m=m, ps=ps: t.matmul(
                        ps[:, 0:TL], wdt[:, m * 128:(m + 1) * 128],
                        dtr[:, 0:TL], start=True, stop=True))
                    p.wait("s", tkey, td)
                    p.wait("s", "v")
                    sd = p.op("s", lambda s, ps=ps, m=m: s.activation(
                        out=scr[:, 0:TL], in_=ps[:, 0:TL],
                        func=AF.Identity, bias=bdt[:, m:m + 1]))
                    pool.done(j, "s", sd)
                    emit_softplus(p, dt_t[m], scr, scr2)
                p.barrier()
                h0c = h00 if seg == 0 else sc["hend"][0]
                hendc = sc["hend"][seg]
                emit_mamba1_scan_seg(p, dt_t, u_view, bc_bf, acol, h0c, yacc,
                                     sc, onehot, bc_ps, hendc, s0, TL,
                                     wdu=wdu)
                p.wait_all("sync", ("v", "s", "gp"))
                for i in range(CD):
                    p.dma("sync", u_out[i * 128:(i + 1) * 128, s0:s0 + TL],
                          u_view[i])
                    p.dma("sync", dt_out[i * 128:(i + 1) * 128, s0:s0 + TL],
                          dt_t[i][:])
                    p.dma("sync", y_out[i * 128:(i + 1) * 128, s0:s0 + TL],
                          yacc[i][:])
                    if seg == 1:
                        p.dma("sync", hend_out[i * 128:(i + 1) * 128, :],
                              hendc[i][:])
            p.dma("sync", bc_out[:, :], bc_f[:])
            p.wait("sync", "sync")
        return body


def emit_conv4_silu_inplace(p, u_ext, cw, cb, scratch, TL):
    for c in range(CD):
        ue = u_ext[c]
        p.op("v", lambda v, c=c, ue=ue: v.tensor_scalar(
            scratch[:, :TL], ue[:, 0:TL], cw[:, 4 * c:4 * c + 1], None,
            ALU.mult))
        for k in (1, 2, 3):
            p.op("v", lambda v, c=c, k=k, ue=ue: v.scalar_tensor_tensor(
                scratch[:, :TL], ue[:, k:k + TL],
                cw[:, 4 * c + k:4 * c + k + 1], scratch[:, :TL],
                ALU.mult, ALU.add))
        vd = p.op("v", lambda v, c=c: v.tensor_scalar(
            scratch[:, :TL], scratch[:, :TL], cb[:, c:c + 1], None, ALU.add))
        p.wait("s", "v", vd)
        sd = p.op("s", lambda s, c=c, ue=ue: s.activation(
            out=ue[:, 3:3 + TL], in_=scratch[:, :TL], func=AF.Silu))
        p.wait("v", "s", sd)


def emit_mamba1_scan_seg(p, dt, u_view, bc_bf, acol, h0, yacc, sc, onehot,
                         bc_ps, hend, s0, TL, wdu=None):
    if wdu is not None:
        for c in range(CD):
            p.op("gp", lambda g, c=c: g.tensor_mul(
                wdu[c][:, :TL], dt[c][:, :TL], u_view[c]))
    for n in range(D_STATE):
        par = n % 2
        bbc, cbc = sc["bbc"][par], sc["cbc"][par]
        p.wait("v", "gp")
        emit_bcast_row_seg(p, bc_bf, onehot, n, bbc, bc_ps, s0, TL)
        vd = emit_bcast_row_seg(p, bc_bf, onehot, 16 + n, cbc, bc_ps, s0, TL)
        p.wait("gp", "v", vd)
        p.wait("s", "v", vd)
        for c in range(CD):
            cpar = c % 2
            da, dbu, hsc, ym = (sc["da"][cpar], sc["dbu"][cpar],
                                sc["hs"][cpar], sc["ym"][cpar])
            sd = p.op("s", lambda s, c=c, n=n, da=da: s.activation(
                out=da[:, :TL], in_=dt[c][:, :TL], func=AF.Exp,
                scale=acol[:, c * D_STATE + n:c * D_STATE + n + 1]))
            if wdu is not None:
                pd = p.op("gp", lambda g, dbu=dbu, bbc=bbc, c=c: g.tensor_mul(
                    dbu[:, :TL], wdu[c][:, :TL], bbc[:, :TL]))
            else:
                pd = p.op("gp", lambda g, c=c, dbu=dbu: g.tensor_mul(
                    dbu[:, :TL], dt[c][:, :TL], u_view[c]))
                pd = p.op("gp", lambda g, dbu=dbu, bbc=bbc: g.tensor_mul(
                    dbu[:, :TL], dbu[:, :TL], bbc[:, :TL]))
            p.wait("v", "s", sd)
            p.wait("v", "gp", pd)
            p.op("v", lambda v, da=da, dbu=dbu, hsc=hsc, c=c, n=n:
                 v.tensor_tensor_scan(hsc[:, :TL], da[:, :TL], dbu[:, :TL],
                                      h0[c][:, n:n + 1], ALU.mult, ALU.add))
            p.op("v", lambda v, hsc=hsc, c=c, n=n: v.tensor_copy(
                hend[c][:, n:n + 1], hsc[:, TL - 1:TL]))
            vd_scan = p.counts["v"]
            p.wait("gp", "v", vd_scan)
            pd2 = p.op("gp", lambda g, hsc=hsc, ym=ym, cbc=cbc: g.tensor_mul(
                ym[:, :TL], hsc[:, :TL], cbc[:, :TL]))
            p.wait("v", "gp", pd2)
            if n == 0:
                p.op("v", lambda v, c=c, ym=ym: v.tensor_copy(
                    yacc[c][:, :TL], ym[:, :TL]))
            else:
                p.op("v", lambda v, c=c, ym=ym: v.tensor_add(
                    yacc[c][:, :TL], yacc[c][:, :TL], ym[:, :TL]))
            p.wait("s", "v", vd_scan)
            p.wait("gp", "v", vd_scan)
    p.barrier()


def emit_bcast_row_seg(p, src_bf, onehot, row, out_sb, ps, s0, TL):
    td = p.opk("t", "t", lambda t: t.matmul(
        ps[:, 0:TL], onehot[:, row * 128:(row + 1) * 128],
        src_bf[0:32, s0:s0 + TL], start=True, stop=True))
    p.wait("v", td)
    vd = p.op("v", lambda v: v.tensor_copy(out_sb[:, 0:TL], ps[:, 0:TL]))
    p.wait("t", "v", vd)
    return vd



def stage_s2c(p, io):
    u_pre_in = io["u_pre"]
    halo_in = io["halo"]
    us_in = io["u_silu"]
    dt_in = io["dt"]
    bc_in = io["bc"]
    h0_in = io["h0"]
    cw_in = io["conv_w"]
    cb_in = io["conv_b"]
    wxp_in = io["w_xproj_t"]
    wdt_in = io["w_dt_t"]
    bdt_in = io["b_dt"]
    alog_in = io["a_log"]
    oh_in = io["onehot32"]
    id_in = io["identf"]
    usf_out = io["u_silu_fix"]
    yf_out = io["y_fix"]

    if True:
        cw = p.sb("cw", [128, CD * 4])
        cbv = p.sb("cbv", [128, CD])
        wxp = [p.sb(f"wxp_{k}", [128, 160]) for k in range(CD)]
        wdt = p.sb("wdt", [128, D])
        bdt = p.sb("bdt", [128, CD])
        acol = p.sb("acol", [128, CD * D_STATE])
        onehot = p.sb("onehot", [32, 32 * 128], BF16)
        identf = p.sb("identf_sb", [128, 128])
        bc_f = p.sb("bc_f", [32, L])
        bc_bf = p.sb("bc_bf", [32, L], BF16)
        cat6 = [p.sb(f"cat6_{c}", [128, 6]) for c in range(CD)]
        un3 = p.sb("un3", [128, CD * 3])
        dtn3 = p.sb("dtn3", [128, CD * 3])
        dtr3 = p.sb("dtr3", [128, 3])
        bcn3 = p.sb("bcn3", [32, 3])
        scr3 = p.sb("scr3", [128, CD * 3])
        TL = 512
        ut = [p.sb(f"ut{i}", [128, TL]) for i in range(CD)]
        dt_t = p.fm("dt", l=TL)
        wdu = p.fm("wdu", l=TL)
        yacc = p.fm("yacc", l=TL)
        h00 = [p.sb(f"h00_{i}", [128, D_STATE]) for i in range(CD)]
        sc = dict(
            bbc=[p.sb(f"bbc{i}", [128, TL]) for i in range(2)],
            cbc=[p.sb(f"cbc{i}", [128, TL]) for i in range(2)],
            da=[p.sb(f"da{i}", [128, TL]) for i in range(2)],
            dbu=[p.sb(f"dbu{i}", [128, TL]) for i in range(2)],
            hs=[p.sb(f"hs{i}", [128, TL]) for i in range(2)],
            ym=[p.sb(f"ym{i}", [128, TL]) for i in range(2)],
            hend=[[p.sb(f"hend{s_}_{i}", [128, D_STATE]) for i in range(CD)]
                  for s_ in range(2)],
        )
        bc_ps = p.ps("bc_ps", [128, 512])
        ps_a = p.ps("ps_a", [128, 512])
        ps_b = p.ps("ps_b", [128, 512])
        ps_t = p.ps("ps_t", [1, 128])
        ka = p.new_key("ka")
        kb = p.new_key("kb")
        ldk = [p.new_key("l2r0"), p.new_key("l2r1")]

        def body():
            p.dma("sync", cw[:], cw_in[:, :], sem_key="cst")
            p.dma("sync", cbv[:], cb_in[:, :], sem_key="cst")
            p.dma("sync", bdt[:], bdt_in[:, :], sem_key="cst")
            p.dma("sync", acol[:], alog_in[:, :], sem_key="cst")
            p.dma("sync", wdt[:], wdt_in[:, :], sem_key="cst")
            p.dma("sync", bc_f[:], bc_in[:, :], sem_key="cst")
            p.dma("sync", identf[:], id_in[:, :], sem_key="cst")
            for k in range(CD):
                p.dma("s", wxp[k][:], wxp_in[k * 128:(k + 1) * 128, :],
                      sem_key="cst")
            p.dma("gp", onehot[:], oh_in[:, :], sem_key="cst")
            for c in range(CD):
                r = c * 128
                p.dma("gp", cat6[c][:, 0:3], halo_in[r:r + 128, :],
                      sem_key="cst")
                p.dma("gp", cat6[c][:, 3:6], u_pre_in[r:r + 128, 0:3],
                      sem_key="cst")
                p.dma("gp", h00[c][:], h0_in[r:r + 128, :],
                      sem_key="cst")
            for e in ("v", "s", "t", "gp"):
                p.wait_all(e, ("cst",))
            p.op("s", lambda s: s.activation(out=acol[:], in_=acol[:],
                                             func=AF.Exp))
            sd = p.op("s", lambda s: s.activation(out=acol[:], in_=acol[:],
                                                  func=AF.Copy, scale=-1.0))
            p.wait("v", "s", sd)
            for c in range(CD):
                p.op("v", lambda v, c=c: v.tensor_scalar(
                    scr3[:, c * 3:c * 3 + 3], cat6[c][:, 0:3],
                    cw[:, 4 * c:4 * c + 1], None, ALU.mult))
                for k in (1, 2, 3):
                    p.op("v", lambda v, c=c, k=k: v.scalar_tensor_tensor(
                        scr3[:, c * 3:c * 3 + 3], cat6[c][:, k:k + 3],
                        cw[:, 4 * c + k:4 * c + k + 1],
                        scr3[:, c * 3:c * 3 + 3], ALU.mult, ALU.add))
                vd = p.op("v", lambda v, c=c: v.tensor_scalar(
                    scr3[:, c * 3:c * 3 + 3], scr3[:, c * 3:c * 3 + 3],
                    cbv[:, c:c + 1], None, ALU.add))
            p.wait("s", "v", vd)
            sd = p.op("s", lambda s: s.activation(out=un3[:], in_=scr3[:],
                                                  func=AF.Silu))
            p.wait("t", "s", sd)
            p.wait("v", "s", sd)
            for (mo, mw, dst) in ((0, 128, dtr3), (128, 32, bcn3)):
                for k in range(CD):
                    fn = (lambda t, k=k, mo=mo, mw=mw: t.matmul(
                        ps_a[0:mw, 0:3], wxp[k][:, mo:mo + mw],
                        un3[:, k * 3:k * 3 + 3], start=(k == 0),
                        stop=(k == CD - 1)))
                    if k == CD - 1:
                        td = p.opk("t", ka, fn)
                    else:
                        p.op("t", fn)
                p.wait("v", ka, td)
                vd = p.op("v", lambda v, mw=mw, dst=dst: v.tensor_copy(
                    dst[0:mw, 0:3], ps_a[0:mw, 0:3]))
                p.wait("t", "v", vd)
            for m in range(CD):
                td = p.opk("t", ka, lambda t, m=m: t.matmul(
                    ps_b[:, 0:3], wdt[:, m * 128:(m + 1) * 128],
                    dtr3[:, 0:3], start=True, stop=True))
                p.wait("s", ka, td)
                sd = p.op("s", lambda s, m=m: s.activation(
                    out=scr3[:, m * 3:m * 3 + 3], in_=ps_b[:, 0:3],
                    func=AF.Identity, bias=bdt[:, m:m + 1]))
                p.wait("t", "s", sd)
            emit_softplus(p, dtn3, scr3, sc["da"][0])
            p.barrier()
            vd = p.op("v", lambda v: v.tensor_copy(bc_f[0:32, 0:3],
                                                   bcn3[0:32, 0:3]))
            vd = p.op("v", lambda v: v.tensor_copy(bc_bf[:], bc_f[:]))
            p.wait("t", "v", vd)
            p.barrier()
            for seg in range(2):
                s0 = seg * TL
                p.barrier()
                p.wait_all("sync", ("v", "s", "gp"))
                p.wait("sync", "sync")
                for c in range(CD):
                    r = c * 128
                    p.dma("sync", ut[c][:, 0:TL], us_in[r:r + 128, s0:s0 + TL])
                    p.dma("sync", dt_t[c][:, 0:TL],
                          dt_in[r:r + 128, s0:s0 + TL])
                for e in ("v", "s", "t", "gp"):
                    p.wait_all(e, ("sync",))
                if seg == 0:
                    for c in range(CD):
                        p.op("v", lambda v, c=c: v.tensor_copy(
                            ut[c][:, 0:3], un3[:, c * 3:c * 3 + 3]))
                        p.op("v", lambda v, c=c: v.tensor_copy(
                            dt_t[c][:, 0:3], dtn3[:, c * 3:c * 3 + 3]))
                    p.barrier()
                u_view = [ut[i].ap()[:, 0:TL] for i in range(CD)]
                h0c = h00 if seg == 0 else sc["hend"][0]
                hendc = sc["hend"][seg]
                emit_mamba1_scan_seg(p, dt_t, u_view, bc_bf, acol, h0c, yacc,
                                     sc, onehot, bc_ps, hendc, s0, TL,
                                     wdu=wdu)
                p.wait_all("sync", ("v", "s", "gp"))
                for c in range(CD):
                    r = c * 128
                    p.dma("sync", usf_out[r:r + 128, s0:s0 + TL],
                          ut[c][:, 0:TL])
                    p.dma("sync", yf_out[r:r + 128, s0:s0 + TL],
                          yacc[c][:])
            p.barrier()
            p.wait("sync", "sync")
        return body


def stage_s4(p, io):
    x_in = io["x"]
    y_in = io["y"]
    u_in = io["u_silu"]
    z_in = io["z"]
    dskip = io["dskip"]
    w_out = io["w_out_t"]
    x1_out = io["x1"]

    if True:
        yf = p.fm("yf", dtype=BF16)
        dsk = p.sb("dsk", [128, CD])
        rot = [p.sb(f"rot{i}", [128, L]) for i in range(6)]
        ldk = [p.new_key("l0ld0"), p.new_key("l0ld1")]
        xr = [p.sb(f"xr{i}", [128, L]) for i in range(4)]
        xo = [p.sb(f"xo{i}", [128, L]) for i in range(4)]
        scr = p.sb("scr", [128, L])
        pool = PsumPool(p, "mmp", 4)
        ws = WStream(p, "wout", w_out, D, D, mg=512)

        def body():
            p.dma("sync", dsk[:], dskip[:, :], sem_key="cst")
            p.wait("v", "cst")
            for c in range(CD):
                ybuf, ubuf, zbuf = rot[(c % 2) * 3], rot[(c % 2) * 3 + 1], rot[(c % 2) * 3 + 2]
                key = ldk[c % 2]
                if c >= 2:
                    p.wait("sync", "v", p.counts["v"])
                    p.wait("sync", "s", p.counts["s"])
                p.dma("sync", ybuf[:], y_in[c * 128:(c + 1) * 128, :], sem_key=key)
                p.dma("sync", ubuf[:], u_in[c * 128:(c + 1) * 128, :], sem_key=key)
                ld = p.dma("sync", zbuf[:], z_in[c * 128:(c + 1) * 128, :], sem_key=key)
                p.wait("v", key, ld)
                p.wait("s", key, ld)
                sd = p.op("s", lambda s, zbuf=zbuf: s.activation(
                    out=zbuf[:], in_=zbuf[:], func=AF.Silu))
                vd = p.op("v", lambda v, c=c, ubuf=ubuf, ybuf=ybuf:
                          v.scalar_tensor_tensor(ybuf[:], ubuf[:],
                                                 dsk[:, c:c + 1], ybuf[:],
                                                 ALU.mult, ALU.add))
                p.wait("v", "s", sd)
                vd = p.op("v", lambda v, c=c, ybuf=ybuf, zbuf=zbuf:
                          v.tensor_mul(yf[c][:], ybuf[:], zbuf[:]))
            p.barrier()

            def out_cb(m, n0, ps, tkey, tdone):
                xrb = xr[m % 4]
                xob = xo[m % 4]
                if n0 == 0:
                    p.wait("sync", "v")
                    p.wait("sync", "sync")
                    p.dma("sync", xrb[:], x_in[m * 128:(m + 1) * 128, :])
                p.wait("v", tkey, tdone)
                p.wait("v", "sync")
                vd = p.op("v", lambda v: v.tensor_add(xob[:, n0:n0 + 512],
                                                      ps[:, 0:512],
                                                      xrb[:, n0:n0 + 512]))
                if n0 + 512 >= L:
                    p.wait("sync", "v", vd)
                    p.dma("sync", x1_out[m * 128:(m + 1) * 128, :], xob[:])
                return ("v", vd)

            emit_proj_stream(p, ws, yf, out_cb, pool)
            p.barrier()
            p.wait("sync", "sync")
        return body


def emit_ln_streamed(p, hbm_src, out_bf, cm, rot2, nchunk=CD):
    if p._k("lnk0") == "lnk0" and "lnk0" not in p.sems:
        p.new_key("lnk0")
        p.new_key("lnk1")
    lnk = ["lnk0", "lnk1"]
    l = out_bf[0].shape[1]
    inv_n = 1.0 / (nchunk * 128)
    ones2, sq = cm["ones2"], cm["sq"]
    row_ps, row_ps2, bc_ps = cm["row_ps"], cm["row_ps2"], cm["bc_ps"]
    mb, rb, mrow, vrow = cm["mb"], cm["rb"], cm["mrow"], cm["vrow"]
    for n0 in range(0, l, 512):
        for i in range(nchunk):
            xb = rot2[i % 2]
            p.wait("sync", "s")
            p.wait("sync", lnk[i % 2])
            ld = p.dma("sync", xb[:, 0:512],
                       hbm_src[i * 128:(i + 1) * 128, n0:n0 + 512])
            p.wait("s", "sync", ld)
            p.wait("s", "lnk0")
            p.wait("s", "lnk1")
            sd = p.op("s", lambda s, xb=xb: s.activation(
                out=sq[:, 0:512], in_=xb[:, 0:512], func=AF.Square))
            p.wait("t", "s", sd)
            p.opk("t", lnk[i % 2], lambda t, i=i: t.matmul(
                row_ps[0:1, 0:512], ones2[:, 0:1], sq[:, 0:512],
                start=(i == 0), stop=(i == nchunk - 1)))
            fn = (lambda t, i=i, xb=xb: t.matmul(
                row_ps2[0:1, 0:512], ones2[:, 0:1], xb[:, 0:512],
                start=(i == 0), stop=(i == nchunk - 1)))
            td = p.opk("t", lnk[i % 2], fn)
        p.wait("s", "lnk0")
        p.wait("s", "lnk1")
        p.op("s", lambda s, n0=n0: s.activation(
            out=vrow[0:1, n0:n0 + 512], in_=row_ps[0:1, 0:512],
            func=AF.Copy, scale=inv_n))
        sd = p.op("s", lambda s, n0=n0: s.activation(
            out=mrow[0:1, n0:n0 + 512], in_=row_ps2[0:1, 0:512],
            func=AF.Copy, scale=inv_n))
    p.op("s", lambda s: s.activation(out=sq[0:1, :l], in_=mrow[0:1, :l],
                                     func=AF.Square))
    p.wait("v", "s")
    p.op("v", lambda v: v.tensor_sub(vrow[0:1, :l], vrow[0:1, :l],
                                     sq[0:1, :l]))
    p.op("v", lambda v: v.tensor_scalar(vrow[0:1, :l], vrow[0:1, :l], LN_EPS,
                                        None, ALU.add))
    p.wait("s", "v")
    p.op("s", lambda s: s.activation(out=vrow[0:1, :l], in_=vrow[0:1, :l],
                                     func=AF.Ln))
    sd = p.op("s", lambda s: s.activation(out=vrow[0:1, :l], in_=vrow[0:1, :l],
                                          func=AF.Exp, scale=-0.5))
    p.wait("t", "s", sd)
    for src_row, dst in ((vrow, rb), (mrow, mb)):
        for n0 in range(0, l, 512):
            td = p.opk("t", "t", lambda t, n0=n0, src_row=src_row: t.matmul(
                bc_ps[:, 0:512], cm["ones_row"][0:1, :],
                src_row[0:1, n0:n0 + 512], start=True, stop=True))
            p.wait("v", td)
            vd = p.op("v", lambda v, n0=n0, dst=dst: v.tensor_copy(
                dst[:, n0:n0 + 512], bc_ps[:, 0:512]))
            p.wait("t", "v", vd)
    p.barrier()
    for i in range(nchunk):
        xb = rot2[i % 2]
        p.wait("sync", "v")
        ld = p.dma("sync", xb[:, :l], hbm_src[i * 128:(i + 1) * 128, :])
        p.wait("v", "sync", ld)
        p.op("v", lambda v, xb=xb: v.tensor_sub(xb[:, :l], xb[:, :l],
                                                mb[:, :l]))
        vd = p.op("v", lambda v, i=i, xb=xb: v.tensor_mul(out_bf[i][:, :l],
                                                          xb[:, :l],
                                                          rb[:, :l]))
    p.barrier()


def stage_s5(p, io):
    K_EXP, R_RANK = 8, 64
    x1_in = io["x1"]
    w1 = io["w1_t"]
    b1 = io["b1"]
    w2 = io["w2_t"]
    b2 = io["b2"]
    v_t = io["v_t"]
    u_t = io["u_t"]
    sel_in = io["sel8"]
    ident_in = io["ident128"]
    x2_out = io["x2"]

    if True:
        cm = ln_common(p, row_w=512)
        cm["row_ps2"] = p.ps("ln_row_ps2", [1, 512])
        xn = p.fm("xn", dtype=BF16)
        hbf = p.fm("hbf", dtype=BF16)
        rot2 = [p.sb(f"rot{i}", [128, L]) for i in range(2)]
        w2sb = [p.sb(f"w2_{k}", [128, K_EXP], BF16) for k in range(CD)]
        b2sb = p.sb("b2sb", [128, K_EXP])
        b1sb = p.sb("b1sb", [128, CD])
        vsb = [p.sb(f"vsb_{k}", [128, 512], BF16) for k in range(CD)]
        usb = [p.sb(f"usb_{k}", [128, D], BF16) for k in range(4)]
        sel = p.sb("sel", [8, 4 * 128], BF16)
        ident = p.sb("ident", [128, 128], BF16)
        at = [p.sb(f"at{i}", [128, K_EXP]) for i in range(8)]
        atc = p.sb("atc", [128, 8])
        afm = p.sb("afm", [8, L])
        afm_bf = p.sb("afm_bf", [8, L], BF16)
        ab = [p.sb(f"ab{i}", [128, L]) for i in range(4)]
        vx = [p.sb(f"vx{i}", [128, L], BF16) for i in range(4)]
        xr = [p.sb(f"xr{i}", [128, L]) for i in range(2)]
        xo = [p.sb(f"xo{i}", [128, L]) for i in range(2)]
        pool = PsumPool(p, "mmp", 4)
        tp_ps = p.ps("tp_ps", [128, 128], BF16)
        ws1 = WStream(p, "w1s", w1, D, D, mg=512)

        def body():
            p.dma("sync", b1sb[:], b1[:, :], sem_key="cst")
            p.dma("sync", b2sb[:], b2[:, :], sem_key="cst")
            p.dma("gp", sel[:], sel_in[:, :])
            p.dma("gp", ident[:], ident_in[:, :])
            for k in range(CD):
                p.dma("gp", w2sb[k][:], w2[k * 128:(k + 1) * 128, :])
                p.dma("gp", vsb[k][:], v_t[k * 128:(k + 1) * 128, :])
            for k in range(4):
                p.dma("gp", usb[k][:], u_t[k * 128:(k + 1) * 128, :])
            for e in ("v", "s", "t", "gp"):
                p.wait_all(e, ("cst", "dgp"))
            p.op("v", lambda v: v.memset(cm["ones2"][:], 1.0))
            vd = p.op("v", lambda v: v.memset(cm["ones_row"][:], 1.0))
            p.wait("t", "v", vd)
            emit_ln_streamed(p, x1_in, xn, cm, rot2)

            def h_cb(m, n0, ps, tkey, tdone):
                p.wait("s", tkey, tdone)
                sd = p.op("s", lambda s: s.activation(
                    out=hbf[m][:, n0:n0 + 512], in_=ps[:, 0:512],
                    func=AF.Relu, bias=b1sb[:, m:m + 1]))
                return ("s", sd)
            emit_proj_stream(p, ws1, xn, h_cb, pool)
            p.barrier()

            for tt in range(8):
                j, ps, tkey = pool.next()
                for k in range(CD):
                    fn = (lambda t, k=k, tt=tt, ps=ps: t.matmul(
                        ps[0:128, 0:K_EXP], hbf[k][:, tt * 128:(tt + 1) * 128],
                        w2sb[k][:, 0:K_EXP], start=(k == 0), stop=(k == CD - 1)))
                    if k == CD - 1:
                        td = p.opk("t", tkey, fn)
                    else:
                        p.op("t", fn)
                p.wait("v", tkey, td)
                a = at[tt]
                vd = p.op("v", lambda v, a=a, ps=ps: v.tensor_add(
                    a[:], ps[0:128, 0:K_EXP], b2sb[:, :]))
                pool.done(j, "v", vd)
                p.op("v", lambda v, a=a: v.tensor_reduce(
                    atc[:, 0:1], a[:], AX.X, ALU.max))
                p.op("v", lambda v, a=a: v.tensor_scalar(
                    a[:], a[:], atc[:, 0:1], None, ALU.subtract))
                vd = p.counts["v"]
                p.wait("s", "v", vd)
                sd = p.op("s", lambda s, a=a: s.activation(out=a[:], in_=a[:],
                                                           func=AF.Exp))
                p.wait("v", "s", sd)
                p.op("v", lambda v, a=a: v.tensor_reduce(
                    atc[:, 1:2], a[:], AX.X, ALU.add))
                p.op("v", lambda v: v.reciprocal(atc[:, 2:3], atc[:, 1:2]))
                vd = p.op("v", lambda v, a=a: v.tensor_scalar(
                    a[:], a[:], atc[:, 2:3], None, ALU.mult))
                p.op("v", lambda v, a=a: v.tensor_copy(atc[:, 0:8], a[:]))
            p.barrier()
            abf = at
            for tt in range(8):
                p.wait_t("v")
                vd = p.op("v", lambda v, tt=tt: v.tensor_copy(
                    vx[0][:, tt * 8:(tt + 1) * 8], at[tt][:]))
            p.barrier()
            for tt in range(8):
                td = p.opk("t", "t", lambda t, tt=tt: t.transpose(
                    tp_ps[0:8, 0:128], vx[0][:, tt * 8:(tt + 1) * 8], ident[:]))
                p.wait("v", td)
                vd = p.op("v", lambda v, tt=tt: v.tensor_copy(
                    afm[0:8, tt * 128:(tt + 1) * 128], tp_ps[0:8, 0:128]))
                p.wait("t", "v", vd)
            vd = p.op("v", lambda v: v.tensor_copy(afm_bf[:], afm[:]))
            p.barrier()
            for c in range(4):
                for n0 in range(0, L, 512):
                    td = p.opk("t", "t", lambda t, c=c, n0=n0: t.matmul(
                        cm["bc_ps"][:, 0:512], sel[:, c * 128:(c + 1) * 128],
                        afm_bf[0:8, n0:n0 + 512], start=True, stop=True))
                    p.wait("v", td)
                    vd = p.op("v", lambda v, c=c, n0=n0: v.tensor_copy(
                        ab[c][:, n0:n0 + 512], cm["bc_ps"][:, 0:512]))
                    p.wait("t", "v", vd)
            p.barrier()
            for c in range(4):
                for n0 in range(0, L, 512):
                    j, ps, tkey = pool.next()
                    for k in range(CD):
                        fn = (lambda t, k=k, c=c, n0=n0, ps=ps: t.matmul(
                            ps[:, 0:512], vsb[k][:, c * 128:(c + 1) * 128],
                            xn[k][:, n0:n0 + 512], start=(k == 0),
                            stop=(k == CD - 1)))
                        if k == CD - 1:
                            td = p.opk("t", tkey, fn)
                        else:
                            p.op("t", fn)
                    p.wait("v", tkey, td)
                    vd = p.op("v", lambda v, c=c, n0=n0, ps=ps: v.tensor_mul(
                        vx[c][:, n0:n0 + 512], ps[:, 0:512],
                        ab[c][:, n0:n0 + 512]))
                    pool.done(j, "v", vd)
            p.barrier()
            for m in range(CD):
                xrb, xob = xr[m % 2], xo[m % 2]
                p.wait("sync", "v")
                p.wait("sync", "sync")
                p.dma("sync", xrb[:], x1_in[m * 128:(m + 1) * 128, :])
                for n0 in range(0, L, 512):
                    j, ps, tkey = pool.next()
                    for k in range(4):
                        fn = (lambda t, k=k, m=m, n0=n0, ps=ps: t.matmul(
                            ps[:, 0:512], usb[k][:, m * 128:(m + 1) * 128],
                            vx[k][:, n0:n0 + 512], start=(k == 0),
                            stop=(k == 3)))
                        if k == 3:
                            td = p.opk("t", tkey, fn)
                        else:
                            p.op("t", fn)
                    p.wait("v", tkey, td)
                    p.wait("v", "sync")
                    vd = p.op("v", lambda v, n0=n0, ps=ps, xrb=xrb, xob=xob:
                              v.tensor_add(xob[:, n0:n0 + 512], ps[:, 0:512],
                                           xrb[:, n0:n0 + 512]))
                    pool.done(j, "v", vd)
                p.wait("sync", "v")
                p.dma("sync", x2_out[m * 128:(m + 1) * 128, :], xob[:])
            p.barrier()
            p.wait("sync", "sync")
        return body



H1 = 16
DH1 = 128
T1 = 128
NCH1 = L // T1


def stage_s6(p, io):
    x2_in = io["x2"]
    wq_in = io["wq"]
    wk_in = io["wk"]
    wv_in = io["wv"]
    wg_in = io["wg"]
    bg_in = io["bg"]
    wo_in = io["wo"]
    bo_in = io["bo"]
    s0_in = io["s0"]
    z0_in = io["z0"]
    trow_in = io["trow"]
    iotaj = io["iotaj"]
    ssign_in = io["ssign"]
    maskut_in = io["maskut"]
    ones128_in = io["ones128"]
    ident_in = io["ident128"]

    x3_out = io["x3"]
    send_out = io["s_end"]
    crow_dram = p.nc.dram_tensor(p.pref + "crow_dram", [H1, L], F32)
    zend_out = io["z_end"]

    LN2PI = 1.8378770664093453
    TWO_PI = 6.283185307179586
    INV_2PI = 1.0 / TWO_PI
    LN1E4 = 9.210340371976184

    if True:
        cm = ln_common(p, row_w=512)
        cm["row_ps2"] = p.ps("ln_row_ps2", [1, 512])
        xn = p.fm("xn", dtype=BF16)
        wqh = [p.sb(f"wqh_{i}", [128, CD * 128], BF16) for i in range(2)]
        wkh = [p.sb(f"wkh_{i}", [128, CD * 128], BF16) for i in range(2)]
        wvh = [p.sb(f"wvh_{i}", [128, CD * 128], BF16) for i in range(2)]
        wvk = [p.new_key("wvh0"), p.new_key("wvh1")]
        crk = p.new_key("crk")
        szk = p.new_key("szk")
        wg = [p.sb(f"wg_{k}", [128, H1], BF16) for k in range(CD)]
        bgc = p.sb("bgc", [H1, 1])
        ybf = p.fm("ybf", dtype=BF16)
        beta = p.sb("beta", [H1, L])
        crow = p.sb("crow", [H1, L])
        onesL = p.sb("onesL", [H1, T1])
        crow_row = p.sb("crow_row", [1, L])
        maskut = p.sb("maskut_sb", [128, 128])
        ones128 = p.sb("ones128_sb", [128, 1], BF16)
        invf = p.sb("invf", [128, 1])
        ssign = p.sb("ssign_sb", [128, 1])
        costab = p.sb("costab", [128, L])
        sintab = p.sb("sintab", [128, L])
        qf = p.sb("qf", [128, L])
        kf = p.sb("kf", [128, L])
        qrot = p.sb("qrot", [128, L])
        krot = p.sb("krot", [128, L])
        rot2 = [qrot, krot]

        cb_h = p.sb("cb_h", [128, L])
        pb = p.sb("pb", [128, L])
        pib = p.sb("pib", [128, L])
        ptd = p.sb("ptd", [128, L])
        qp_bf = p.sb("qp_bf", [128, L], BF16)
        kp_bf = p.sb("kp_bf", [128, L], BF16)
        kpp_bf = p.sb("kpp_bf", [128, L], BF16)
        vtm_bf = p.sb("vtm_bf", [128, L], BF16)
        kpp_tm = p.sb("kpp_tm", [128, T1], BF16)
        kpp_seg = p.sb("kpp_seg", [128, T1], BF16)
        qp_scr = p.sb("qp_scr", [128, L])
        den_bf = p.sb("den_bf", [128, 1], BF16)
        identity_bf = p.sb("identity_bf", [128, 128], BF16)
        yck2 = p.sb("yck2", [128, 512])
        sstate = p.sb("sstate", [128, DH1])
        sstate_bf = p.sb("sstate_bf", [128, DH1], BF16)
        zstate = p.sb("zstate", [128, 1])
        zstate_bf = p.sb("zstate_bf", [128, 1], BF16)
        scm = p.sb("scm", [128, T1])
        scm_bf = p.sb("scm_bf", [128, T1], BF16)
        den = p.sb("den", [128, 1])
        denb = p.sb("denb", [128, T1])
        dent = p.sb("dent", [1, 128])
        yck = p.sb("yck", [128, T1])
        bosb = p.sb("bosb", [128, CD])
        pool = PsumPool(p, "mmp", 4)
        tp_ps = p.ps("tp_ps", [128, 128], BF16)
        ws_o = WStream(p, "wos", wo_in, D, D, mg=256)

        def body():
            for k in range(CD):
                p.dma("gp", wg[k][:], wg_in[k * 128:(k + 1) * 128, :])
            p.dma("gp", ones128[:], ones128_in[:, :])
            p.dma("sync", bgc[:], bg_in[:, :], sem_key="cst")
            p.dma("sync", maskut[:], maskut_in[:, :], sem_key="cst")
            p.dma("sync", pb[0:1, :], trow_in[:, :], sem_key="cst")
            p.dma("sync", invf[:], iotaj[:, :], sem_key="cst")
            p.dma("sync", ssign[:], ssign_in[:, :], sem_key="cst")
            p.dma("sync", bosb[:], bo_in[:, :], sem_key="cst")
            p.dma("gp", identity_bf[:], ident_in[:, :])
            for e in ("v", "s", "t", "gp"):
                p.wait_all(e, ("cst", "dgp"))
            p.op("v", lambda v: v.memset(cm["ones2"][:], 1.0))
            p.op("v", lambda v: v.memset(cm["ones_row"][:], 1.0))
            vd = p.op("v", lambda v: v.memset(onesL[:], 1.0))
            p.wait("t", "v", vd)
            p.wait("s", "v", vd)

            rscr = [qp_scr, qf, kf]
            p.op("s", lambda s: s.activation(out=invf[:], in_=invf[:],
                                             func=AF.Exp, scale=-LN1E4 / 64.0))
            sd = p.counts["s"]
            p.wait("t", "s", sd)
            for n0 in range(0, L, 512):
                td = p.opk("t", "t", lambda t, n0=n0: t.matmul(
                    cm["bc_ps"][:, 0:512], cm["ones_row"][0:1, :],
                    pb[0:1, n0:n0 + 512], start=True, stop=True))
                p.wait("v", td)
                vd = p.op("v", lambda v, n0=n0: v.tensor_copy(
                    costab[:, n0:n0 + 512], cm["bc_ps"][:, 0:512]))
                p.wait("t", "v", vd)
            p.wait("v", "s", sd)
            p.op("v", lambda v: v.tensor_scalar(costab[:, :], costab[:, :],
                                                invf[:, 0:1], None, ALU.mult))
            p.op("v", lambda v: v.tensor_scalar(rscr[0][:, :], costab[:, :],
                                                INV_2PI, None, ALU.mult))
            p.op("v", lambda v: v.memset(rscr[1][:, :], 0.0))
            for bit in (256.0, 128.0, 64.0, 32.0, 16.0, 8.0, 4.0, 2.0, 1.0):
                p.op("v", lambda v, bit=bit: v.tensor_scalar(
                    rscr[2][:, :], rscr[0][:, :], bit, None, ALU.is_ge))
                p.op("v", lambda v, bit=bit: v.scalar_tensor_tensor(
                    rscr[0][:, :], rscr[2][:, :], -bit, rscr[0][:, :],
                    ALU.mult, ALU.add))
                p.op("v", lambda v, bit=bit: v.scalar_tensor_tensor(
                    rscr[1][:, :], rscr[2][:, :], bit, rscr[1][:, :],
                    ALU.mult, ALU.add))
            p.op("v", lambda v: v.tensor_scalar(rscr[1][:, :], rscr[0][:, :],
                                                0.5, None, ALU.is_ge))
            p.op("v", lambda v: v.tensor_sub(rscr[2][:, :], rscr[0][:, :],
                                             rscr[1][:, :]))
            vd = p.counts["v"]
            p.wait("s", "v", vd)
            sd = p.op("s", lambda s: s.activation(out=sintab[:, :],
                                             in_=rscr[2][:, :],
                                             func=AF.Sin, scale=TWO_PI))
            p.wait("v", "s", sd)
            p.op("v", lambda v: v.tensor_scalar(rscr[0][:, :], rscr[0][:, :],
                                                0.25, None, ALU.add))
            p.op("v", lambda v: v.tensor_scalar(rscr[1][:, :], rscr[0][:, :],
                                                1.0, None, ALU.is_ge))
            p.op("v", lambda v: v.tensor_sub(rscr[0][:, :], rscr[0][:, :],
                                             rscr[1][:, :]))
            p.op("v", lambda v: v.tensor_scalar(rscr[1][:, :], rscr[0][:, :],
                                                0.5, None, ALU.is_ge))
            vd = p.op("v", lambda v: v.tensor_sub(rscr[2][:, :],
                                                  rscr[0][:, :],
                                                  rscr[1][:, :]))
            p.wait("s", "v", vd)
            sd = p.op("s", lambda s: s.activation(out=costab[:, :],
                                                  in_=rscr[2][:, :],
                                                  func=AF.Sin, scale=TWO_PI))
            p.wait("v", "s", sd)
            p.op("v", lambda v: v.tensor_scalar(sintab[:, :], sintab[:, :],
                                                ssign[:, 0:1], None, ALU.mult))
            p.barrier()

            emit_ln_streamed(p, x2_in, xn, cm, rot2)

            for n0 in range(0, L, 512):
                j, ps, tkey = pool.next()
                for k in range(CD):
                    fn = (lambda t, k=k, n0=n0, ps=ps: t.matmul(
                        ps[0:H1, 0:512], wg[k][:, 0:H1],
                        xn[k][:, n0:n0 + 512], start=(k == 0),
                        stop=(k == CD - 1)))
                    if k == CD - 1:
                        td = p.opk("t", tkey, fn)
                    else:
                        p.op("t", fn)
                p.wait("s", tkey, td)
                sd = p.op("s", lambda s, n0=n0, ps=ps: s.activation(
                    out=beta[:, n0:n0 + 512], in_=ps[0:H1, 0:512],
                    func=AF.Sigmoid, bias=bgc[:, 0:1]))
                pool.done(j, "s", sd)
            p.wait("v", "s")
            p.op("v", lambda v: v.tensor_scalar(beta[:], beta[:], 0.8, None,
                                                ALU.max))
            vd = p.op("v", lambda v: v.tensor_scalar(beta[:], beta[:], 0.9995,
                                                     None, ALU.min))
            p.wait("s", "v", vd)
            sd = p.op("s", lambda s: s.activation(out=crow[:], in_=beta[:],
                                                  func=AF.Ln))
            p.wait("v", "s", sd)
            scnt = p.sb if False else None
            for c in range(NCH1):
                p.op("v", lambda v, c=c: v.tensor_tensor_scan(
                    beta[:, c * T1:(c + 1) * T1], onesL[:, 0:T1],
                    crow[:, c * T1:(c + 1) * T1], 0.0, ALU.mult, ALU.add))
            vd = p.op("v", lambda v: v.tensor_copy(crow[:], beta[:]))
            p.wait("sync", "v", vd)
            p.dma("sync", crow_dram[:, :], crow[:])
            p.barrier()

            wv_loaded = {}
            wv_consumed = {}

            def wv_issue(h):
                if h >= H1:
                    return
                par = h % 2
                if h >= 2 and (h - 2) in wv_consumed:
                    for k_, v_ in wv_consumed[h - 2]:
                        p.wait("gp", k_, v_)
                for (whbm, wsb) in ((wq_in, wqh), (wk_in, wkh), (wv_in, wvh)):
                    srcv = whbm[0:D, h * 128:(h + 1) * 128].rearrange(
                        "(k p) m -> p k m", p=128)
                    dstv = wsb[par].ap().rearrange("p (k m) -> p k m", m=128)
                    wv_loaded[h] = p.dma("gp", dstv, srcv, sem_key=wvk[par])

            wv_issue(0)
            wv_issue(1)
            for h in range(H1):
                p.wait("t", wvk[h % 2], wv_loaded[h])
                for (dst, wsb) in ((qf, wqh), (kf, wkh)):
                    wsrc = wsb[h % 2]
                    for n0 in range(0, L, 512):
                        j, ps, tkey = pool.next()
                        for kk in range(CD):
                            fn = (lambda t, kk=kk, n0=n0, ps=ps, wsrc=wsrc:
                                  t.matmul(ps[:, 0:512],
                                           wsrc[:, kk * 128:(kk + 1) * 128],
                                           xn[kk][:, n0:n0 + 512],
                                           start=(kk == 0), stop=(kk == CD - 1)))
                            if kk == CD - 1:
                                td = p.opk("t", tkey, fn)
                            else:
                                p.op("t", fn)
                        p.wait("v", tkey, td)
                        vd = p.op("v", lambda v, n0=n0, ps=ps, dst=dst:
                                  v.tensor_copy(dst[:, n0:n0 + 512],
                                                ps[:, 0:512]))
                        pool.done(j, "v", vd)
                wvb = wvh[h % 2]
                for tt in range(NCH1):
                    j, ps, tkey = pool.next()
                    for kk in range(CD):
                        fn = (lambda t, kk=kk, tt=tt, ps=ps, wvb=wvb: t.matmul(
                            ps[:, 0:128], xn[kk][:, tt * 128:(tt + 1) * 128],
                            wvb[:, kk * 128:(kk + 1) * 128],
                            start=(kk == 0), stop=(kk == CD - 1)))
                        if kk == CD - 1:
                            td = p.opk("t", tkey, fn)
                        else:
                            p.op("t", fn)
                    p.wait("t", wvk[h % 2], wv_loaded[h])
                    p.wait("v", tkey, td)
                    vd = p.op("v", lambda v, tt=tt, ps=ps: v.tensor_copy(
                        vtm_bf[:, tt * 128:(tt + 1) * 128], ps[:, 0:128]))
                    pool.done(j, "v", vd)
                wv_consumed[h] = [(k_, p.counts[k_]) for k_ in pool.keys]
                idmask = list(range(32))
                for (src, dst) in ((qf, qrot), (kf, krot)):
                    p.op("v", lambda v, src=src: v.stream_shuffle(
                        qp_scr[0:64, :], src[64:128, :], idmask))
                    p.op("v", lambda v, src=src: v.stream_shuffle(
                        qp_scr[64:128, :], src[0:64, :], idmask))
                    p.op("v", lambda v, src=src, dst=dst: v.tensor_mul(
                        dst[:, :], src[:, :], costab[:, :]))
                    p.op("v", lambda v: v.tensor_mul(
                        qp_scr[:, :], qp_scr[:, :], sintab[:, :]))
                    vd = p.op("v", lambda v, dst=dst: v.tensor_add(
                        dst[:, :], dst[:, :], qp_scr[:, :]))
                for (src, dst) in ((qrot, qf), (krot, kf)):
                    vd = p.op("v", lambda v, src=src: v.tensor_scalar(
                        qp_scr[:, :], src[:, :], 0.0, None, ALU.min))
                    p.wait("s", "v", vd)
                    sd = p.op("s", lambda s, dst=dst: s.activation(
                        out=dst[:, :], in_=qp_scr[:, :], func=AF.Exp))
                    sd = p.op("s", lambda s, src=src, dst=dst: s.activation(
                        out=qp_scr[:, :], in_=src[:, :], func=AF.Relu))
                    p.wait("v", "s", sd)
                    vd = p.op("v", lambda v, dst=dst: v.tensor_add(
                        dst[:, :], dst[:, :], qp_scr[:, :]))
                p.wait("sync", "v")
                p.wait("sync", "sync")
                ld = p.dma("sync", crow_row[:], crow_dram[h:h + 1, :],
                           sem_key=crk)
                p.wait("t", crk, ld)
                for n0 in range(0, L, 512):
                    td = p.opk("t", "t", lambda t, n0=n0: t.matmul(
                        cm["bc_ps"][:, 0:512], cm["ones_row"][0:1, :],
                        crow_row[0:1, n0:n0 + 512], start=True, stop=True))
                    p.wait("v", td)
                    vd = p.op("v", lambda v, n0=n0: v.tensor_copy(
                        cb_h[:, n0:n0 + 512], cm["bc_ps"][:, 0:512]))
                    p.wait("t", "v", vd)
                p.wait("s", "v", vd)
                p.op("s", lambda s: s.activation(out=pb[:], in_=cb_h[:],
                                                 func=AF.Exp))
                sd = p.op("s", lambda s: s.activation(out=pib[:], in_=cb_h[:],
                                                      func=AF.Exp, scale=-1.0))
                for c in range(NCH1):
                    sd = p.op("s", lambda s, c=c: s.activation(
                        out=ptd[:, c * T1:(c + 1) * T1],
                        in_=cb_h[:, c * T1:(c + 1) * T1], func=AF.Exp,
                        scale=-1.0,
                        bias=cb_h[:, (c + 1) * T1 - 1:(c + 1) * T1]))
                p.wait("v", "s", sd)
                p.op("v", lambda v: v.tensor_mul(qp_scr[:], qf[:], pb[:]))
                p.op("v", lambda v: v.tensor_copy(qp_bf[:], qp_scr[:]))
                p.op("v", lambda v: v.tensor_mul(qp_scr[:], kf[:], pib[:]))
                p.op("v", lambda v: v.tensor_copy(kp_bf[:], qp_scr[:]))
                p.op("v", lambda v: v.tensor_mul(qp_scr[:], kf[:], ptd[:]))
                vd = p.op("v", lambda v: v.tensor_copy(kpp_bf[:], qp_scr[:]))
                ld = p.dma("sync", sstate[:], s0_in[h * DH1:(h + 1) * DH1, :],
                           sem_key=szk)
                ld = p.dma("sync", zstate[:], z0_in[h * DH1:(h + 1) * DH1, :],
                           sem_key=szk)
                p.wait("v", szk, ld)
                p.op("v", lambda v: v.tensor_copy(sstate_bf[:], sstate[:]))
                vd = p.op("v", lambda v: v.tensor_copy(zstate_bf[:],
                                                       zstate[:]))
                p.wait("t", "v", vd)
                for c in range(NCH1):
                    c0 = c * T1
                    j, ps, tkey = pool.next()
                    td = p.opk("t", tkey, lambda t, c0=c0, ps=ps: t.matmul(
                        ps[:, 0:T1], kp_bf[:, c0:c0 + T1],
                        qp_bf[:, c0:c0 + T1], start=True, stop=True))
                    p.wait("v", tkey, td)
                    vd = p.op("v", lambda v, ps=ps: v.tensor_mul(
                        scm[:, 0:T1], ps[:, 0:T1], maskut[:, :]))
                    pool.done(j, "v", vd)
                    vd = p.op("v", lambda v: v.tensor_copy(scm_bf[:], scm[:]))
                    p.wait("t", "v", vd)
                    j2, ps2, tkey2 = pool.next()
                    p.op("t", lambda t, ps2=ps2: t.matmul(
                        ps2[:, 0:1], scm_bf[:, 0:T1], ones128[:, 0:1],
                        start=True, stop=False))
                    td2 = p.opk("t", tkey2, lambda t, c0=c0, ps2=ps2: t.matmul(
                        ps2[:, 0:1], qp_bf[:, c0:c0 + T1], zstate_bf[:, 0:1],
                        start=False, stop=True))
                    p.wait("v", tkey2, td2)
                    p.op("v", lambda v, ps2=ps2: v.tensor_scalar(
                        den[:, 0:1], ps2[:, 0:1], 1e-6, None, ALU.add))
                    vd = p.op("v", lambda v: v.reciprocal(den[:, 0:1],
                                                          den[:, 0:1]))
                    pool.done(j2, "v", vd)
                    p.wait("t", "v", vd)
                    vd = p.op("v", lambda v: v.tensor_copy(
                        den_bf[:, 0:1], den[:, 0:1]))
                    p.wait("t", "v", vd)
                    td = p.opk("t", "t", lambda t: t.transpose(
                        tp_ps[0:1, 0:128], den_bf[:, 0:1], identity_bf[:, :]))
                    p.wait("v", td)
                    vd = p.op("v", lambda v: v.tensor_copy(dent[0:1, 0:128],
                                                           tp_ps[0:1, 0:128]))
                    p.wait("t", "v", vd)
                    td = p.opk("t", "t", lambda t: t.matmul(
                        cm["bc_ps"][:, 0:T1], cm["ones_row"][0:1, :],
                        dent[0:1, 0:128], start=True, stop=True))
                    p.wait("v", td)
                    vd = p.op("v", lambda v: v.tensor_copy(denb[:, 0:T1],
                                                           cm["bc_ps"][:, 0:T1]))
                    p.wait("t", "v", vd)
                    j3, ps3, tkey3 = pool.next()
                    p.op("t", lambda t, c=c, ps3=ps3: t.matmul(
                        ps3[:, 0:T1], vtm_bf[:, c * 128:(c + 1) * 128],
                        scm_bf[:, 0:T1], start=True, stop=False))
                    td3 = p.opk("t", tkey3, lambda t, c0=c0, ps3=ps3: t.matmul(
                        ps3[:, 0:T1], sstate_bf[:, 0:DH1],
                        qp_bf[:, c0:c0 + T1], start=False, stop=True))
                    p.wait("v", tkey3, td3)
                    vd = p.op("v", lambda v, ps3=ps3: v.tensor_mul(
                        yck[:, 0:T1], ps3[:, 0:T1], denb[:, 0:T1]))
                    pool.done(j3, "v", vd)
                    vd = p.op("v", lambda v, h=h, c0=c0: v.tensor_copy(
                        ybf[h][:, c0:c0 + T1], yck[:, 0:T1]))
                    vd = p.op("v", lambda v, c0=c0: v.tensor_copy(
                        kpp_seg[:, 0:T1], kpp_bf[:, c0:c0 + T1]))
                    p.wait("t", "v", vd)
                    td = p.opk("t", "t", lambda t: t.transpose(
                        tp_ps[:, 0:T1], kpp_seg[:, 0:T1], identity_bf[:, :]))
                    p.wait("v", td)
                    vd = p.op("v", lambda v: v.tensor_copy(kpp_tm[:, 0:T1],
                                                           tp_ps[:, 0:T1]))
                    p.wait("t", "v", vd)
                    j4, ps4, tkey4 = pool.next()
                    td4 = p.opk("t", tkey4, lambda t, c=c, ps4=ps4: t.matmul(
                        ps4[:, 0:DH1], kpp_tm[:, 0:T1],
                        vtm_bf[:, c * 128:(c + 1) * 128], start=True,
                        stop=True))
                    ptcol = pb[:, c0 + T1 - 1:c0 + T1]
                    p.op("v", lambda v, ptcol=ptcol: v.tensor_scalar(
                        sstate[:, :], sstate[:, :], ptcol, None, ALU.mult))
                    p.wait("v", tkey4, td4)
                    vd = p.op("v", lambda v, ps4=ps4: v.tensor_add(
                        sstate[:, :], sstate[:, :], ps4[:, 0:DH1]))
                    pool.done(j4, "v", vd)
                    p.op("v", lambda v: v.tensor_copy(sstate_bf[:],
                                                      sstate[:]))
                    p.op("v", lambda v, c0=c0: v.tensor_reduce(
                        den[:, 0:1], qp_scr[:, c0:c0 + T1], AX.X, ALU.add))
                    p.op("v", lambda v, ptcol=ptcol: v.tensor_scalar(
                        zstate[:, 0:1], zstate[:, 0:1], ptcol, None, ALU.mult))
                    p.op("v", lambda v: v.tensor_add(zstate[:, 0:1],
                                                     zstate[:, 0:1],
                                                     den[:, 0:1]))
                    vd = p.op("v", lambda v: v.tensor_copy(zstate_bf[:, 0:1],
                                                           zstate[:, 0:1]))
                    p.wait("t", "v", vd)
                wv_issue(h + 2)
                p.wait("sync", "v")
                p.dma("sync", send_out[h * DH1:(h + 1) * DH1, :], sstate[:])
                p.dma("sync", zend_out[h * DH1:(h + 1) * DH1, :], zstate[:])
            p.barrier()
            xr = [qf, kf]
            xo = [qrot, krot]
            def out_cb(m, n0, ps, tkey, tdone):
                xrb, xob = xr[m % 2], xo[m % 2]
                if n0 == 0:
                    p.wait("sync", "v")
                    p.wait("sync", "sync")
                    p.dma("sync", xrb[:], x2_in[m * 128:(m + 1) * 128, :])
                p.wait("s", tkey, tdone)
                p.wait("s", "v")
                sd = p.op("s", lambda s, m=m, n0=n0, ps=ps: s.activation(
                    out=yck2[:, 0:512], in_=ps[:, 0:512], func=AF.Identity,
                    bias=bosb[:, m:m + 1]))
                p.wait("v", "s", sd)
                p.wait("v", "sync")
                vd = p.op("v", lambda v, n0=n0, xrb=xrb, xob=xob: v.tensor_add(
                    xob[:, n0:n0 + 512], yck2[:, 0:512], xrb[:, n0:n0 + 512]))
                if n0 + 512 >= L:
                    p.wait("sync", "v", vd)
                    p.dma("sync", x3_out[m * 128:(m + 1) * 128, :], xob[:])
                return ("v", vd)
            emit_proj_stream(p, ws_o, ybf, out_cb, pool)
            p.barrier()
            p.wait("sync", "sync")
        return body



def stage_s7a(p, io):
    x3_in = io["x3"]
    identf_in = io["identf"]
    xn_out = io["xn_ln"]
    amax_out = io["amax"]
    if True:
        cm = ln_common(p, row_w=512)
        cm["row_ps2"] = p.ps("ln_row_ps2", [1, 512])
        rot2 = [p.sb(f"rot{i}", [128, L]) for i in range(2)]
        xn = p.fm("xn")
        identf = p.sb("identf_sb", [128, 128])
        rmax = p.sb("rmax", [128, 4])
        rrow = p.sb("rrow", [1, 128])
        tp_ps = p.ps("tp_ps", [128, 128])

        def body():
            p.dma("sync", identf[:], identf_in[:, :], sem_key="cst")
            p.wait_all("v", ("cst",))
            p.op("v", lambda v: v.memset(cm["ones2"][:], 1.0))
            p.op("v", lambda v: v.memset(cm["ones_row"][:], 1.0))
            vd = p.op("v", lambda v: v.memset(rmax[:], 0.0))
            p.wait("t", "v", vd)
            emit_ln_streamed(p, x3_in, xn, cm, rot2)
            p.wait_all("sync", ("v",))
            for i in range(CD):
                p.dma("sync", xn_out[i * 128:(i + 1) * 128, :], xn[i][:])
                p.op("v", lambda v, i=i: v.tensor_reduce(
                    rmax[:, 1:2], xn[i][:], AX.X, ALU.max))
                p.op("v", lambda v, i=i: v.tensor_reduce(
                    rmax[:, 2:3], xn[i][:], AX.X, ALU.min))
                p.op("v", lambda v: v.tensor_scalar(
                    rmax[:, 2:3], rmax[:, 2:3], -1.0, None, ALU.mult))
                p.op("v", lambda v: v.tensor_tensor(
                    rmax[:, 1:2], rmax[:, 1:2], rmax[:, 2:3], ALU.max))
                vd = p.op("v", lambda v: v.tensor_tensor(
                    rmax[:, 0:1], rmax[:, 0:1], rmax[:, 1:2], ALU.max))
            p.wait("t", "v", vd)
            td = p.opk("t", "t", lambda t: t.transpose(
                tp_ps[0:1, 0:128], rmax[:, 0:1], identf[:]))
            p.wait("v", td)
            vd = p.op("v", lambda v: v.tensor_copy(rrow[0:1, :],
                                                   tp_ps[0:1, 0:128]))
            vd = p.op("v", lambda v: v.tensor_reduce(
                rmax[0:1, 3:4], rrow[0:1, :], AX.X, ALU.max))
            p.wait("sync", "v", vd)
            p.dma("sync", amax_out[0:1, 0:1], rmax[0:1, 3:4])
            p.wait("sync", "sync")
        return body


def emit_fp8_quant(p, eng, src_view, dst_bf, q, rs_col):
    l = dst_bf.shape[1]
    E = eng
    p.op(E, lambda e: e.tensor_scalar(q["u"][:, :l], src_view, rs_col, None,
                                      ALU.mult))
    ed = p.counts[E]
    p.wait("s", E, ed)
    p.op("s", lambda s: s.activation(out=q["sg"][:, :l], in_=q["u"][:, :l],
                                     func=AF.Sign))
    sd = p.op("s", lambda s: s.activation(out=q["u"][:, :l], in_=q["u"][:, :l],
                                          func=AF.Abs, scale=64.0))
    p.wait(E, "s", sd)
    p.op(E, lambda e: e.memset(q["s"][:, :l], 2.0 ** -9))
    for f in (256.0, 16.0, 4.0, 2.0):
        p.op(E, lambda e, f=f: e.tensor_scalar(q["c"][:, :l], q["u"][:, :l],
                                               f, None, ALU.is_ge))
        p.op(E, lambda e, f=f: e.tensor_scalar(q["m"][:, :l], q["c"][:, :l],
                                               1.0 / f - 1.0, 1.0, ALU.mult,
                                               ALU.add))
        p.op(E, lambda e: e.tensor_mul(q["u"][:, :l], q["u"][:, :l],
                                       q["m"][:, :l]))
        p.op(E, lambda e, f=f: e.tensor_scalar(q["m"][:, :l], q["c"][:, :l],
                                               f - 1.0, 1.0, ALU.mult,
                                               ALU.add))
        p.op(E, lambda e: e.tensor_mul(q["s"][:, :l], q["s"][:, :l],
                                       q["m"][:, :l]))
    p.op(E, lambda e: e.tensor_scalar(q["u"][:, :l], q["u"][:, :l], 8.0, None,
                                      ALU.mult))
    p.op(E, lambda e: e.tensor_copy(q["r0"][:, :l], q["u"][:, :l]))
    for f in (8.0, 4.0, 2.0, 1.0):
        p.op(E, lambda e, f=f: e.tensor_scalar(q["c"][:, :l], q["u"][:, :l],
                                               f, None, ALU.is_ge))
        p.op(E, lambda e, f=f: e.scalar_tensor_tensor(
            q["u"][:, :l], q["c"][:, :l], -f, q["u"][:, :l], ALU.mult,
            ALU.add))
    p.op(E, lambda e: e.tensor_scalar(q["c"][:, :l], q["u"][:, :l], 0.5, None,
                                      ALU.is_ge))
    p.op(E, lambda e: e.tensor_sub(q["r0"][:, :l], q["r0"][:, :l],
                                   q["u"][:, :l]))
    p.op(E, lambda e: e.tensor_add(q["r0"][:, :l], q["r0"][:, :l],
                                   q["c"][:, :l]))
    p.op(E, lambda e: e.tensor_mul(q["r0"][:, :l], q["r0"][:, :l],
                                   q["s"][:, :l]))
    return p.op(E, lambda e: e.tensor_mul(dst_bf[:, :l], q["r0"][:, :l],
                                          q["sg"][:, :l]))


def stage_s7b(p, io):
    x3_in = io["x3"]
    xn_in = io["xn_ln"]
    amax8 = io["amax8"]
    w_in = io["w_fp8_t"]
    wsc_in = io["wscale"]
    bias_in = io["bias"]
    x4_out = io["x4"]
    if True:
        xq = p.fm("xq", dtype=BF16)
        q = {k: p.sb("q_" + k, [128, L]) for k in ("u", "s", "c", "m", "r0", "sg")}
        amx = p.sb("amx", [1, 8])
        wsc = p.sb("wsc", [1, 1])
        rs_bc = p.sb("rs_bc", [128, 1])
        ws_bc = p.sb("ws_bc", [128, 1])
        onesr = p.sb("onesr", [1, 128])
        biassb = p.sb("biassb", [128, CD])
        ld2 = [p.sb(f"ld2_{i}", [128, L]) for i in range(2)]
        xr = [p.sb(f"xr{i}", [128, L]) for i in range(2)]
        xo = [p.sb(f"xo{i}", [128, L]) for i in range(2)]
        ldk = [p.new_key("qld0"), p.new_key("qld1")]
        bps = p.ps("bps", [128, 512])
        pool = PsumPool(p, "mmp", 4)
        ws = WStream(p, "wf8", w_in, D, D, mg=512)

        def body():
            p.dma("sync", amx[:], amax8[:, :], sem_key="cst")
            p.dma("sync", wsc[:], wsc_in[:, :], sem_key="cst")
            p.dma("sync", biassb[:], bias_in[:, :], sem_key="cst")
            p.wait_all("v", ("cst",))
            p.op("v", lambda v: v.memset(onesr[:], 1.0))
            p.op("v", lambda v: v.tensor_reduce(amx[0:1, 0:1], amx[0:1, 0:8],
                                                AX.X, ALU.max))
            p.op("v", lambda v: v.tensor_scalar(amx[0:1, 0:1], amx[0:1, 0:1],
                                                1.0 / 448.0, None, ALU.mult))
            p.op("v", lambda v: v.tensor_scalar(amx[0:1, 0:1], amx[0:1, 0:1],
                                                1e-7, None, ALU.max))
            vd = p.op("v", lambda v: v.reciprocal(amx[0:1, 1:2],
                                                  amx[0:1, 0:1]))
            p.wait("t", "v", vd)
            for (srcap, dst) in ((amx[0:1, 1:2], rs_bc), (wsc[0:1, 0:1], ws_bc)):
                td = p.opk("t", "t", lambda t, srcap=srcap: t.matmul(
                    bps[:, 0:1], onesr[0:1, :], srcap, start=True, stop=True))
                p.wait("v", td)
                vd = p.op("v", lambda v, dst=dst: v.tensor_copy(dst[:, 0:1],
                                                                bps[:, 0:1]))
                p.wait("t", "v", vd)
            for i in range(CD):
                buf = ld2[i % 2]
                p.wait("sync", "v")
                ld = p.dma("sync", buf[:], xn_in[i * 128:(i + 1) * 128, :],
                           sem_key=ldk[i % 2])
                p.wait("v", ldk[i % 2], ld)
                emit_fp8_quant(p, "v", buf[:, :], xq[i], q, rs_bc[:, 0:1])
            p.barrier()

            def out_cb(m, n0, ps, tkey, tdone):
                xrb, xob = xr[m % 2], xo[m % 2]
                if n0 == 0:
                    p.wait("sync", "v")
                    p.wait("sync", "sync")
                    p.dma("sync", xrb[:], x3_in[m * 128:(m + 1) * 128, :])
                p.wait("v", tkey, tdone)
                p.wait("v", "sync")
                p.op("v", lambda v, n0=n0, xrb=xrb, xob=xob, ps=ps:
                     v.scalar_tensor_tensor(xob[:, n0:n0 + 512], ps[:, 0:512],
                                            ws_bc[:, 0:1],
                                            xrb[:, n0:n0 + 512], ALU.mult,
                                            ALU.add))
                vd = p.op("v", lambda v, m=m, n0=n0, xob=xob: v.tensor_scalar(
                    xob[:, n0:n0 + 512], xob[:, n0:n0 + 512],
                    biassb[:, m:m + 1], None, ALU.add))
                if n0 + 512 >= L:
                    p.wait("sync", "v", vd)
                    p.dma("sync", x4_out[m * 128:(m + 1) * 128, :], xob[:])
                return ("v", vd)
            emit_proj_stream(p, ws, xq, out_cb, pool)
            p.barrier()
            p.wait("sync", "sync")
        return body



H2, DH2, T2 = 16, 128, 128
NCH2 = L // T2


def stage_s8(p, io):
    x4_in = io["x4"]
    wq_in = io["wq"]
    wk_in = io["wk"]
    wv_in = io["wv"]
    s0_in = io["s0"]
    z0_in = io["z0"]
    maskut_in = io["maskut"]
    ones128_in = io["ones128"]
    identf_in = io["identf"]

    sq_out = io["sigma_q"]
    num_out = io["num_loc"]
    den_out = io["den_loc"]
    send_out = io["s_end"]
    zend_out = io["z_end"]
    qbar_out = io["qbar"]

    if True:
        cm = ln_common(p, row_w=512)
        cm["row_ps2"] = p.ps("ln_row_ps2", [1, 512])
        xn = p.fm("xn", dtype=BF16)
        qf = p.sb("qf", [128, L])
        kf = p.sb("kf", [128, L])
        rot2 = [qf, kf]
        wqh = [p.sb(f"wqh_{i}", [128, CD * 128], BF16) for i in range(2)]
        wkh = [p.sb(f"wkh_{i}", [128, CD * 128], BF16) for i in range(2)]
        wvh = [p.sb(f"wvh_{i}", [128, CD * 128], BF16) for i in range(2)]
        wvk = [p.new_key("wvh0"), p.new_key("wvh1")]
        szk = p.new_key("szk")
        qp_bf = p.sb("qp_bf", [128, L], BF16)
        kp_bf = p.sb("kp_bf", [128, L], BF16)
        kp_f = p.sb("kp_f", [128, L])
        vtm_bf = p.sb("vtm_bf", [128, L], BF16)
        kp_tm = p.sb("kp_tm", [128, T2], BF16)
        sstate = p.sb("sstate", [128, DH2])
        sstate_bf = p.sb("sstate_bf", [128, DH2], BF16)
        zstate = p.sb("zstate", [128, 1])
        zstate_bf = p.sb("zstate_bf", [128, 1], BF16)
        scm = p.sb("scm", [128, T2])
        scm_bf = p.sb("scm_bf", [128, T2], BF16)
        den = p.sb("den", [128, 2])
        dent = p.sb("dent", [1, 128])
        den_row = p.sb("den_row", [1, L])
        numt = p.sb("numt", [128, L])
        qbar = p.sb("qbar_sb", [128, CD])
        maskut = p.sb("maskut_sb", [128, 128])
        ones128 = p.sb("ones128_sb", [128, 1], BF16)
        identf = p.sb("identf_sb", [128, 128])
        qp_scr = p.sb("qp_scr", [128, L])
        pool = PsumPool(p, "mmp", 4)
        tp_ps = p.ps("tp_ps", [128, 128])

        def body():
            p.dma("sync", maskut[:], maskut_in[:, :], sem_key="cst")
            p.dma("gp", ones128[:], ones128_in[:, :])
            p.dma("gp", identf[:], identf_in[:, :])
            for e in ("v", "s", "t", "gp"):
                p.wait_all(e, ("cst", "dgp"))
            p.op("v", lambda v: v.memset(cm["ones2"][:], 1.0))
            vd = p.op("v", lambda v: v.memset(cm["ones_row"][:], 1.0))
            p.wait("t", "v", vd)
            emit_ln_streamed(p, x4_in, xn, cm, rot2)

            wv_loaded, wv_consumed = {}, {}

            def wv_issue(h):
                if h >= H2:
                    return
                par = h % 2
                if h >= 2 and (h - 2) in wv_consumed:
                    for k_, v_ in wv_consumed[h - 2]:
                        p.wait("gp", k_, v_)
                for (whbm, wsb) in ((wq_in, wqh), (wk_in, wkh), (wv_in, wvh)):
                    srcv = whbm[0:D, h * 128:(h + 1) * 128].rearrange(
                        "(k p) m -> p k m", p=128)
                    dstv = wsb[par].ap().rearrange("p (k m) -> p k m", m=128)
                    wv_loaded[h] = p.dma("gp", dstv, srcv, sem_key=wvk[par])

            wv_issue(0)
            wv_issue(1)
            for h in range(H2):
                p.wait("t", wvk[h % 2], wv_loaded[h])
                p.wait("v", "sync")
                p.wait("s", "sync")
                for (dst, wsb) in ((qf, wqh), (kf, wkh)):
                    wsrc = wsb[h % 2]
                    for n0 in range(0, L, 512):
                        j, ps, tkey = pool.next()
                        for kk in range(CD):
                            fn = (lambda t, kk=kk, n0=n0, ps=ps, wsrc=wsrc:
                                  t.matmul(ps[:, 0:512],
                                           wsrc[:, kk * 128:(kk + 1) * 128],
                                           xn[kk][:, n0:n0 + 512],
                                           start=(kk == 0), stop=(kk == CD - 1)))
                            if kk == CD - 1:
                                td = p.opk("t", tkey, fn)
                            else:
                                p.op("t", fn)
                        p.wait("v", tkey, td)
                        vd = p.op("v", lambda v, n0=n0, ps=ps, dst=dst:
                                  v.tensor_copy(dst[:, n0:n0 + 512],
                                                ps[:, 0:512]))
                        pool.done(j, "v", vd)
                wvb = wvh[h % 2]
                for tt in range(NCH2):
                    j, ps, tkey = pool.next()
                    for kk in range(CD):
                        fn = (lambda t, kk=kk, tt=tt, ps=ps: t.matmul(
                            ps[:, 0:128], xn[kk][:, tt * 128:(tt + 1) * 128],
                            wvb[:, kk * 128:(kk + 1) * 128],
                            start=(kk == 0), stop=(kk == CD - 1)))
                        if kk == CD - 1:
                            td = p.opk("t", tkey, fn)
                        else:
                            p.op("t", fn)
                    p.wait("t", wvk[h % 2], wv_loaded[h])
                    p.wait("v", tkey, td)
                    vd = p.op("v", lambda v, tt=tt, ps=ps: v.tensor_copy(
                        vtm_bf[:, tt * 128:(tt + 1) * 128], ps[:, 0:128]))
                    pool.done(j, "v", vd)
                wv_consumed[h] = [(k_, p.counts[k_]) for k_ in pool.keys]
                wv_issue(h + 2)
                p.op("v", lambda v, h=h: v.tensor_reduce(
                    qbar[:, h:h + 1], qf[:, :], AX.X, ALU.add))
                for (src, dstf, dstb) in ((qf, qp_scr, qp_bf), (kf, kp_f, kp_bf)):
                    vd = p.op("v", lambda v, src=src: v.tensor_scalar(
                        numt[:, :], src[:, :], 0.0, None, ALU.min))
                    p.wait("s", "v", vd)
                    sd = p.op("s", lambda s, dstf=dstf: s.activation(
                        out=dstf[:, :], in_=numt[:, :], func=AF.Exp))
                    sd = p.op("s", lambda s, src=src: s.activation(
                        out=numt[:, :], in_=src[:, :], func=AF.Relu))
                    p.wait("v", "s", sd)
                    p.op("v", lambda v, dstf=dstf: v.tensor_add(
                        dstf[:, :], dstf[:, :], numt[:, :]))
                    vd = p.op("v", lambda v, dstf=dstf, dstb=dstb: v.tensor_copy(
                        dstb[:, :], dstf[:, :]))
                p.wait("sync", "v", vd)
                p.dma("sync", sq_out[h * 128:(h + 1) * 128, :], qp_scr[:])
                ld = p.dma("sync", sstate[:], s0_in[h * DH2:(h + 1) * DH2, :],
                           sem_key=szk)
                ld = p.dma("sync", zstate[:], z0_in[h * DH2:(h + 1) * DH2, :],
                           sem_key=szk)
                p.wait("v", szk, ld)
                p.op("v", lambda v: v.tensor_copy(sstate_bf[:], sstate[:]))
                vd = p.op("v", lambda v: v.tensor_copy(zstate_bf[:], zstate[:]))
                p.wait("t", "v", vd)
                for c in range(NCH2):
                    c0 = c * T2
                    j, ps, tkey = pool.next()
                    td = p.opk("t", tkey, lambda t, c0=c0, ps=ps: t.matmul(
                        ps[:, 0:T2], kp_bf[:, c0:c0 + T2], qp_bf[:, c0:c0 + T2],
                        start=True, stop=True))
                    p.wait("v", tkey, td)
                    vd = p.op("v", lambda v, ps=ps: v.tensor_mul(
                        scm[:, 0:T2], ps[:, 0:T2], maskut[:, :]))
                    pool.done(j, "v", vd)
                    vd = p.op("v", lambda v: v.tensor_copy(scm_bf[:], scm[:]))
                    p.wait("t", "v", vd)
                    j2, ps2, tkey2 = pool.next()
                    p.op("t", lambda t, ps2=ps2: t.matmul(
                        ps2[:, 0:1], scm_bf[:, 0:T2], ones128[:, 0:1],
                        start=True, stop=False))
                    td2 = p.opk("t", tkey2, lambda t, c0=c0, ps2=ps2: t.matmul(
                        ps2[:, 0:1], qp_bf[:, c0:c0 + T2], zstate_bf[:, 0:1],
                        start=False, stop=True))
                    p.wait("v", tkey2, td2)
                    vd = p.op("v", lambda v, ps2=ps2: v.tensor_copy(
                        den[:, 0:1], ps2[:, 0:1]))
                    pool.done(j2, "v", vd)
                    p.wait("t", "v", vd)
                    td = p.opk("t", "t", lambda t: t.transpose(
                        tp_ps[0:1, 0:128], den[:, 0:1], identf[:, :]))
                    p.wait("v", td)
                    vd = p.op("v", lambda v, c0=c0: v.tensor_copy(
                        den_row[0:1, c0:c0 + T2], tp_ps[0:1, 0:128]))
                    p.wait("t", "v", vd)
                    j3, ps3, tkey3 = pool.next()
                    p.op("t", lambda t, c=c, ps3=ps3: t.matmul(
                        ps3[:, 0:T2], vtm_bf[:, c * 128:(c + 1) * 128],
                        scm_bf[:, 0:T2], start=True, stop=False))
                    td3 = p.opk("t", tkey3, lambda t, c0=c0, ps3=ps3: t.matmul(
                        ps3[:, 0:T2], sstate_bf[:, 0:DH2],
                        qp_bf[:, c0:c0 + T2], start=False, stop=True))
                    p.wait("v", tkey3, td3)
                    vd = p.op("v", lambda v, c0=c0, ps3=ps3: v.tensor_copy(
                        numt[:, c0:c0 + T2], ps3[:, 0:T2]))
                    pool.done(j3, "v", vd)
                    td = p.opk("t", "t", lambda t, c0=c0: t.transpose(
                        tp_ps[:, 0:T2], kp_f[:, c0:c0 + T2], identf[:, :]))
                    p.wait("v", td)
                    vd = p.op("v", lambda v: v.tensor_copy(kp_tm[:, 0:T2],
                                                           tp_ps[:, 0:T2]))
                    p.wait("t", "v", vd)
                    j4, ps4, tkey4 = pool.next()
                    td4 = p.opk("t", tkey4, lambda t, c=c, ps4=ps4: t.matmul(
                        ps4[:, 0:DH2], kp_tm[:, 0:T2],
                        vtm_bf[:, c * 128:(c + 1) * 128], start=True,
                        stop=True))
                    p.wait("v", tkey4, td4)
                    vd = p.op("v", lambda v, ps4=ps4: v.tensor_add(
                        sstate[:, :], sstate[:, :], ps4[:, 0:DH2]))
                    pool.done(j4, "v", vd)
                    p.op("v", lambda v: v.tensor_copy(sstate_bf[:], sstate[:]))
                    p.op("v", lambda v, c0=c0: v.tensor_reduce(
                        den[:, 1:2], kp_f[:, c0:c0 + T2], AX.X, ALU.add))
                    p.op("v", lambda v: v.tensor_add(zstate[:, 0:1],
                                                     zstate[:, 0:1],
                                                     den[:, 1:2]))
                    vd = p.op("v", lambda v: v.tensor_copy(zstate_bf[:, 0:1],
                                                           zstate[:, 0:1]))
                    p.wait("t", "v", vd)
                p.barrier()
                p.wait("sync", "v")
                p.dma("sync", num_out[h * 128:(h + 1) * 128, :], numt[:])
                p.dma("sync", send_out[h * DH2:(h + 1) * DH2, :], sstate[:])
                p.dma("sync", zend_out[h * DH2:(h + 1) * DH2, :], zstate[:])
                p.dma("sync", den_out[h:h + 1, :], den_row[0:1, :])
            vd = p.op("v", lambda v: v.nop())
            p.wait("sync", "v")
            p.dma("sync", qbar_out[:, :], qbar[:])
            p.barrier()
            p.wait("sync", "sync")
        return body



def stage_s9(p, io):
    x4_in = io["x4"]
    s0c_in = io["s0c"]
    z0c_in = io["z0c"]
    sq_in = io["sigma_q"]
    num_in = io["num"]
    den_in = io["den"]
    qbar_a = io["qbar_a"]
    qbar_b = io["qbar_b"]
    mem_in = io["mem_t"]
    norm2_in = io["norm2"]
    gate_in = io["gate_bc"]
    diag_in = io["diag_mh"]
    identf_in = io["identf"]
    wo_in = io["wo_t"]
    x5_out = io["x5"]

    if True:
        sqh = [p.sb(f"sqh{i}", [128, L], BF16) for i in range(2)]
        numh = [p.sb(f"numh{i}", [128, L]) for i in range(2)]
        sqk = [p.new_key("sq0"), p.new_key("sq1")]
        s0b = [p.sb(f"s0b{i}", [128, DH2], BF16) for i in range(2)]
        s0k = [p.new_key("s0k0"), p.new_key("s0k1")]
        z0cf = p.sb("z0cf", [128, H2])
        z0cb = p.sb("z0cb", [128, H2], BF16)
        memb = [p.sb(f"memb{i}", [128, 4 * 128], BF16) for i in range(2)]
        memk = [p.new_key("mem0"), p.new_key("mem1")]
        norm2 = p.sb("norm2_sb", [128, 64], BF16)
        qbar = p.sb("qbar_t", [128, CD])
        qbarb = p.sb("qbarb", [128, CD], BF16)
        gate = p.sb("gate_sb", [128, H2])
        diag = p.sb("diag_sb", [64, H2])
        identf = p.sb("identf_sb", [128, 128])
        relc = p.sb("relc", [64, H2])
        relrow = p.sb("relrow", [1, 64])
        wrow = p.sb("wrow", [1, 64])
        den_row = p.sb("den_row", [1, L])
        rn_row = p.sb("rn_row", [1, L])
        denb = p.sb("denb", [128, L])
        macc = p.sb("macc", [128, L])
        tmp = p.sb("tmp", [128, L])
        yloc = p.sb("yloc", [128, L])
        ybf = p.fm("ybf", dtype=BF16)
        xr = [p.sb(f"xr{i}", [128, L]) for i in range(2)]
        xo = [p.sb(f"xo{i}", [128, L]) for i in range(2)]
        onesr = p.sb("onesr", [1, 128])
        drk = p.new_key("drk")
        pool = PsumPool(p, "mmp", 4)
        tp_ps = p.ps("tp_ps", [128, 128])
        bps = p.ps("bps", [128, 512])
        ws_o = WStream(p, "wos", wo_in, D, D, mg=256)

        def body():
            p.dma("sync", qbar[:], qbar_a[:, :], sem_key="cst")
            p.dma("sync", denb[:, 0:CD], qbar_b[:, :], sem_key="cst")
            p.dma("sync", gate[:], gate_in[:, :], sem_key="cst")
            p.dma("sync", diag[:], diag_in[:, :], sem_key="cst")
            p.dma("sync", identf[:], identf_in[:, :], sem_key="cst")
            p.dma("gp", norm2[:], norm2_in[:, :])
            p.dma("sync", z0cf[:], z0c_in[:, :], sem_key="cst")
            for e in ("v", "s", "t", "gp"):
                p.wait_all(e, ("cst", "dgp"))
            p.op("v", lambda v: v.tensor_copy(z0cb[:], z0cf[:]))
            p.op("v", lambda v: v.memset(onesr[:], 1.0))
            p.op("v", lambda v: v.tensor_add(qbar[:, :], qbar[:, :],
                                             denb[:, 0:CD]))
            vd = p.op("v", lambda v: v.tensor_copy(qbarb[:], qbar[:]))
            sd = p.op("s", lambda s: s.activation(out=gate[:], in_=gate[:],
                                                  func=AF.Sigmoid))
            p.wait("t", "v", vd)
            j, ps, tkey = pool.next()
            td = p.opk("t", tkey, lambda t, ps=ps: t.matmul(
                ps[0:64, 0:CD], norm2[:, :], qbarb[:, :], start=True,
                stop=True))
            p.wait("v", tkey, td)
            vd = p.op("v", lambda v, ps=ps: v.tensor_mul(relc[:, :],
                                                         ps[0:64, 0:CD],
                                                         diag[:, :]))
            pool.done(j, "v", vd)
            p.op("v", lambda v: v.tensor_reduce(relc[:, 0:1], relc[:, :],
                                                AX.X, ALU.add))
            vd = p.op("v", lambda v: v.tensor_scalar(
                relc[:, 0:1], relc[:, 0:1],
                float((128 ** -0.5) / 2048.0), None, ALU.mult))
            p.wait("t", "v", vd)
            td = p.opk("t", "t", lambda t: t.transpose(
                tp_ps[0:1, 0:64], relc[0:64, 0:1], identf[0:64, 0:64]))
            p.wait("v", td)
            vd = p.op("v", lambda v: v.tensor_copy(relrow[0:1, :],
                                                   tp_ps[0:1, 0:64]))
            p.op("v", lambda v: v.tensor_tensor(
                rn_row[0:1, 0:16], relrow[0:1, 0:16], relrow[0:1, 16:32],
                ALU.max))
            p.op("v", lambda v: v.tensor_tensor(
                rn_row[0:1, 16:32], relrow[0:1, 32:48], relrow[0:1, 48:64],
                ALU.max))
            p.op("v", lambda v: v.tensor_tensor(
                rn_row[0:1, 0:16], rn_row[0:1, 0:16], rn_row[0:1, 16:32],
                ALU.max))
            for m in range(4):
                vd = p.op("v", lambda v, m=m: v.tensor_sub(
                    wrow[0:1, m * 16:(m + 1) * 16],
                    relrow[0:1, m * 16:(m + 1) * 16], rn_row[0:1, 0:16]))
            p.wait("s", "v", vd)
            sd = p.op("s", lambda s: s.activation(out=wrow[0:1, :],
                                                  in_=wrow[0:1, :],
                                                  func=AF.Exp))
            p.wait("v", "s", sd)
            p.op("v", lambda v: v.tensor_tensor(
                relrow[0:1, 0:16], wrow[0:1, 0:16], wrow[0:1, 16:32], ALU.add))
            p.op("v", lambda v: v.tensor_tensor(
                relrow[0:1, 16:32], wrow[0:1, 32:48], wrow[0:1, 48:64],
                ALU.add))
            p.op("v", lambda v: v.tensor_add(
                relrow[0:1, 0:16], relrow[0:1, 0:16], relrow[0:1, 16:32]))
            p.op("v", lambda v: v.reciprocal(relrow[0:1, 32:48],
                                             relrow[0:1, 0:16]))
            for m in range(4):
                p.op("v", lambda v, m=m: v.tensor_mul(
                    wrow[0:1, m * 16:(m + 1) * 16],
                    wrow[0:1, m * 16:(m + 1) * 16], relrow[0:1, 32:48]))
            p.barrier()

            ld_issue = {}

            hd_consumed = {}

            def issue_head(h):
                if h >= H2:
                    return
                par = h % 2
                if h >= 2 and (h - 2) in hd_consumed:
                    for k_, v_ in hd_consumed[h - 2]:
                        p.wait("gp", k_, v_)
                ld1 = p.dma("gp", sqh[par][:], sq_in[h * 128:(h + 1) * 128, :],
                            sem_key=sqk[par])
                ld1 = p.dma("gp", numh[par][:],
                            num_in[h * 128:(h + 1) * 128, :],
                            sem_key=sqk[par])
                for m in range(4):
                    p.dma("gp", memb[par][:, m * 128:(m + 1) * 128],
                          mem_in[(m * H2 + h) * 128:(m * H2 + h + 1) * 128, :],
                          sem_key=memk[par])
                lds = p.dma("gp", s0b[par][:],
                            s0c_in[h * DH2:(h + 1) * DH2, :],
                            sem_key=s0k[par])
                ld_issue[h] = (p.counts[sqk[par]], p.counts[memk[par]], lds)

            issue_head(0)
            issue_head(1)
            for h in range(H2):
                par = h % 2
                if h >= 2:
                    pass
                p.wait("v", sqk[par], ld_issue[h][0])
                p.wait("t", sqk[par], ld_issue[h][0])
                p.wait("t", memk[par], ld_issue[h][1])
                p.wait("t", s0k[par], ld_issue[h][2])
                p.wait("sync", "v")
                ldd = p.dma("sync", den_row[0:1, :], den_in[h:h + 1, :],
                            sem_key=drk)
                p.wait("v", drk, ldd)
                for n0 in (0, 512):
                    jc, psc, tkc = pool.next()
                    tdc = p.opk("t", tkc, lambda t, n0=n0, psc=psc, par=par,
                                h=h: t.matmul(
                        psc[0:1, 0:512], z0cb[:, h:h + 1],
                        sqh[par][:, n0:n0 + 512], start=True, stop=True))
                    p.wait("v", tkc, tdc)
                    vdc = p.op("v", lambda v, n0=n0, psc=psc: v.tensor_add(
                        den_row[0:1, n0:n0 + 512], den_row[0:1, n0:n0 + 512],
                        psc[0:1, 0:512]))
                    pool.done(jc, "v", vdc)
                for n0 in (0, 512):
                    jc, psc, tkc = pool.next()
                    tdc = p.opk("t", tkc, lambda t, n0=n0, psc=psc, par=par:
                                t.matmul(
                        psc[:, 0:512], s0b[par][:, 0:DH2],
                        sqh[par][:, n0:n0 + 512], start=True, stop=True))
                    p.wait("v", tkc, tdc)
                    vdc = p.op("v", lambda v, n0=n0, psc=psc, par=par:
                               v.tensor_add(
                        numh[par][:, n0:n0 + 512], numh[par][:, n0:n0 + 512],
                        psc[:, 0:512]))
                    pool.done(jc, "v", vdc)
                p.op("v", lambda v: v.tensor_scalar(den_row[0:1, :],
                                                    den_row[0:1, :], 1e-6,
                                                    None, ALU.max))
                vd = p.op("v", lambda v: v.reciprocal(rn_row[0:1, :],
                                                      den_row[0:1, :]))
                p.wait("t", "v", vd)
                for n0 in range(0, L, 512):
                    td = p.opk("t", "t", lambda t, n0=n0: t.matmul(
                        bps[:, 0:512], onesr[0:1, :], rn_row[0:1, n0:n0 + 512],
                        start=True, stop=True))
                    p.wait("v", td)
                    vd = p.op("v", lambda v, n0=n0: v.tensor_copy(
                        denb[:, n0:n0 + 512], bps[:, 0:512]))
                    p.wait("t", "v", vd)
                vd = p.op("v", lambda v, par=par: v.tensor_mul(
                    yloc[:, :], numh[par][:, :], denb[:, :]))
                for m in range(4):
                    for n0 in range(0, L, 512):
                        j2, ps2, tkey2 = pool.next()
                        fn = (lambda t, m=m, h=h, n0=n0, ps2=ps2, par=par:
                              t.matmul(ps2[0:1, 0:512],
                                       norm2[:, m * 16 + h:m * 16 + h + 1],
                                       sqh[par][:, n0:n0 + 512], start=True,
                                       stop=True))
                        td2 = p.opk("t", tkey2, fn)
                        p.wait("v", tkey2, td2)
                        vd = p.op("v", lambda v, n0=n0, ps2=ps2: v.tensor_copy(
                            rn_row[0:1, n0:n0 + 512], ps2[0:1, 0:512]))
                        pool.done(j2, "v", vd)
                    p.op("v", lambda v: v.tensor_scalar(
                        rn_row[0:1, :], rn_row[0:1, :], 1e-6, None, ALU.max))
                    p.op("v", lambda v: v.reciprocal(rn_row[0:1, :],
                                                     rn_row[0:1, :]))
                    vd = p.op("v", lambda v, m=m, h=h: v.tensor_scalar(
                        rn_row[0:1, :], rn_row[0:1, :],
                        wrow[0:1, m * 16 + h:m * 16 + h + 1], None, ALU.mult))
                    p.wait("t", "v", vd)
                    for n0 in range(0, L, 512):
                        td = p.opk("t", "t", lambda t, n0=n0: t.matmul(
                            bps[:, 0:512], onesr[0:1, :],
                            rn_row[0:1, n0:n0 + 512], start=True, stop=True))
                        p.wait("v", td)
                        vd = p.op("v", lambda v, n0=n0: v.tensor_copy(
                            denb[:, n0:n0 + 512], bps[:, 0:512]))
                        p.wait("t", "v", vd)
                    for n0 in range(0, L, 512):
                        j, ps, tkey = pool.next()
                        td = p.opk("t", tkey, lambda t, m=m, n0=n0, ps=ps,
                                   par=par: t.matmul(
                            ps[:, 0:512], memb[par][:, m * 128:(m + 1) * 128],
                            sqh[par][:, n0:n0 + 512], start=True, stop=True))
                        p.wait("v", tkey, td)
                        p.op("v", lambda v, n0=n0, ps=ps: v.tensor_mul(
                            tmp[:, n0:n0 + 512], ps[:, 0:512],
                            denb[:, n0:n0 + 512]))
                        if m == 0:
                            vd = p.op("v", lambda v, n0=n0: v.tensor_copy(
                                macc[:, n0:n0 + 512], tmp[:, n0:n0 + 512]))
                        else:
                            vd = p.op("v", lambda v, n0=n0: v.tensor_add(
                                macc[:, n0:n0 + 512], macc[:, n0:n0 + 512],
                                tmp[:, n0:n0 + 512]))
                        pool.done(j, "v", vd)
                hd_consumed[h] = ([(k_, p.counts[k_]) for k_ in pool.keys]
                                  + [("v", p.counts["v"])])
                issue_head(h + 2)
                p.op("v", lambda v: v.tensor_sub(tmp[:, :], macc[:, :],
                                                 yloc[:, :]))
                vd = p.op("v", lambda v, h=h: v.scalar_tensor_tensor(
                    ybf[h][:, :], tmp[:, :], gate[:, h:h + 1], yloc[:, :],
                    ALU.mult, ALU.add))
            p.barrier()
            xr_ = xr
            def out_cb(m, n0, ps, tkey, tdone):
                xrb, xob = xr_[m % 2], xo[m % 2]
                if n0 == 0:
                    p.wait("sync", "v")
                    p.wait("sync", "sync")
                    p.dma("sync", xrb[:], x4_in[m * 128:(m + 1) * 128, :])
                p.wait("v", tkey, tdone)
                p.wait("v", "sync")
                vd = p.op("v", lambda v, n0=n0, ps=ps, xrb=xrb, xob=xob:
                          v.tensor_add(xob[:, n0:n0 + 512], ps[:, 0:512],
                                       xrb[:, n0:n0 + 512]))
                if n0 + 512 >= L:
                    p.wait("sync", "v", vd)
                    p.dma("sync", x5_out[m * 128:(m + 1) * 128, :], xob[:])
                return ("v", vd)
            emit_proj_stream(p, ws_o, ybf, out_cb, pool)
            p.barrier()
            p.wait("sync", "sync")
        return body



def stage_s10(p, io):
    HF = 8192
    CH = HF // 128
    x5_in = io["x5"]
    w1_in = io["w1_t"]
    w3_in = io["w3_t"]
    w2_in = io["w2_t"]
    h64_in = io["h64c"]
    h128_in = io["h128c"]
    x6_out = io["x6"]
    hid_a = p.nc.dram_tensor(p.pref + "hid_a", [HF, L], BF16)
    hid_b = p.nc.dram_tensor(p.pref + "hid_b", [HF, L], BF16)

    if True:
        cm = ln_common(p, row_w=512)
        cm["row_ps2"] = p.ps("ln_row_ps2", [1, 512])
        rot2 = [p.sb(f"rot{i}", [128, L]) for i in range(2)]
        xn = p.fm("xn", dtype=BF16)
        h64b = p.sb("h64b", [128, 64], BF16)
        h128b = p.sb("h128b", [128, 128], BF16)
        hbuf = p.sb("hbuf", [128, L])
        hbf = p.sb("hbf", [128, L], BF16)
        scr = p.sb("scr", [128, L])
        xh = [p.sb(f"xh{i}", [128, L], BF16) for i in range(2)]
        xhk = [p.new_key("xh0"), p.new_key("xh1")]
        stk = [p.new_key("st0"), p.new_key("st1")]
        xgk = [p.new_key("xg0"), p.new_key("xg1")]
        w2b = [p.sb(f"w2b{i}", [128, 512], BF16) for i in range(2)]
        w2k = [p.new_key("w2b0"), p.new_key("w2b1")]
        xhu = [p.new_key("xhu0"), p.new_key("xhu1")]
        xr = [p.sb(f"xr{i}", [128, L]) for i in range(2)]
        xo = [p.sb(f"xo{i}", [128, L]) for i in range(2)]
        pool = PsumPool(p, "mmp", 4)
        ws1 = WStream(p, "w1s", w1_in, D, HF, mg=512)
        ws3 = WStream(p, "w3s", w3_in, D, HF, mg=512)

        def body():
            p.dma("sync", h64b[:], h64_in[:, :], sem_key="cst")
            p.dma("sync", h128b[:], h128_in[:, :], sem_key="cst")
            for e in ("v", "s", "t", "gp"):
                p.wait_all(e, ("cst",))
            p.op("v", lambda v: v.memset(cm["ones2"][:], 1.0))
            vd = p.op("v", lambda v: v.memset(cm["ones_row"][:], 1.0))
            p.wait("t", "v", vd)
            emit_ln_streamed(p, x5_in, xn, cm, rot2)

            ws1.prefetch(); ws1.prefetch()
            ws3.prefetch(); ws3.prefetch()
            for g in range(ws1.ng):
                buf1 = ws1.group(g)
                buf3 = ws3.group(g)
                for mo in range(0, 512, 128):
                    ch = g * 4 + mo // 128
                    for n0 in range(0, L, 512):
                        j1, psA, tk1 = pool.next()
                        for k in range(CD):
                            fn = (lambda t, k=k, mo=mo, n0=n0, psA=psA, buf1=buf1:
                                  t.matmul(psA[:, 0:512], ws1.chunk(buf1, k, mo),
                                           xn[k][:, n0:n0 + 512],
                                           start=(k == 0), stop=(k == CD - 1)))
                            if k == CD - 1:
                                td1 = p.opk("t", tk1, fn)
                            else:
                                p.op("t", fn)
                        j3, psB, tk3 = pool.next()
                        for k in range(CD):
                            fn = (lambda t, k=k, mo=mo, n0=n0, psB=psB, buf3=buf3:
                                  t.matmul(psB[:, 0:512], ws3.chunk(buf3, k, mo),
                                           xn[k][:, n0:n0 + 512],
                                           start=(k == 0), stop=(k == CD - 1)))
                            if k == CD - 1:
                                td3 = p.opk("t", tk3, fn)
                            else:
                                p.op("t", fn)
                        p.wait("s", tk1, td1)
                        sd = p.op("s", lambda s, n0=n0, psA=psA: s.activation(
                            out=scr[:, n0:n0 + 512], in_=psA[:, 0:512],
                            func=AF.Silu))
                        p.wait("v", "s", sd)
                        p.wait("v", tk3, td3)
                        vd = p.op("v", lambda v, n0=n0, psB=psB: v.tensor_mul(
                            hbuf[:, n0:n0 + 512], psB[:, 0:512],
                            scr[:, n0:n0 + 512]))
                        pool.done(j1, "s", sd)
                        pool.done(j3, "v", vd)
                    vd = p.op("v", lambda v: v.tensor_copy(hbf[:], hbuf[:]))
                    p.wait("t", "v", vd)
                    for n0 in range(0, L, 512):
                        j, ps, tkey = pool.next()
                        p.op("t", lambda t, n0=n0, ps=ps: t.matmul(
                            ps[0:64, 0:512], h64b[0:64, :],
                            hbf[0:64, n0:n0 + 512], start=True, stop=True))
                        td = p.opk("t", tkey, lambda t, n0=n0, ps=ps: t.matmul(
                            ps[64:128, 0:512], h64b[64:128, :],
                            hbf[64:128, n0:n0 + 512], start=True, stop=True))
                        p.wait("v", tkey, td)
                        vd = p.op("v", lambda v, n0=n0, ps=ps: v.tensor_scalar(
                            scr[:, n0:n0 + 512], ps[:, 0:512],
                            float(1.0 / math.sqrt(8192.0)), None, ALU.mult))
                        p.wait("gp", "v", vd)
                        pd = p.op("gp", lambda g_, n0=n0: g_.tensor_copy(
                            hbf[:, n0:n0 + 512], scr[:, n0:n0 + 512]))
                        pool.done(j, "gp", pd)
                    p.wait("sync", "gp")
                    p.dma("sync", hid_a[ch * 128:(ch + 1) * 128, :], hbf[:])
                    p.wait("t", "sync")
                    p.wait("v", "sync")
                    p.wait("s", "gp")
                ws1.prefetch()
                ws3.prefetch()
            p.barrier()
            p.wait("sync", "sync")

            for lo in range(64):
                par = lo % 2
                p.wait("sync", stk[par], p.counts[stk[par]])
                ld = p.dma("sync", xh[par][:],
                           hid_a.rearrange("(hi lo) t -> hi lo t", lo=64)[:, lo:lo + 1, :].rearrange("hi lo t -> hi (lo t)"),
                           sem_key=xhk[par])
                p.wait("t", xhk[par], ld)
                for n0 in range(0, L, 512):
                    j, ps, tkey = pool.next()
                    td = p.opk("t", tkey, lambda t, n0=n0, ps=ps, par=par:
                               t.matmul(ps[:, 0:512], h128b[:, :],
                                        xh[par][:, n0:n0 + 512], start=True,
                                        stop=True))
                    p.wait("v", tkey, td)
                    vd = p.op("v", lambda v, n0=n0, ps=ps, par=par: v.tensor_copy(
                        hbf[:, n0:n0 + 512] if False else xh[par][:, n0:n0 + 512],
                        ps[:, 0:512]))
                    pool.done(j, "v", vd)
                p.wait("sync", "v")
                p.dma("sync", hid_b.rearrange("(hi lo) t -> hi lo t", lo=64)[:, lo:lo + 1, :].rearrange("hi lo t -> hi (lo t)"),
                      xh[par][:], sem_key=stk[par])
            p.barrier()
            p.wait("sync", "sync")

            p.wait("gp", stk[0], p.counts[stk[0]])
            p.wait("gp", stk[1], p.counts[stk[1]])
            w2ld = {}
            xhld = {}

            def w2_issue(gi):
                if gi >= 8 * CH or gi in w2ld:
                    return
                mg, k = divmod(gi, CH)
                par = gi % 2
                p.wait("gp", xhu[par], p.counts[xhu[par]])
                w2ld[gi] = p.dma("gp", w2b[par][:, 0:256],
                                 w2_in[k * 128:(k + 1) * 128,
                                       mg * 256:(mg + 1) * 256],
                                 sem_key=w2k[par])
                xhld[gi] = p.dma("gp", xh[par][:],
                                 hid_b[k * 128:(k + 1) * 128, :],
                                 sem_key=xgk[par])

            for mg in range(8):
                for i2 in range(2):
                    m = mg * 2 + i2
                    p.wait("sync", "v")
                    p.wait("sync", "sync")
                    p.dma("sync", xr[i2][:], x5_in[m * 128:(m + 1) * 128, :])
                slots = [pool.next() for _ in range(4)]
                for k in range(CH):
                    gi = mg * CH + k
                    par = gi % 2
                    w2_issue(gi)
                    w2_issue(gi + 1)
                    p.wait("t", w2k[par], w2ld[gi])
                    p.wait("t", xgk[par], xhld[gi])
                    for i2 in range(2):
                        for h2 in range(2):
                            j, ps, tkey = slots[i2 * 2 + h2]
                            last = (i2 == 1 and h2 == 1)
                            fn = (lambda t, k=k, i2=i2, h2=h2, ps=ps, par=par:
                                  t.matmul(ps[:, 0:512],
                                           w2b[par][:, i2 * 128:(i2 + 1) * 128],
                                           xh[par][:, h2 * 512:(h2 + 1) * 512],
                                           start=(k == 0), stop=(k == CH - 1)))
                            if k == CH - 1:
                                p.opk("t", tkey, fn)
                            elif last:
                                p.opk("t", xhu[par], fn)
                            else:
                                p.op("t", fn)
                    if k < CH - 1:
                        w2_issue(gi + 2)
                p.wait("v", "sync")
                for i2 in range(2):
                    for h2 in range(2):
                        j, ps, tkey = slots[i2 * 2 + h2]
                        p.wait("v", tkey, p.counts[tkey])
                        vd = p.op("v", lambda v, i2=i2, h2=h2, ps=ps:
                                  v.tensor_add(xo[i2][:, h2 * 512:(h2 + 1) * 512],
                                               ps[:, 0:512],
                                               xr[i2][:, h2 * 512:(h2 + 1) * 512]))
                        pool.done(j, "v", vd)
                    p.wait("sync", "v")
                    m = mg * 2 + i2
                    p.dma("sync", x6_out[m * 128:(m + 1) * 128, :], xo[i2][:])
                p.wait("gp", "v")
                w2_issue((mg + 1) * CH + 1)
            p.barrier()
            p.wait("sync", "sync")
        return body



def stage_s11(p, io):
    x6_in = io["x6"]
    w_in = io["w_in"]
    ln1w_in = io["ln1w"]
    hs_out = io["hs"]
    z_out = io["zz"]
    xbc_out = io["xbc"]
    dtr_out = io["dtr"]

    if True:
        cm = ln_common(p, row_w=512)
        cm["row_ps2"] = p.ps("ln_row_ps2", [1, 512])
        rot2 = [p.sb(f"rot{i}", [128, L]) for i in range(2)]
        hn = p.fm("hn", dtype=BF16)
        hsout = [p.sb(f"hso{i}", [128, L]) for i in range(2)]
        hsk = [p.new_key("hs0"), p.new_key("hs1")]
        rmb = p.sb("rmb", [128, L])
        rmsrow = p.sb("rmsrow", [1, L])
        g1 = p.sb("g1", [128, CD])
        wdt = p.sb("wdt", [128, CD * 64], BF16)
        dtt = p.sb("dtt", [64, L])
        outp = [p.sb(f"outp_{i}", [128, L]) for i in range(4)]
        pool = PsumPool(p, "mmp", 4)
        ws = WStream(p, "win", w_in, D, 9216, mg=512)
        lnk = [p.new_key("lnk0"), p.new_key("lnk1")]
        p.new_key("cstg")

        def body():
            p.dma("sync", g1[:], ln1w_in[:, :], sem_key="cst")
            p.dma("gp", wdt.ap().rearrange("p (k m) -> p k m", m=64),
                  w_in[0:D, 9216:9280].rearrange("(k p) m -> p k m", p=128),
                  sem_key="cstg")
            for e in ("v", "s", "t", "gp"):
                p.wait_all(e, ("cst", "cstg"))
            p.op("v", lambda v: v.memset(cm["ones2"][:], 1.0))
            vd = p.op("v", lambda v: v.memset(cm["ones_row"][:], 1.0))
            p.wait("t", "v", vd)
            ones2, sq = cm["ones2"], cm["sq"]
            row_ps, row_ps2, bc_ps = cm["row_ps"], cm["row_ps2"], cm["bc_ps"]
            mb, rb, mrow, vrow = cm["mb"], cm["rb"], cm["mrow"], cm["vrow"]
            inv_n = 1.0 / D
            for n0 in range(0, L, 512):
                for i in range(CD):
                    xb = rot2[i % 2]
                    p.wait("sync", "s")
                    p.wait("sync", lnk[i % 2])
                    ld = p.dma("sync", xb[:, 0:512],
                               x6_in[i * 128:(i + 1) * 128, n0:n0 + 512])
                    p.wait("s", "sync", ld)
                    p.wait("s", "lnk0")
                    p.wait("s", "lnk1")
                    sd = p.op("s", lambda s, xb=xb: s.activation(
                        out=sq[:, 0:512], in_=xb[:, 0:512], func=AF.Square))
                    p.wait("t", "s", sd)
                    p.opk("t", lnk[i % 2], lambda t, i=i: t.matmul(
                        row_ps[0:1, 0:512], ones2[:, 0:1], sq[:, 0:512],
                        start=(i == 0), stop=(i == CD - 1)))
                    td = p.opk("t", lnk[i % 2], lambda t, i=i, xb=xb: t.matmul(
                        row_ps2[0:1, 0:512], ones2[:, 0:1], xb[:, 0:512],
                        start=(i == 0), stop=(i == CD - 1)))
                p.wait("s", "lnk0")
                p.wait("s", "lnk1")
                p.op("s", lambda s, n0=n0: s.activation(
                    out=vrow[0:1, n0:n0 + 512], in_=row_ps[0:1, 0:512],
                    func=AF.Copy, scale=inv_n))
                sd = p.op("s", lambda s, n0=n0: s.activation(
                    out=mrow[0:1, n0:n0 + 512], in_=row_ps2[0:1, 0:512],
                    func=AF.Copy, scale=inv_n))
            p.op("s", lambda s: s.activation(out=sq[0:1, :L], in_=mrow[0:1, :L],
                                             func=AF.Square))
            p.wait("v", "s")
            p.op("v", lambda v: v.tensor_sub(vrow[0:1, :L], vrow[0:1, :L],
                                             sq[0:1, :L]))
            p.op("v", lambda v: v.tensor_scalar(vrow[0:1, :L], vrow[0:1, :L],
                                                LN_EPS, None, ALU.add))
            p.op("v", lambda v: v.reciprocal(sq[0:1, :L], vrow[0:1, :L]))
            vd = p.op("v", lambda v: v.tensor_scalar(
                rmsrow[0:1, :L], sq[0:1, :L], -LN_EPS, 1.0 + LN_EPS,
                ALU.mult, op1=ALU.add))
            p.wait("s", "v", vd)
            p.op("s", lambda s: s.activation(out=rmsrow[0:1, :L],
                                             in_=rmsrow[0:1, :L], func=AF.Ln))
            p.op("s", lambda s: s.activation(out=rmsrow[0:1, :L],
                                             in_=rmsrow[0:1, :L], func=AF.Exp,
                                             scale=-0.5))
            p.op("s", lambda s: s.activation(out=vrow[0:1, :L],
                                             in_=vrow[0:1, :L], func=AF.Ln))
            sd = p.op("s", lambda s: s.activation(out=vrow[0:1, :L],
                                                  in_=vrow[0:1, :L],
                                                  func=AF.Exp, scale=-0.5))
            p.wait("t", "s", sd)
            for src_row, dst in ((vrow, rb), (mrow, mb), (rmsrow, rmb)):
                for n0 in range(0, L, 512):
                    td = p.opk("t", "t", lambda t, n0=n0, src_row=src_row:
                               t.matmul(bc_ps[:, 0:512], cm["ones_row"][0:1, :],
                                        src_row[0:1, n0:n0 + 512], start=True,
                                        stop=True))
                    p.wait("v", td)
                    vd = p.op("v", lambda v, n0=n0, dst=dst: v.tensor_copy(
                        dst[:, n0:n0 + 512], bc_ps[:, 0:512]))
                    p.wait("t", "v", vd)
            p.barrier()
            for i in range(CD):
                xb = rot2[i % 2]
                ho = hsout[i % 2]
                p.wait("sync", "v")
                ld = p.dma("sync", xb[:, :L], x6_in[i * 128:(i + 1) * 128, :])
                p.wait("v", "sync", ld)
                p.wait("v", hsk[i % 2], p.counts[hsk[i % 2]])
                p.op("v", lambda v, xb=xb: v.tensor_sub(xb[:, :L], xb[:, :L],
                                                        mb[:, :L]))
                vd = p.op("v", lambda v, xb=xb, ho=ho: v.tensor_mul(
                    ho[:, :L], xb[:, :L], rb[:, :L]))
                p.wait("sync", "v", vd)
                p.dma("sync", hs_out[i * 128:(i + 1) * 128, :], ho[:],
                      sem_key=hsk[i % 2])
                p.op("v", lambda v, ho=ho: v.tensor_mul(sq[:, :L], ho[:, :L],
                                                        rmb[:, :L]))
                vd = p.op("v", lambda v, i=i: v.tensor_scalar(
                    hn[i][:, :L], sq[:, :L], g1[:, i:i + 1], None, ALU.mult))
            p.barrier()

            def out_cb(m, n0, ps, tkey, tdone):
                ot = outp[m % 4]
                p.wait("v", "sync")
                p.wait("v", tkey, tdone)
                vd = p.opi("v", lambda v: v.tensor_copy(ot[:, n0:n0 + 512],
                                                        ps[:, 0:512]))
                if n0 + 512 >= L:
                    p.wait("sync", "v", vd)
                    if m < 32:
                        p.dma("sync", z_out[m * 128:(m + 1) * 128, :], ot[:])
                    else:
                        r = (m - 32) * 128
                        p.dma("sync", xbc_out[r:r + 128, :], ot[:])
                return ("v", vd)

            emit_proj_stream(p, ws, hn, out_cb, pool)
            for n0 in range(0, L, 512):
                j, ps, tkey = pool.next()
                for k in range(CD):
                    fn = (lambda t, k=k, n0=n0, ps=ps: t.matmul(
                        ps[0:64, 0:512], wdt[:, k * 64:(k + 1) * 64],
                        hn[k][:, n0:n0 + 512], start=(k == 0),
                        stop=(k == CD - 1)))
                    if k == CD - 1:
                        td = p.opk("t", tkey, fn)
                    else:
                        p.op("t", fn)
                p.wait("v", tkey, td)
                vd = p.op("v", lambda v, n0=n0, ps=ps: v.tensor_copy(
                    dtt[0:64, n0:n0 + 512], ps[0:64, 0:512]))
                pool.done(j, "v", vd)
            p.wait("sync", "v")
            p.dma("sync", dtr_out[0:64, :], dtt[0:64, :])
            p.barrier()
            p.wait("sync", "sync")
        return body



def stage_s12(p, io):
    NH, HD2, T, NCH = 64, 64, 128, 8
    xbc_in = io["xbc"]
    halo_in = io["halo"]
    convw_in = io["convw"]
    convb_in = io["convb"]
    convw64_in = io["convw64"]
    convb64_in = io["convb64"]
    dtr_in = io["dtr"]
    dtb_in = io["dtbias"]
    alog_in = io["alog"]
    drow_in = io["drow"]
    cin_in = io["carry_in"]
    mask_in = io["maskut"]
    id_in = io["identf"]
    y_out = io["y_ssm"]
    cout_out = io["carry_out"]
    c_dram = io.get("c_conv")
    if c_dram is None:
        c_dram = p.nc.dram_tensor(p.pref + "c_scr", [4096, L], F32)

    if True:
        xtt = [p.sb(f"xtt{g}", [64, 3 + L]) for g in range(8)]
        btt = [p.sb(f"btt{g}", [64, 3 + L]) for g in range(8)]
        xtm = [p.sb(f"xtm{g}", [128, NCH * 64]) for g in range(8)]
        btm = [p.sb(f"btm{g}", [128, NCH * 64]) for g in range(8)]
        carry = [p.sb(f"car{h}", [64, 64]) for h in range(NH)]
        dt_t = p.sb("dt_t", [64, L])
        da = p.sb("da", [64, L])
        acum = p.sb("acum", [64, L])
        scr64 = p.sb("scr64", [64, L])
        ones64 = p.sb("ones64", [64, T])
        zcol = p.sb("zcol", [64, 1])
        dtT = p.sb("dtT", [128, NCH * 64])
        acT = p.sb("acT", [128, NCH * 64])
        dsT = p.sb("dsT", [128, NCH * 64])
        maskut = p.sb("maskut_s", [128, 128])
        identf = p.sb("identf_s", [128, 128])
        onesr = p.sb("onesr", [1, 128])
        rowten = p.sb("rowten", [1, 64])
        dbc = p.sb("dbc", [64, 64])
        acol = p.sb("acol", [64, 1])
        dtbc = p.sb("dtbc", [64, 1])
        convw = p.sb("convw_s", [128, 128])
        convb = p.sb("convb_s", [128, 32])
        convw64 = p.sb("convw64_s", [64, 64])
        convb64 = p.sb("convb64_s", [64, 16])
        cvx = [p.sb(f"cvx{i}", [128, 3 + L]) for i in range(2)]
        cvs = p.sb("cvs", [128, L])
        cb = [p.sb(f"cb{i}", [64, L]) for i in range(2)]
        decsb = p.sb("decsb", [128, 128])
        dec2 = p.sb("dec2", [128, 128])
        scm = p.sb("scm_f", [128, 128])
        xdtm = p.sb("xdtm", [128, 64])
        bdtm = p.sb("bdtm", [128, 64])
        cdf = p.sb("cdf", [64, 128])
        odbh = p.sb("odbh", [64, 128])
        yh = [p.sb(f"yh{i}", [64, L]) for i in range(2)]
        ps_ab = p.ps("ps_ab", [128, 128])
        ps_sc = p.ps("ps_sc", [128, 128])
        ps_od = p.ps("ps_od", [64, 128])
        ps_y = p.ps("ps_y", [64, 128])
        ps_st = p.ps("ps_st", [64, 64])
        ps_tp = p.ps("ps_tp", [128, 128])
        ps_bc = p.ps("ps_bc", [128, 64])
        ps_tp2 = p.ps("ps_tp2", [1, 128])
        acrow = p.sb("acrow", [1, 128])
        for k in ("kab", "ksc", "kod", "ky", "kst", "ktp", "kbc", "ktp2"):
            p.new_key(k)
        cbk = [p.new_key("cb0"), p.new_key("cb1")]
        yhk = [p.new_key("yh0"), p.new_key("yh1")]
        cvk = [p.new_key("cv0"), p.new_key("cv1")]
        p.new_key("cstg")

        def conv_one(tile, ch, TL=L):
            P = tile.shape[0]
            cw, cbc = (convw64, convb64) if P == 64 else (convw, convb)
            p.op("v", lambda v: v.tensor_scalar(
                cvs[0:P, :TL], tile[:, 0:TL],
                cw[:, 4 * ch:4 * ch + 1], None, ALU.mult))
            for k in (1, 2, 3):
                p.op("v", lambda v, k=k: v.scalar_tensor_tensor(
                    cvs[0:P, :TL], tile[:, k:k + TL],
                    cw[:, 4 * ch + k:4 * ch + k + 1],
                    cvs[0:P, :TL], ALU.mult, ALU.add))
            vd = p.op("v", lambda v: v.tensor_scalar(
                cvs[0:P, :TL], cvs[0:P, :TL],
                cbc[:, ch:ch + 1], None, ALU.add))
            p.wait("s", "v", vd)
            sd = p.op("s", lambda s: s.activation(
                out=tile[:, 3:3 + TL], in_=cvs[0:tile.shape[0], :TL],
                func=AF.Silu))
            p.wait("v", "s", sd)
            return sd

        def body():
            p.dma("sync", convw[:], convw_in[:, :], sem_key="cst")
            p.dma("sync", convb[:], convb_in[:, :], sem_key="cst")
            p.dma("sync", convw64[:], convw64_in[:, :], sem_key="cst")
            p.dma("sync", convb64[:], convb64_in[:, :], sem_key="cst")
            p.dma("sync", dtbc[:], dtb_in[:, :], sem_key="cst")
            p.dma("sync", acol[:], alog_in[:, :], sem_key="cst")
            p.dma("sync", rowten[0:1, :], drow_in[:, :], sem_key="cst")
            p.dma("sync", maskut[:], mask_in[:, :], sem_key="cst")
            p.dma("sync", identf[:], id_in[:, :], sem_key="cst")
            for h in range(NH):
                p.dma("sync", carry[h][:], cin_in[h * 64:(h + 1) * 64, :],
                      sem_key="cst")
            p.dma("sync", dt_t[:], dtr_in[:, :], sem_key="cst")
            for e in ("v", "s", "t", "gp"):
                p.wait_all(e, ("cst",))
            p.op("v", lambda v: v.memset(ones64[:], 1.0))
            p.op("v", lambda v: v.memset(onesr[:], 1.0))
            vdm = p.op("v", lambda v: v.memset(zcol[:], 0.0))
            p.wait("t", "v", vdm)
            td = p.opk("t", "kbc", lambda t: t.matmul(
                ps_bc[0:64, 0:64], onesr[0:1, 0:64], rowten[0:1, 0:64],
                start=True, stop=True))
            p.wait("v", "kbc", td)
            p.op("v", lambda v: v.tensor_copy(dbc[:], ps_bc[0:64, 0:64]))
            sd = p.op("s", lambda s: s.activation(out=acol[:, 0:1],
                                                  in_=acol[:, 0:1], func=AF.Exp))
            p.wait("v", "s", sd)
            p.op("v", lambda v: v.tensor_scalar(acol[:, 0:1], acol[:, 0:1],
                                                -1.0, None, ALU.mult))
            p.op("v", lambda v: v.tensor_scalar(scr64[:, :L], dt_t[:, :L],
                                                dtbc[:, 0:1], None, ALU.add))
            emit_softplus(p, dt_t, scr64, da)
            vd = p.op("v", lambda v: v.tensor_scalar(da[:, :L], dt_t[:, :L],
                                                     acol[:, 0:1], None,
                                                     ALU.mult))
            for c in range(NCH):
                vd = p.op("v", lambda v, c=c: v.tensor_tensor_scan(
                    acum[:, c * T:(c + 1) * T], ones64[:, :T],
                    da[:, c * T:(c + 1) * T], zcol[:, 0:1],
                    ALU.mult, ALU.add))
            for g in range(8):
                r = g * 64
                p.wait("sync", "s")
                p.dma("sync", xtt[g][:, 0:3], halo_in[r:r + 64, :])
                ld = p.dma("sync", xtt[g][:, 3:3 + L], xbc_in[r:r + 64, :])
                p.wait("v", "sync", ld)
                conv_one(xtt[g], g)
            for g in range(8):
                r = 512 + g * 64
                p.wait("sync", "s")
                p.dma("sync", btt[g][:, 0:3], halo_in[r:r + 64, :])
                ld = p.dma("sync", btt[g][:, 3:3 + L], xbc_in[r:r + 64, :])
                p.wait("v", "sync", ld)
                sd = conv_one(btt[g], 8 + g)
            for ch in range(32):
                par = ch % 2
                r = 1024 + ch * 128
                p.wait("sync", cvk[par], p.counts[cvk[par]])
                p.wait("sync", "s")
                p.dma("sync", cvx[par][:, 0:3], halo_in[r:r + 128, :])
                ld = p.dma("sync", cvx[par][:, 3:3 + L], xbc_in[r:r + 128, :])
                p.wait("v", "sync", ld)
                sd = conv_one(cvx[par], ch)
                p.wait("gp", "s", sd)
                p.dma("gp", c_dram[ch * 128:(ch + 1) * 128, :],
                      cvx[par][:, 3:3 + L], sem_key=cvk[par])
            p.wait("t", "v")
            p.wait("t", "s")
            tcons = {"ktp": None, "kbc": ("v", p.counts["v"])}

            def tp_into(src_ap, dst_ap, w=None):
                po, fo = src_ap.shape[1], src_ap.shape[0]
                if tcons["ktp"] is not None:
                    p.wait("t", tcons["ktp"][0], tcons["ktp"][1])
                td = p.opk("t", "ktp", lambda t: t.transpose(
                    ps_tp[0:po, 0:fo], src_ap, identf[0:fo, 0:fo]))
                p.wait("v", "ktp", td)
                vd = p.op("v", lambda v: v.tensor_copy(dst_ap,
                                                       ps_tp[0:po, 0:fo]))
                tcons["ktp"] = ("v", vd)

            for c in range(NCH):
                sl = slice(c * T, (c + 1) * T)
                cc = slice(c * 64, (c + 1) * 64)
                tp_into(dt_t[:, sl], dtT[:, cc], 64)
                tp_into(acum[:, sl], acT[:, cc], 64)
                for g in range(8):
                    tp_into(xtt[g][:, 3 + c * T:3 + (c + 1) * T],
                            xtm[g][:, cc], 64)
                    tp_into(btt[g][:, 3 + c * T:3 + (c + 1) * T],
                            btm[g][:, cc], 64)
                p.wait("v", "kbc", p.counts["kbc"])
                tp_into(acum[:, c * T + T - 1:c * T + T], rowten[0:1, 0:64], 64)
                ck, cv = tcons["kbc"]
                p.wait("t", ck, cv)
                p.wait("t", tcons["ktp"][0], tcons["ktp"][1])
                td = p.opk("t", "kbc", lambda t: t.matmul(
                    ps_bc[:, 0:64], onesr[0:1, :], rowten[0:1, 0:64],
                    start=True, stop=True))
                p.wait("v", "kbc", td)
                vd = p.op("v", lambda v, cc=cc: v.tensor_sub(
                    dsT[:, cc], ps_bc[:, 0:64], acT[:, cc]))
                tcons["kbc"] = ("v", vd)
                p.wait("s", "v", vd)
                p.op("s", lambda s, cc=cc: s.activation(
                    out=dsT[:, cc], in_=dsT[:, cc], func=AF.Exp))
            p.barrier()

            cons = {k: None for k in ("kab", "ksc", "kod", "ky", "kst",
                                      "ktp2")}

            def pwait(key):
                if cons[key] is not None:
                    p.wait("t", cons[key][0], cons[key][1])

            p.wait("gp", cvk[0], p.counts[cvk[0]])
            p.wait("gp", cvk[1], p.counts[cvk[1]])
            ld0 = p.dma("gp", cb[0][:, :L], c_dram[0:64, :], sem_key=cbk[0])
            cbld = [ld0, None]
            for h in range(NH):
                par = h % 2
                g = h // 8
                if h + 1 < NH:
                    p.wait("gp", "t")
                    p.wait("gp", "gp")
                    cbld[(h + 1) % 2] = p.dma(
                        "gp", cb[(h + 1) % 2][:, :L],
                        c_dram[(h + 1) * 64:(h + 2) * 64, :],
                        sem_key=cbk[(h + 1) % 2])
                p.wait("t", cbk[par], cbld[par])
                p.wait("v", yhk[par], p.counts[yhk[par]])
                for c in range(NCH):
                    sl = slice(c * T, (c + 1) * T)
                    cc = c * 64
                    pwait("ktp2")
                    tdr = p.opk("t", "ktp2", lambda t, cc=cc, h=h: t.transpose(
                        ps_tp2[0:1, 0:T], acT[:, cc + h:cc + h + 1],
                        identf[:, :]))
                    p.wait("v", "ktp2", tdr)
                    vdr = p.op("v", lambda v: v.tensor_copy(acrow[0:1, 0:T],
                                                            ps_tp2[0:1, 0:T]))
                    cons["ktp2"] = ("v", vdr)
                    pwait("kab")
                    p.wait("t", "v", vdr)
                    tdab = p.opk("t", "kab", lambda t: t.matmul(
                        ps_ab[:, 0:T], onesr[0:1, 0:128],
                        acrow[0:1, 0:T], start=True, stop=True))
                    p.wait("v", "gp")
                    p.wait("v", "kab", tdab)
                    vd = p.op("v", lambda v, cc=cc, h=h: v.tensor_scalar(
                        decsb[:, 0:T], ps_ab[:, 0:T],
                        acT[:, cc + h:cc + h + 1], 0.0, ALU.subtract,
                        op1=ALU.min))
                    cons["kab"] = ("v", vd)
                    p.wait("s", "v", vd)
                    sd = p.op("s", lambda s: s.activation(
                        out=decsb[:, 0:T], in_=decsb[:, 0:T], func=AF.Exp))
                    p.wait("gp", "s", sd)
                    pd = p.op("gp", lambda g_: g_.tensor_mul(
                        dec2[:, 0:T], decsb[:, 0:T], maskut[:, 0:T]))
                    pwait("ksc")
                    p.wait("t", "s", sd)
                    tdsc = p.opk("t", "ksc", lambda t, g=g, sl=sl, par=par:
                                 t.matmul(ps_sc[:, 0:T],
                                          btt[g][0:64, 3 + sl.start:3 + sl.stop],
                                          cb[par][0:64, sl], start=True,
                                          stop=True))
                    p.wait("v", "ksc", tdsc)
                    p.wait("v", "gp", pd)
                    vd6 = p.op("v", lambda v: v.tensor_mul(
                        scm[:, 0:T], ps_sc[:, 0:T], dec2[:, 0:T]))
                    cons["ksc"] = ("v", vd6)
                    vd7 = p.op("v", lambda v, g=g, cc=cc, h=h: v.tensor_scalar(
                        xdtm[:, 0:64], xtm[g][:, cc:cc + 64],
                        dtT[:, cc + h:cc + h + 1], None, ALU.mult))
                    p.wait("gp", "v", vd7)
                    pd8 = p.op("gp", lambda g_, g=g, cc=cc, h=h:
                               g_.tensor_scalar(bdtm[:, 0:64],
                                                btm[g][:, cc:cc + 64],
                                                dsT[:, cc + h:cc + h + 1],
                                                None, ALU.mult))
                    pwait("kod")
                    tdod = p.opk("t", "kod", lambda t: t.matmul(
                        ps_od[0:64, 0:T], onesr[0:1, 0:64],
                        acrow[0:1, 0:T], start=True, stop=True))
                    p.wait("s", "kod", tdod)
                    p.wait("s", "gp")
                    sd10 = p.op("s", lambda s: s.activation(
                        out=odbh[:, 0:T], in_=ps_od[0:64, 0:T], func=AF.Exp))
                    cons["kod"] = ("s", sd10)
                    p.wait("gp", "s", sd10)
                    pd12 = p.op("gp", lambda g_, par=par, sl=sl: g_.tensor_mul(
                        cdf[:, 0:T], cb[par][:, sl], odbh[:, 0:T]))
                    pwait("ky")
                    p.wait("t", "v", vd7)
                    p.op("t", lambda t: t.matmul(
                        ps_y[0:64, 0:T], xdtm[:, 0:64], scm[:, 0:T],
                        start=True, stop=False))
                    p.wait("t", "gp", pd12)
                    p.wait("t", "v", p.counts["v"])
                    tdy = p.opk("t", "ky", lambda t, h=h: t.matmul(
                        ps_y[0:64, 0:T], carry[h][:, 0:64], cdf[:, 0:T],
                        start=False, stop=True))
                    pwait("kst")
                    p.wait("t", "gp", pd8)
                    tdst = p.opk("t", "kst", lambda t: t.matmul(
                        ps_st[0:64, 0:64], bdtm[:, 0:64], xdtm[:, 0:64],
                        start=True, stop=True))
                    p.wait("v", "ky", tdy)
                    vd16 = p.op("v", lambda v, g=g, h=h, par=par, c=c:
                                v.scalar_tensor_tensor(
                                    yh[par][:, c * T:(c + 1) * T],
                                    xtt[g][:, 3 + c * T:3 + (c + 1) * T],
                                    dbc[:, h:h + 1], ps_y[0:64, 0:T],
                                    ALU.mult, ALU.add))
                    cons["ky"] = ("v", vd16)
                    p.wait("v", "kst", tdst)
                    vd17 = p.op("v", lambda v, h=h: v.scalar_tensor_tensor(
                        carry[h][:, :], carry[h][:, :], odbh[:, 127:128],
                        ps_st[0:64, 0:64], ALU.mult, ALU.add))
                    cons["kst"] = ("v", vd17)
                p.wait("sync", "v")
                p.dma("sync", y_out[h * 64:(h + 1) * 64, :], yh[par][:, :L],
                      sem_key=yhk[par])
            for h in range(NH):
                p.dma("sync", cout_out[h * 64:(h + 1) * 64, :], carry[h][:, :])
            p.barrier()
            p.wait("sync", "sync")
        return body



def stage_s12c(p, io):
    NH = 64
    dtr_in = io["dtr"]
    dtb_in = io["dtbias"]
    alog_in = io["alog"]
    cin_in = io["carry_in"]
    cconv_in = io["c_conv"]
    yz_in = io["y_zero"]
    eh_in = io["eh64"]
    y_out = io["y_ssm"]

    if True:
        dt_t = p.sb("dt_t", [64, L])
        da = p.sb("da", [64, L])
        scr64 = p.sb("scr64", [64, L])
        eaG = p.sb("eaG", [64, L])
        ones64 = p.sb("ones64", [64, 512])
        zcol = p.sb("zcol", [64, 1])
        acol = p.sb("acol", [64, 1])
        dtbc = p.sb("dtbc", [64, 1])
        eh = p.sb("eh_sb", [64, 64 * 64])
        carry = [p.sb(f"car{h}", [64, 64]) for h in range(NH)]
        cbh = [p.sb(f"cbh{i}", [64, L]) for i in range(2)]
        ybh = [p.sb(f"ybh{i}", [64, L]) for i in range(2)]
        w2 = p.sb("w2_sb", [64, L])
        mmh = p.sb("mmh", [64, L])
        ps1 = p.ps("ps1", [64, 512])
        ps2 = p.ps("ps2", [64, 512])
        k1 = p.new_key("k1")
        k2 = p.new_key("k2")
        cbk = [p.new_key("cb0"), p.new_key("cb1")]
        ybk = [p.new_key("yb0"), p.new_key("yb1")]

        def body():
            p.dma("sync", dtbc[:], dtb_in[:, :], sem_key="cst")
            p.dma("sync", acol[:], alog_in[:, :], sem_key="cst")
            p.dma("sync", dt_t[:], dtr_in[:, :], sem_key="cst")
            p.dma("sync", eh[:], eh_in[:, :], sem_key="cst")
            for h in range(NH):
                p.dma("sync", carry[h][:], cin_in[h * 64:(h + 1) * 64, :],
                      sem_key="cst")
            for e in ("v", "s", "t", "gp"):
                p.wait_all(e, ("cst",))
            p.op("v", lambda v: v.memset(ones64[:], 1.0))
            p.op("v", lambda v: v.memset(zcol[:], 0.0))
            sd = p.op("s", lambda s: s.activation(out=acol[:, 0:1],
                                                  in_=acol[:, 0:1],
                                                  func=AF.Exp))
            p.wait("v", "s", sd)
            p.op("v", lambda v: v.tensor_scalar(acol[:, 0:1], acol[:, 0:1],
                                                -1.0, None, ALU.mult))
            p.op("v", lambda v: v.tensor_scalar(scr64[:, :L], dt_t[:, :L],
                                                dtbc[:, 0:1], None, ALU.add))
            emit_softplus(p, dt_t, scr64, da)
            p.op("v", lambda v: v.tensor_scalar(da[:, :L], dt_t[:, :L],
                                                acol[:, 0:1], None, ALU.mult))
            p.op("v", lambda v: v.tensor_tensor_scan(
                eaG[:, 0:512], ones64[:, 0:512], da[:, 0:512], zcol[:, 0:1],
                ALU.mult, ALU.add))
            vd = p.op("v", lambda v: v.tensor_tensor_scan(
                eaG[:, 512:L], ones64[:, 0:512], da[:, 512:L],
                eaG[:, 511:512], ALU.mult, ALU.add))
            p.wait("s", "v", vd)
            sd = p.op("s", lambda s: s.activation(out=eaG[:, :L],
                                                  in_=eaG[:, :L],
                                                  func=AF.Exp))
            p.wait("t", "s", sd)
            p.wait("v", "s", sd)

            ld_c = [None, None]
            ld_y = [None, None]

            def issue(h):
                if h >= NH:
                    return
                par = h % 2
                ld_c[par] = p.dma("gp", cbh[par][:, :],
                                  cconv_in[h * 64:(h + 1) * 64, :],
                                  sem_key=cbk[par])
                ld_y[par] = p.dma("gp", ybh[par][:, :],
                                  yz_in[h * 64:(h + 1) * 64, :],
                                  sem_key=ybk[par])

            issue(0)
            issue(1)
            vprev = {}
            for h in range(NH):
                par = h % 2
                p.wait("t", cbk[par], ld_c[par])
                p.wait("v", ybk[par], ld_y[par])
                for n0 in (0, 512):
                    tdw = p.opk("t", k1, lambda t, h=h, n0=n0: t.matmul(
                        ps1[:, 0:512], eh[:, h * 64:(h + 1) * 64],
                        eaG[:, n0:n0 + 512], start=True, stop=True))
                    p.wait("v", k1, tdw)
                    vdw = p.op("v", lambda v, n0=n0: v.tensor_copy(
                        w2[:, n0:n0 + 512], ps1[:, 0:512]))
                    p.wait("t", "v", vdw)
                    tdm = p.opk("t", k2, lambda t, h=h, n0=n0, par=par:
                                t.matmul(
                        ps2[:, 0:512], carry[h][:, 0:64],
                        cbh[par][:, n0:n0 + 512], start=True, stop=True))
                    p.wait("v", k2, tdm)
                    p.op("v", lambda v, n0=n0: v.tensor_mul(
                        mmh[:, n0:n0 + 512], ps2[:, 0:512],
                        w2[:, n0:n0 + 512]))
                    vda = p.op("v", lambda v, n0=n0, par=par: v.tensor_add(
                        ybh[par][:, n0:n0 + 512], ybh[par][:, n0:n0 + 512],
                        mmh[:, n0:n0 + 512]))
                p.wait("sync", "v", vda)
                p.dma("sync", y_out[h * 64:(h + 1) * 64, :], ybh[par][:, :],
                      sem_key=ybk[par])
                if h + 2 < NH:
                    p.wait("gp", "v", vda)
                    p.wait("gp", ybk[par], p.counts[ybk[par]])
                    p.wait("gp", k2, p.counts[k2])
                issue(h + 2)
            p.barrier()
            p.wait("sync", "sync")
        return body


def stage_s13(p, io):
    INTER = 5632
    NKU = 32
    y_in = io["y_ssm"]
    z_in = io["zz"]
    hs_in = io["hs"]
    x6_in = io["x6"]
    normw_in = io["normw"]
    ln2w_in = io["ln2w"]
    wout_in = io["w_out"]
    gw_in = io["gate_w"]
    uw_in = io["up_w"]
    dw_in = io["down_w"]
    x7_out = io["x7"]
    h_scr = p.nc.dram_tensor(p.pref + "h_scr", [D, L], F32)
    m_scr = p.nc.dram_tensor(p.pref + "m_scr", [INTER, L], BF16)

    if True:
        tiles = [p.sb(f"tl{i}", [128, L], BF16) for i in range(NKU)]
        big0 = p.sb("big0", [128, 16384], BF16)
        big1 = p.sb("big1", [128, 16384], BF16)
        yb = [p.sb(f"yb{i}", [128, L]) for i in range(2)]
        zb = [p.sb(f"zb{i}", [128, L]) for i in range(2)]
        scr = p.sb("scr", [128, L])
        ub = p.sb("ub", [128, L])
        rmsb = p.sb("rmsb", [128, L])
        rrow = p.sb("rrow", [1, L])
        onesc = p.sb("onesc", [128, 1])
        onesr = p.sb("onesr", [1, 128])
        gwn = p.sb("gwn", [128, 32])
        gw2 = p.sb("gw2", [128, CD])
        outp = [p.sb(f"outp{i}", [128, L]) for i in range(2)]
        xr = [p.sb(f"xr{i}", [128, L]) for i in range(2)]
        hr = [p.sb(f"hr{i}", [128, L]) for i in range(2)]
        mtile = p.sb("mtile", [128, L], BF16)
        mxh = [p.sb(f"mxh{i}", [128, L], BF16) for i in range(2)]
        row_ps = p.ps("row_ps", [1, 512])
        row_ps2 = p.ps("row_ps2", [1, 512])
        bc_ps = p.ps("bc_ps", [128, 512])
        pool = PsumPool(p, "mmp", 4)
        ws_out = WStream(p, "wso", wout_in, 4096, D, mg=512,
                         bufs=[big0.ap()[:, 0:16384], big1.ap()[:, 0:16384]])
        ws_g = WStream(p, "wsg", gw_in, D, INTER, mg=512,
                       bufs=[big0.ap()[:, 0:8192], big0.ap()[:, 8192:16384]])
        ws_u = WStream(p, "wsu", uw_in, D, INTER, mg=512,
                       bufs=[big1.ap()[:, 0:8192], big1.ap()[:, 8192:16384]])
        ws_d = WStream(p, "wsd", dw_in, INTER, D, mg=256,
                       bufs=[big0.ap()[:, 0:11264], big1.ap()[:, 0:11264]])
        lnk = [p.new_key("lnk0"), p.new_key("lnk1")]
        mk = [p.new_key("mx0"), p.new_key("mx1")]
        mu = [p.new_key("mu0"), p.new_key("mu1")]
        p.new_key("msk")

        def u_chunk(i, out_ap, extra=None):
            par = i % 2
            p.wait("sync", "v")
            p.wait("sync", "s")
            p.dma("sync", yb[par][:, :L], y_in[i * 128:(i + 1) * 128, :])
            ld = p.dma("sync", zb[par][:, :L], z_in[i * 128:(i + 1) * 128, :])
            p.wait("s", "sync", ld)
            p.wait("s", "v")
            sd = p.op("s", lambda s, par=par: s.activation(
                out=scr[:, :L], in_=zb[par][:, :L], func=AF.Silu))
            p.wait("v", "s", sd)
            vd = p.op("v", lambda v, par=par: v.tensor_mul(
                out_ap, yb[par][:, :L], scr[:, :L]))
            return vd

        def rms_rows(i, src_ap, last):
            p.wait("s", "lnk0")
            p.wait("s", "lnk1")
            sd = p.op("s", lambda s: s.activation(out=ub[:, :L], in_=src_ap,
                                                  func=AF.Square))
            p.wait("t", "s", sd)
            for n0 in (0, 512):
                ps = row_ps if n0 == 0 else row_ps2
                fn = (lambda t, ps=ps, n0=n0: t.matmul(
                    ps[0:1, 0:512], onesc[:, 0:1], ub[:, n0:n0 + 512],
                    start=(i == 0), stop=last))
                td = p.opk("t", lnk[n0 // 512], fn)
            return td

        def rms_factor(nfeat):
            p.wait("s", "lnk0")
            p.wait("s", "lnk1")
            for n0 in (0, 512):
                ps = row_ps if n0 == 0 else row_ps2
                p.op("s", lambda s, ps=ps, n0=n0: s.activation(
                    out=rrow[0:1, n0:n0 + 512], in_=ps[0:1, 0:512],
                    func=AF.Copy, scale=1.0 / nfeat))
            p.wait("v", "s")
            p.op("v", lambda v: v.tensor_scalar(rrow[0:1, :L], rrow[0:1, :L],
                                                LN_EPS, None, ALU.add))
            p.wait("s", "v")
            p.op("s", lambda s: s.activation(out=rrow[0:1, :L],
                                             in_=rrow[0:1, :L], func=AF.Ln))
            sd = p.op("s", lambda s: s.activation(out=rrow[0:1, :L],
                                                  in_=rrow[0:1, :L],
                                                  func=AF.Exp, scale=-0.5))
            p.wait("t", "s", sd)
            for n0 in (0, 512):
                td = p.opk("t", "t", lambda t, n0=n0: t.matmul(
                    bc_ps[:, 0:512], onesr[0:1, :], rrow[0:1, n0:n0 + 512],
                    start=True, stop=True))
                p.wait("v", td)
                vd = p.op("v", lambda v, n0=n0: v.tensor_copy(
                    rmsb[:, n0:n0 + 512], bc_ps[:, 0:512]))
                p.wait("t", "v", vd)

        def body():
            p.dma("sync", gwn[:], normw_in[:, :], sem_key="cst")
            p.dma("sync", gw2[:], ln2w_in[:, :], sem_key="cst")
            for e in ("v", "s", "t", "gp"):
                p.wait_all(e, ("cst",))
            p.op("v", lambda v: v.memset(onesc[:], 1.0))
            vd = p.op("v", lambda v: v.memset(onesr[:], 1.0))
            p.wait("t", "v", vd)
            p.wait("s", "v", vd)
            for i in range(NKU):
                vd = u_chunk(i, ub[:, :L] if False else scr[:, :L])
                p.wait("s", "v", vd)
                rms_rows(i, scr[:, :L], i == NKU - 1)
                p.wait("v", "lnk0")
                p.wait("v", "lnk1")
            rms_factor(4096.0)
            p.barrier()
            for i in range(NKU):
                vd = u_chunk(i, scr[:, :L])
                p.wait("gp", "v", vd)
                pd = p.op("gp", lambda g_: g_.tensor_mul(ub[:, :L], scr[:, :L],
                                                         rmsb[:, :L]))
                p.wait("v", "gp", pd)
                p.wait("v", "t")
                vd = p.op("v", lambda v, i=i: v.tensor_scalar(
                    tiles[i][:, :L], ub[:, :L], gwn[:, i:i + 1], None,
                    ALU.mult))
            p.barrier()
            def hcb(m, n0, ps, tkey, tdone):
                par = m % 2
                if n0 == 0:
                    p.wait("sync", "v")
                    p.wait("sync", "sync")
                    p.dma("sync", hr[par][:, :L],
                          hs_in[m * 128:(m + 1) * 128, :])
                p.wait("v", tkey, tdone)
                p.wait("v", "sync")
                vd = p.opi("v", lambda v: v.tensor_add(
                    outp[par][:, n0:n0 + 512], ps[:, 0:512],
                    hr[par][:, n0:n0 + 512]))
                if n0 + 512 >= L:
                    p.wait("sync", "v", vd)
                    p.dma("sync", h_scr[m * 128:(m + 1) * 128, :],
                          outp[par][:])
                return ("v", vd)

            emit_proj_stream(p, ws_out, tiles, hcb, pool)
            p.barrier()
            p.wait("sync", "sync")
            for i in range(CD):
                par = i % 2
                p.wait("sync", "s")
                ld = p.dma("sync", yb[par][:, :L],
                           h_scr[i * 128:(i + 1) * 128, :])
                p.wait("s", "sync", ld)
                rms_rows(i, yb[par][:, :L], i == CD - 1)
                p.wait("sync", "lnk0")
                p.wait("sync", "lnk1")
            rms_factor(2048.0)
            p.barrier()
            for i in range(CD):
                par = i % 2
                p.wait("sync", "gp")
                ld = p.dma("sync", yb[par][:, :L],
                           h_scr[i * 128:(i + 1) * 128, :])
                p.wait("gp", "sync", ld)
                p.wait("gp", "t")
                pd = p.op("gp", lambda g_, par=par: g_.tensor_mul(
                    ub[:, :L], yb[par][:, :L], rmsb[:, :L]))
                p.wait("v", "gp", pd)
                vd = p.op("v", lambda v, i=i: v.tensor_scalar(
                    tiles[i][:, :L], ub[:, :L], gw2[:, i:i + 1], None,
                    ALU.mult))
                p.wait("gp", "v", vd)
            p.barrier()
            p.wait("gp", "t")
            for k_ in pool.keys + ["tA", "tB"]:
                p.wait("gp", k_, p.counts[k_])
            ws_g.prefetch(); ws_g.prefetch()
            ws_u.prefetch(); ws_u.prefetch()
            h2t = tiles[:CD]
            for g in range(ws_g.ng):
                bufg = ws_g.group(g)
                bufu = ws_u.group(g)
                for mo in range(0, 512, 128):
                    ch = g * 4 + mo // 128
                    for n0 in (0, 512):
                        j1, psA, tk1 = pool.next()
                        for k in range(CD):
                            fn = (lambda t, k=k, mo=mo, n0=n0, psA=psA,
                                  bufg=bufg: t.matmul(
                                      psA[:, 0:512], ws_g.chunk(bufg, k, mo),
                                      h2t[k][:, n0:n0 + 512], start=(k == 0),
                                      stop=(k == CD - 1)))
                            if k == CD - 1:
                                td1 = p.opk("t", tk1, fn)
                            else:
                                p.op("t", fn)
                        j3, psB, tk3 = pool.next()
                        for k in range(CD):
                            fn = (lambda t, k=k, mo=mo, n0=n0, psB=psB,
                                  bufu=bufu: t.matmul(
                                      psB[:, 0:512], ws_u.chunk(bufu, k, mo),
                                      h2t[k][:, n0:n0 + 512], start=(k == 0),
                                      stop=(k == CD - 1)))
                            if k == CD - 1:
                                td3 = p.opk("t", tk3, fn)
                            else:
                                p.op("t", fn)
                        p.wait("s", tk1, td1)
                        p.wait("s", "v")
                        sd = p.op("s", lambda s, n0=n0, psA=psA: s.activation(
                            out=scr[:, n0:n0 + 512], in_=psA[:, 0:512],
                            func=AF.Silu))
                        p.wait("v", "s", sd)
                        p.wait("v", tk3, td3)
                        p.wait("v", "msk", p.counts["msk"])
                        vd = p.op("v", lambda v, n0=n0, psB=psB: v.tensor_mul(
                            mtile[:, n0:n0 + 512], psB[:, 0:512],
                            scr[:, n0:n0 + 512]))
                        pool.done(j1, "s", sd)
                        pool.done(j3, "v", vd)
                    p.wait("sync", "v", vd)
                    p.dma("sync", m_scr[ch * 128:(ch + 1) * 128, :], mtile[:],
                          sem_key="msk")
                ws_g.last_td[g % 2] = [(k_, p.counts[k_]) for k_ in pool.keys]
                ws_u.last_td[g % 2] = [(k_, p.counts[k_]) for k_ in pool.keys]
                ws_g.prefetch()
                ws_u.prefetch()
            p.barrier()
            p.wait("sync", "sync")
            p.wait("gp", "t")
            for k_ in pool.keys + ["tA", "tB"]:
                p.wait("gp", k_, p.counts[k_])
            p.wait("gp", "msk", p.counts["msk"])
            NKD = 44
            dld = {}
            xld = {}

            def d_issue(gi):
                if gi >= 8 * NKD or gi in dld:
                    return
                mg, k = divmod(gi, NKD)
                par = gi % 2
                p.wait("gp", mu[par], p.counts[mu[par]])
                dld[gi] = p.dma("gp", ws_d.buf[par][:, 0:256].rearrange(
                    "p m -> p m") if False else ws_d.buf[par][:, 0:256],
                    dw_in[k * 128:(k + 1) * 128, mg * 256:(mg + 1) * 256],
                    sem_key=ws_d.ldkey[par])
                xld[gi] = p.dma("gp", mxh[par][:],
                                m_scr[k * 128:(k + 1) * 128, :],
                                sem_key=mk[par])

            for mg in range(8):
                for i2 in range(2):
                    m = mg * 2 + i2
                    p.wait("sync", "v")
                    p.wait("sync", "sync")
                    p.dma("sync", xr[i2][:], x6_in[m * 128:(m + 1) * 128, :])
                    p.dma("sync", hr[i2][:], h_scr[m * 128:(m + 1) * 128, :])
                slots = [pool.next() for _ in range(4)]
                for k in range(NKD):
                    gi = mg * NKD + k
                    par = gi % 2
                    d_issue(gi)
                    d_issue(gi + 1)
                    p.wait("t", ws_d.ldkey[par], dld[gi])
                    p.wait("t", mk[par], xld[gi])
                    for i2 in range(2):
                        for h2 in range(2):
                            j, ps, tkey = slots[i2 * 2 + h2]
                            last = (i2 == 1 and h2 == 1)
                            fn = (lambda t, k=k, i2=i2, h2=h2, ps=ps, par=par:
                                  t.matmul(ps[:, 0:512],
                                           ws_d.buf[par][:, i2 * 128:
                                                         (i2 + 1) * 128],
                                           mxh[par][:, h2 * 512:(h2 + 1) * 512],
                                           start=(k == 0), stop=(k == NKD - 1)))
                            if k == NKD - 1:
                                p.opk("t", tkey, fn)
                            elif last:
                                p.opk("t", mu[par], fn)
                            else:
                                p.op("t", fn)
                    if k < NKD - 1:
                        d_issue(gi + 2)
                p.wait("v", "sync")
                for i2 in range(2):
                    for h2 in range(2):
                        j, ps, tkey = slots[i2 * 2 + h2]
                        p.wait("v", tkey, p.counts[tkey])
                        p.op("v", lambda v, i2=i2, h2=h2, ps=ps: v.tensor_add(
                            outp[i2][:, h2 * 512:(h2 + 1) * 512], ps[:, 0:512],
                            xr[i2][:, h2 * 512:(h2 + 1) * 512]))
                        vd = p.op("v", lambda v, i2=i2, h2=h2: v.tensor_add(
                            outp[i2][:, h2 * 512:(h2 + 1) * 512],
                            outp[i2][:, h2 * 512:(h2 + 1) * 512],
                            hr[i2][:, h2 * 512:(h2 + 1) * 512]))
                        pool.done(j, "v", vd)
                    p.wait("sync", "v")
                    m = mg * 2 + i2
                    p.dma("sync", x7_out[m * 128:(m + 1) * 128, :],
                          outp[i2][:])
                p.wait("gp", "v")
                d_issue((mg + 1) * NKD + 1)
            p.barrier()
            p.wait("sync", "sync")
        return body



def stage_s14(p, io):
    x7_in = io["x7"]
    wod_in = io["w_od"]
    bias_in = io["bias_c"]
    id_in = io["identf"]
    out_out = io["out"]
    xm_scr = p.nc.dram_tensor(p.pref + "xm_scr", [8, 128], F32)

    if True:
        cm = ln_common(p, row_w=512)
        cm["row_ps2"] = p.ps("ln_row_ps2", [1, 512])
        rot2 = [p.sb(f"rot{i}", [128, L]) for i in range(2)]
        wqt = [p.sb(f"wqt{k}", [128, D], BF16) for k in range(CD)]
        xq = [p.sb(f"xq{k}", [128, L], BF16) for k in range(CD)]
        wob = [p.sb(f"wob{i}", [128, D]) for i in range(2)]
        wobk = [p.new_key("wob0"), p.new_key("wob1")]
        u0 = p.sb("u0", [128, D])
        r0 = p.sb("r0", [128, D])
        cmask = p.sb("cmask", [128, D])
        wabs = p.sb("wabs", [128, D])
        accmx = p.sb("accmx", [128, L])
        xsb = p.sb("xsb", [128, L])
        rxsb = p.sb("rxsb", [128, L])
        wsc = p.sb("wsc", [128, CD])
        rws = p.sb("rws", [128, CD])
        xmc = p.sb("xmc", [128, 8])
        xm8 = p.sb("xm8", [8, 128])
        xmrow = p.sb("xmrow", [1, L])
        biasc = p.sb("biasc", [128, CD])
        identf = p.sb("identf_s", [128, 128])
        outp = [p.sb(f"outp{i}", [128, L]) for i in range(2)]
        xr = [p.sb(f"xr{i}", [128, L]) for i in range(2)]
        ps_tp = p.ps("ps_tp", [128, 128])
        pool = PsumPool(p, "mmp", 4)
        lnk = [p.new_key("lnk0"), p.new_key("lnk1")]
        p.new_key("ktp")
        p.new_key("xmk")

        def ladder(eng, uap, rap, cmap, w, nbit=7):
            d = p.op(eng, lambda e: e.tensor_copy(rap[:, 0:w], uap[:, 0:w]))
            f = 64.0
            for _ in range(nbit):
                d = p.op(eng, lambda e, f=f: e.tensor_scalar(
                    cmap[:, 0:w], rap[:, 0:w], f, None, ALU.is_ge))
                d = p.op(eng, lambda e, f=f: e.scalar_tensor_tensor(
                    rap[:, 0:w], cmap[:, 0:w], -f, rap[:, 0:w],
                    ALU.mult, ALU.add))
                f /= 2
            d = p.op(eng, lambda e: e.tensor_scalar(
                cmap[:, 0:w], rap[:, 0:w], 0.5, None, ALU.is_ge))
            d = p.op(eng, lambda e: e.tensor_sub(
                uap[:, 0:w], uap[:, 0:w], rap[:, 0:w]))
            d = p.op(eng, lambda e: e.tensor_add(
                uap[:, 0:w], uap[:, 0:w], cmap[:, 0:w]))
            return d

        def body():
            p.dma("sync", biasc[:], bias_in[:, :], sem_key="cst")
            p.dma("sync", identf[:], id_in[:, :], sem_key="cst")
            for e in ("v", "s", "t", "gp"):
                p.wait_all(e, ("cst",))
            p.op("v", lambda v: v.memset(cm["ones2"][:], 1.0))
            vd = p.op("v", lambda v: v.memset(cm["ones_row"][:], 1.0))
            p.wait("t", "v", vd)

            ld = [None, None]
            ld[0] = p.dma("gp", wob[0][:], wod_in[0:128, :], sem_key=wobk[0])
            ld[1] = p.dma("gp", wob[1][:], wod_in[128:256, :], sem_key=wobk[1])
            for j in range(CD):
                par = j % 2
                wb = wob[par]
                p.wait("s", wobk[par], ld[par])
                p.wait("s", "v")
                sd = p.op("s", lambda s, wb=wb: s.activation(
                    out=wabs[:, :D], in_=wb[:, :D], func=AF.Abs))
                p.wait("v", "s", sd)
                vd = p.op("v", lambda v: v.tensor_reduce(
                    u0[:, 0:1], wabs[:, :D], AX.X, ALU.max))
                vd = p.op("v", lambda v: v.tensor_scalar(
                    u0[:, 0:1], u0[:, 0:1], 1e-8, None, ALU.max))
                vd = p.op("v", lambda v, j=j: v.tensor_scalar(
                    wsc[:, j:j + 1], u0[:, 0:1], 1.0 / 127.0, None, ALU.mult))
                vd = p.op("v", lambda v, j=j: v.reciprocal(
                    r0[:, 0:1], u0[:, 0:1]))
                vd = p.op("v", lambda v, j=j: v.tensor_scalar(
                    rws[:, j:j + 1], r0[:, 0:1], 127.0, None, ALU.mult))
                vd = p.op("v", lambda v, j=j: v.tensor_scalar(
                    u0[:, :D], wabs[:, :D], rws[:, j:j + 1], None, ALU.mult))
                pd = ladder("v", u0, r0, cmask, D)
                pd = p.op("v", lambda v, wb=wb: v.tensor_scalar(
                    cmask[:, :D], wb[:, :D], 0.0, None, ALU.is_ge))
                pd = p.op("v", lambda v: v.tensor_scalar(
                    cmask[:, :D], cmask[:, :D], 2.0, -1.0, ALU.mult,
                    op1=ALU.add))
                pd = p.op("v", lambda v: v.tensor_mul(
                    r0[:, :D], u0[:, :D], cmask[:, :D]))
                p.wait("gp", "v", pd)
                if j + 2 < CD:
                    ld[par] = p.dma("gp", wob[par][:],
                                    wod_in[(j + 2) * 128:(j + 3) * 128, :],
                                    sem_key=wobk[par])
                p.wait("t", "v", pd)
                for k in range(CD):
                    tcn = p.counts["s"]
                    p.wait("t", "s")
                    td = p.opk("t", "ktp", lambda t, k=k: t.transpose(
                        ps_tp[:, 0:128], r0[:, k * 128:(k + 1) * 128],
                        identf[:, :]))
                    p.wait("s", "ktp", td)
                    sd = p.op("s", lambda s, k=k, j=j: s.activation(
                        out=wqt[k][:, j * 128:(j + 1) * 128],
                        in_=ps_tp[:, 0:128], func=AF.Copy))
            p.barrier()

            ones2, sq = cm["ones2"], cm["sq"]
            row_ps, row_ps2 = cm["row_ps"], cm["row_ps2"]
            bc_ps = cm["bc_ps"]
            mb, rb, mrow, vrow = cm["mb"], cm["rb"], cm["mrow"], cm["vrow"]
            inv_n = 1.0 / D
            for n0 in range(0, L, 512):
                for i in range(CD):
                    xb = rot2[i % 2]
                    p.wait("sync", "s")
                    p.wait("sync", lnk[i % 2])
                    ldx = p.dma("sync", xb[:, 0:512],
                                x7_in[i * 128:(i + 1) * 128, n0:n0 + 512])
                    p.wait("s", "sync", ldx)
                    p.wait("s", "lnk0")
                    p.wait("s", "lnk1")
                    sd = p.op("s", lambda s, xb=xb: s.activation(
                        out=sq[:, 0:512], in_=xb[:, 0:512], func=AF.Square))
                    p.wait("t", "s", sd)
                    p.opk("t", lnk[i % 2], lambda t, i=i: t.matmul(
                        row_ps[0:1, 0:512], ones2[:, 0:1], sq[:, 0:512],
                        start=(i == 0), stop=(i == CD - 1)))
                    td = p.opk("t", lnk[i % 2], lambda t, i=i, xb=xb: t.matmul(
                        row_ps2[0:1, 0:512], ones2[:, 0:1], xb[:, 0:512],
                        start=(i == 0), stop=(i == CD - 1)))
                p.wait("s", "lnk0")
                p.wait("s", "lnk1")
                p.op("s", lambda s, n0=n0: s.activation(
                    out=vrow[0:1, n0:n0 + 512], in_=row_ps[0:1, 0:512],
                    func=AF.Copy, scale=inv_n))
                sd = p.op("s", lambda s, n0=n0: s.activation(
                    out=mrow[0:1, n0:n0 + 512], in_=row_ps2[0:1, 0:512],
                    func=AF.Copy, scale=inv_n))
            p.op("s", lambda s: s.activation(out=sq[0:1, :L], in_=mrow[0:1, :L],
                                             func=AF.Square))
            p.wait("v", "s")
            p.op("v", lambda v: v.tensor_sub(vrow[0:1, :L], vrow[0:1, :L],
                                             sq[0:1, :L]))
            p.op("v", lambda v: v.tensor_scalar(vrow[0:1, :L], vrow[0:1, :L],
                                                LN_EPS, None, ALU.add))
            p.wait("s", "v")
            p.op("s", lambda s: s.activation(out=vrow[0:1, :L],
                                             in_=vrow[0:1, :L], func=AF.Ln))
            sd = p.op("s", lambda s: s.activation(out=vrow[0:1, :L],
                                                  in_=vrow[0:1, :L],
                                                  func=AF.Exp, scale=-0.5))
            p.wait("t", "s", sd)
            for src_row, dst in ((vrow, rb), (mrow, mb)):
                for n0 in range(0, L, 512):
                    td = p.opk("t", "t", lambda t, n0=n0, src_row=src_row:
                               t.matmul(bc_ps[:, 0:512], cm["ones_row"][0:1, :],
                                        src_row[0:1, n0:n0 + 512], start=True,
                                        stop=True))
                    p.wait("v", td)
                    vd = p.op("v", lambda v, n0=n0, dst=dst: v.tensor_copy(
                        dst[:, n0:n0 + 512], bc_ps[:, 0:512]))
                    p.wait("t", "v", vd)
            p.barrier()
            for i in range(CD):
                xb = rot2[i % 2]
                p.wait("sync", "s")
                p.wait("sync", "v")
                ldx = p.dma("sync", xb[:, :L], x7_in[i * 128:(i + 1) * 128, :])
                p.wait("v", "sync", ldx)
                p.op("v", lambda v, xb=xb: v.tensor_sub(xb[:, :L], xb[:, :L],
                                                        mb[:, :L]))
                vd = p.op("v", lambda v, xb=xb: v.tensor_mul(xb[:, :L],
                                                             xb[:, :L],
                                                             rb[:, :L]))
                p.wait("s", "v", vd)
                sd = p.op("s", lambda s, xb=xb: s.activation(
                    out=sq[:, :L], in_=xb[:, :L], func=AF.Abs))
                p.wait("v", "s", sd)
                if i == 0:
                    vd = p.op("v", lambda v: v.tensor_copy(accmx[:, :L],
                                                           sq[:, :L]))
                else:
                    vd = p.op("v", lambda v: v.tensor_tensor(
                        accmx[:, :L], accmx[:, :L], sq[:, :L], ALU.max))
            tcons = None
            for b in range(8):
                if tcons is not None:
                    p.wait("t", tcons[0], tcons[1])
                p.wait("t", "v")
                td = p.opk("t", "ktp", lambda t, b=b: t.transpose(
                    ps_tp[:, 0:128], accmx[:, b * 128:(b + 1) * 128],
                    identf[:, :]))
                p.wait("v", "ktp", td)
                vd = p.op("v", lambda v, b=b: v.tensor_reduce(
                    xmc[:, b:b + 1], ps_tp[:, 0:128], AX.X, ALU.max))
                tcons = ("v", vd)
            p.wait("t", tcons[0], tcons[1])
            td = p.opk("t", "ktp", lambda t: t.transpose(
                ps_tp[0:8, 0:128], xmc[:, 0:8], identf[:, :]))
            p.wait("v", "ktp", td)
            vd = p.op("v", lambda v: v.tensor_copy(xm8[0:8, :], ps_tp[0:8, 0:128]))
            p.wait("sync", "v", vd)
            p.dma("sync", xm_scr[:, :], xm8[0:8, :], sem_key="xmk")
            p.wait("sync", "xmk", p.counts["xmk"])
            ldm = p.dma("sync", xmrow[0:1, :], xm_scr.rearrange("a b -> (a b)")[None, :], sem_key="xmk")
            p.wait("v", "xmk", ldm)
            p.op("v", lambda v: v.tensor_scalar(xmrow[0:1, :L], xmrow[0:1, :L],
                                                1.0 / 127.0, None, ALU.mult))
            vd = p.op("v", lambda v: v.tensor_scalar(
                xmrow[0:1, :L], xmrow[0:1, :L], 1e-12, None, ALU.max))
            p.wait("t", "v", vd)
            for n0 in (0, 512):
                td = p.opk("t", "t", lambda t, n0=n0: t.matmul(
                    bc_ps[:, 0:512], cm["ones_row"][0:1, :],
                    xmrow[0:1, n0:n0 + 512], start=True, stop=True))
                p.wait("v", td)
                vd = p.op("v", lambda v, n0=n0: v.tensor_copy(
                    xsb[:, n0:n0 + 512], bc_ps[:, 0:512]))
                p.wait("t", "v", vd)
            vd = p.op("v", lambda v: v.reciprocal(rxsb[:, :L], xsb[:, :L]))
            for i in range(CD):
                xb = rot2[i % 2]
                p.wait("sync", "s")
                p.wait("sync", "v")
                ldx = p.dma("sync", xb[:, :L], x7_in[i * 128:(i + 1) * 128, :])
                p.wait("v", "sync", ldx)
                p.op("v", lambda v, xb=xb: v.tensor_sub(xb[:, :L], xb[:, :L],
                                                        mb[:, :L]))
                vd = p.op("v", lambda v, xb=xb: v.tensor_mul(xb[:, :L],
                                                             xb[:, :L],
                                                             rb[:, :L]))
                p.wait("s", "v", vd)
                sd = p.op("s", lambda s, xb=xb: s.activation(
                    out=sq[:, :L], in_=xb[:, :L], func=AF.Abs))
                p.wait("v", "s", sd)
                pd = p.op("v", lambda v: v.tensor_mul(u0[:, 0:L], sq[:, :L],
                                                      rxsb[:, :L]))
                pd = ladder("v", u0, r0, cmask, L)
                pd = p.op("v", lambda v, xb=xb: v.tensor_scalar(
                    cmask[:, 0:L], xb[:, :L], 0.0, None, ALU.is_ge))
                pd = p.op("v", lambda v: v.tensor_scalar(
                    cmask[:, 0:L], cmask[:, 0:L], 2.0, -1.0, ALU.mult,
                    op1=ALU.add))
                pd = p.op("v", lambda v, i=i: v.tensor_mul(
                    xq[i][:, :L], u0[:, 0:L], cmask[:, 0:L]))
            p.barrier()
            for m in range(CD):
                par = m % 2
                p.wait("sync", "v")
                p.wait("sync", "sync")
                p.dma("sync", xr[par][:], x7_in[m * 128:(m + 1) * 128, :])
                for n0 in (0, 512):
                    j, ps, tkey = pool.next()
                    for k in range(CD):
                        fn = (lambda t, k=k, m=m, n0=n0, ps=ps: t.matmul(
                            ps[:, 0:512], wqt[k][:, m * 128:(m + 1) * 128],
                            xq[k][:, n0:n0 + 512], start=(k == 0),
                            stop=(k == CD - 1)))
                        if k == CD - 1:
                            td = p.opk("t", tkey, fn)
                        else:
                            p.op("t", fn)
                    p.wait("v", tkey, td)
                    p.wait("v", "sync")
                    p.op("v", lambda v, n0=n0, ps=ps, par=par: v.tensor_mul(
                        outp[par][:, n0:n0 + 512], ps[:, 0:512],
                        xsb[:, n0:n0 + 512]))
                    p.op("v", lambda v, n0=n0, par=par, m=m: v.tensor_scalar(
                        outp[par][:, n0:n0 + 512], outp[par][:, n0:n0 + 512],
                        wsc[:, m:m + 1], None, ALU.mult))
                    p.op("v", lambda v, n0=n0, par=par, m=m: v.tensor_scalar(
                        outp[par][:, n0:n0 + 512], outp[par][:, n0:n0 + 512],
                        biasc[:, m:m + 1], None, ALU.add))
                    vd = p.op("v", lambda v, n0=n0, par=par: v.tensor_add(
                        outp[par][:, n0:n0 + 512], outp[par][:, n0:n0 + 512],
                        xr[par][:, n0:n0 + 512]))
                    pool.done(j, "v", vd)
                p.wait("sync", "v", vd)
                p.dma("sync", out_out[m * 128:(m + 1) * 128, :], outp[par][:])
            p.barrier()
            p.wait("sync", "sync")
        return body


I_, O_, N_ = "ExternalInput", "ExternalOutput", "Internal"


def _mk(tensors, stages):
    nc = bass.Bass("TRN2", target_bir_lowering=False, debug=False)
    t = {}
    for name, kind, shape, dt in tensors:
        t[name] = nc.dram_tensor(name, shape, dt, kind=kind).ap()
    es = contextlib.ExitStack()
    with es:
        p = Prog(nc, es)
        bodies = []
        for pref, fn, iomap in stages:
            with p.stage_scope(pref):
                raw = fn(p, {k: t[v] for k, v in iomap.items()})

            def wrapped(raw=raw, pref=pref):
                old_pref = p.pref
                p.pref = pref + "_"
                raw()
                p.pref = old_pref
            bodies.append(wrapped)

        def body():
            for i, b in enumerate(bodies):
                if i:
                    p.stage_barrier()
                b()
        p.build(body)
    return nc


def _id(*names):
    return {n: n for n in names}


_L0_W = [
    ("halo", I_, [D, 3], F32), ("conv_w", I_, [128, CD * 4], F32),
    ("conv_b", I_, [128, CD], F32), ("w_xproj_t", I_, [D, 160], F32),
    ("w_dt_t", I_, [DT_RANK, D], F32), ("b_dt", I_, [128, CD], F32),
    ("a_log", I_, [128, CD * D_STATE], F32),
    ("onehot32", I_, [32, 32 * 128], F32), ("h0", I_, [D, D_STATE], F32),
    ("dskip", I_, [128, CD], F32), ("w_out_t", I_, [D, D], F32),
    ("w1_t", I_, [D, D], F32), ("b1", I_, [128, CD], F32),
    ("w2_t", I_, [D, 8], F32), ("b2", I_, [128, 8], F32),
    ("v_t", I_, [D, 512], F32), ("u_t", I_, [512, D], F32),
    ("sel8", I_, [8, 4 * 128], F32), ("ident128", I_, [128, 128], F32),
    ("wq", I_, [D, D], F32), ("wk", I_, [D, D], F32),
    ("wv", I_, [D, D], F32), ("wg", I_, [D, H1], F32),
    ("bg", I_, [H1, 1], F32), ("wo", I_, [D, D], F32),
    ("bo", I_, [128, CD], F32), ("s0", I_, [H1 * DH1, DH1], F32),
    ("z0", I_, [H1 * DH1, 1], F32), ("trow", I_, [1, L], F32),
    ("iotaj", I_, [128, 1], F32), ("ssign", I_, [128, 1], F32),
    ("maskut", I_, [128, 128], F32), ("ones128", I_, [128, 1], F32),
]

_S2_IO = _id("u_pre", "halo", "conv_w", "conv_b", "w_xproj_t", "w_dt_t",
             "b_dt", "a_log", "onehot32", "h0", "u_silu", "dt", "bc",
             "y_loc", "h_end")
_S4_IO = dict(_id("x", "u_silu", "z", "dskip", "w_out_t", "x1"), y="y_loc")
_S5_IO = _id("x1", "w1_t", "b1", "w2_t", "b2", "v_t", "u_t", "sel8",
             "ident128", "x2")
_S6_IO = _id("x2", "wq", "wk", "wv", "wg", "bg", "wo", "bo", "s0", "z0",
             "trow", "iotaj", "ssign", "maskut", "ones128", "ident128",
             "x3", "s_end", "z_end")


def build_P1():
    tensors = [("x", I_, [D, L], F32), ("w_in_t", I_, [D, 2 * D], F32),
               ("u_pre", O_, [D, L], F32), ("z", O_, [D, L], F32),
               ("h_end", O_, [D, D_STATE], F32),
               ("s_end", O_, [H1 * DH1, DH1], F32),
               ("z_end", O_, [H1 * DH1, 1], F32),
               ("u_silu", O_, [D, L], F32), ("dt", O_, [D, L], F32),
               ("bc", O_, [32, L], F32), ("y_loc", N_, [D, L], F32),
               ("x1", N_, [D, L], F32), ("x2", N_, [D, L], F32),
               ("x3", N_, [D, L], F32)] + _L0_W
    return _mk(tensors, [
        ("t1", stage_s1, _id("x", "w_in_t", "u_pre", "z")),
        ("t2", stage_s2, _S2_IO),
        ("t4", stage_s4, _S4_IO),
        ("t5", stage_s5, _S5_IO),
        ("t6", stage_s6, _S6_IO),
    ])


def build_P2():
    tensors = [("x", I_, [D, L], F32), ("u_pre", I_, [D, L], F32),
               ("z", I_, [D, L], F32),
               ("u_silu", I_, [D, L], F32), ("dt", I_, [D, L], F32),
               ("bc", I_, [32, L], F32),
               ("x3", O_, [D, L], F32), ("xn_ln", O_, [D, L], F32),
               ("amax", O_, [1, 1], F32),
               ("s_end", N_, [H1 * DH1, DH1], F32),
               ("z_end", N_, [H1 * DH1, 1], F32),
               ("u_silu_fix", N_, [D, L], F32), ("y_fix", N_, [D, L], F32),
               ("x1", N_, [D, L], F32), ("x2", N_, [D, L], F32)] + _L0_W
    return _mk(tensors, [
        ("t2", stage_s2c, dict(_id("u_pre", "halo", "u_silu", "dt", "bc",
                                   "h0", "conv_w", "conv_b", "w_xproj_t",
                                   "w_dt_t", "b_dt", "a_log", "onehot32",
                                   "u_silu_fix", "y_fix"),
                               identf="ident128")),
        ("t4", stage_s4, dict(_id("x", "z", "dskip", "w_out_t", "x1"),
                              y="y_fix", u_silu="u_silu_fix")),
        ("t5", stage_s5, _S5_IO),
        ("t6", stage_s6, _S6_IO),
        ("t7", stage_s7a, dict(_id("x3", "xn_ln", "amax"),
                               identf="ident128")),
    ])


_S8_IO = _id("x4", "wq", "wk", "wv", "s0", "z0", "maskut", "ones128",
             "identf", "sigma_q", "num_loc", "den_loc", "s_end", "z_end",
             "qbar")


def build_P3():
    tensors = [("x3", I_, [D, L], F32), ("xn_ln", I_, [D, L], F32),
               ("amax8", I_, [1, 8], F32), ("w_fp8_t", I_, [D, D], F32),
               ("wscale", I_, [1, 1], F32), ("bias", I_, [128, CD], F32),
               ("wq", I_, [D, D], F32), ("wk", I_, [D, D], F32),
               ("wv", I_, [D, D], F32), ("s0", I_, [H2 * DH2, DH2], F32),
               ("z0", I_, [H2 * DH2, 1], F32), ("maskut", I_, [128, 128], F32),
               ("ones128", I_, [128, 1], F32), ("identf", I_, [128, 128], F32),
               ("x4", O_, [D, L], F32), ("s_end", O_, [H2 * DH2, DH2], F32),
               ("z_end", O_, [H2 * DH2, 1], F32), ("qbar", O_, [128, CD], F32),
               ("sigma_q", O_, [D, L], F32), ("num_loc", O_, [D, L], F32),
               ("den_loc", O_, [H2, L], F32)]
    return _mk(tensors, [
        ("t7", stage_s7b, _id("x3", "xn_ln", "amax8", "w_fp8_t", "wscale",
                              "bias", "x4")),
        ("t8", stage_s8, _S8_IO),
    ])


_S12_IO = _id("xbc", "halo", "convw", "convb", "convw64", "convb64", "dtr",
              "dtbias", "alog", "drow", "carry_in", "maskut", "identf",
              "y_ssm", "carry_out")
_L3_CONV = [("halo", I_, [5120, 3], F32), ("convw", I_, [128, 128], F32),
            ("convb", I_, [128, 32], F32), ("convw64", I_, [64, 64], F32),
            ("convb64", I_, [64, 16], F32), ("dtbias", I_, [64, 1], F32),
            ("alog", I_, [64, 1], F32), ("drow", I_, [1, 64], F32),
            ("carry_in", I_, [4096, 64], F32), ("maskut", I_, [128, 128], F32),
            ("identf", I_, [128, 128], F32)]


def build_P45():
    tensors = [("x4", I_, [D, L], F32),
               ("s0", I_, [H2 * DH2, DH2], F32),
               ("z0c", I_, [128, H2], F32),
               ("qbar_a", I_, [128, CD], F32), ("qbar_b", I_, [128, CD], F32),
               ("mem_t", I_, [4 * H2 * 128, 128], F32),
               ("norm2", I_, [128, 64], F32), ("gate_bc", I_, [128, H2], F32),
               ("diag_mh", I_, [64, H2], F32), ("wo_t", I_, [D, D], F32),
               ("w1_t", I_, [D, 8192], F32), ("w3_t", I_, [D, 8192], F32),
               ("w2_t", I_, [8192, D], F32), ("h64c", I_, [128, 64], BF16),
               ("h128c", I_, [128, 128], BF16),
               ("w_in", I_, [D, 9280], F32), ("ln1w", I_, [128, CD], F32),
               ("sigma_q", I_, [D, L], F32), ("num_loc", I_, [D, L], F32),
               ("den_loc", I_, [H2, L], F32),
               ("x6", O_, [D, L], F32),
               ("hs", O_, [D, L], F32), ("zz", O_, [4096, L], F32),
               ("xbc", N_, [5120, L], F32), ("dtr", O_, [64, L], F32),
               ("carry_out", O_, [4096, 64], F32),
               ("x5", N_, [D, L], F32),
               ("c_conv", O_, [4096, L], F32),
               ("y_ssm", O_, [4096, L], F32)] + _L3_CONV
    return _mk(tensors, [
        ("t9", stage_s9, dict(_id("x4", "sigma_q", "qbar_a", "qbar_b",
                                  "mem_t", "norm2", "gate_bc", "diag_mh",
                                  "identf", "wo_t", "x5"),
                              num="num_loc", den="den_loc",
                              s0c="s0", z0c="z0c")),
        ("t10", stage_s10, _id("x5", "w1_t", "w3_t", "w2_t", "h64c",
                               "h128c", "x6")),
        ("t11", stage_s11, _id("x6", "w_in", "ln1w", "hs", "zz", "xbc",
                               "dtr")),
        ("t12", stage_s12, dict(_S12_IO, c_conv="c_conv")),
    ])


def build_P6():
    tensors = [("dtr", I_, [64, L], F32),
               ("dtbias", I_, [64, 1], F32), ("alog", I_, [64, 1], F32),
               ("carry_in", I_, [4096, 64], F32),
               ("identf", I_, [128, 128], F32),
               ("c_conv", I_, [4096, L], F32),
               ("y_zero", I_, [4096, L], F32),
               ("eh64", I_, [64, 64 * 64], F32),
               ("zz", I_, [4096, L], F32), ("hs", I_, [D, L], F32),
               ("x6", I_, [D, L], F32), ("normw", I_, [128, 32], F32),
               ("ln2w", I_, [128, CD], F32), ("w_out", I_, [4096, D], F32),
               ("gate_w", I_, [D, 5632], F32), ("up_w", I_, [D, 5632], F32),
               ("down_w", I_, [5632, D], F32), ("w_od", I_, [D, D], F32),
               ("bias_c", I_, [128, CD], F32),
               ("out", O_, [D, L], F32),
               ("y_ssm", N_, [4096, L], F32),
               ("x7", N_, [D, L], F32)]
    return _mk(tensors, [
        ("t12c", stage_s12c, _id("dtr", "dtbias", "alog", "carry_in",
                                 "c_conv", "y_zero", "eh64", "y_ssm")),
        ("t13", stage_s13, _id("y_ssm", "zz", "hs", "x6", "normw", "ln2w",
                               "w_out", "gate_w", "up_w", "down_w", "x7")),
        ("t14", stage_s14, _id("x7", "w_od", "bias_c", "identf", "out")),
    ])


PROGS = {"P1": build_P1, "P2": build_P2, "P3": build_P3,
         "P45": build_P45, "P6": build_P6}
_NC_CACHE = {}


def _nc(name):
    if name not in _NC_CACHE:
        _NC_CACHE[name] = PROGS[name]()
    return _NC_CACHE[name]


def _launch(name, ims):
    from concourse.bass_utils import run_bass_kernel_spmd
    res = run_bass_kernel_spmd(_nc(name), ims, core_ids=list(range(8)))
    return res.results


def _hadamard(n):
    H = np.ones((1, 1), np.float32)
    while H.shape[0] < n:
        H = np.block([[H, H], [H, -H]])
    return H


def kernel(**inputs):
    import ml_dtypes
    x0 = np.asarray(inputs["L0_attn_x"], np.float32)
    NCores = 8
    Lh = 1024

    def percore(fn):
        return [fn(c, c // 2, c % 2) for c in range(NCores)]

    xs = percore(lambda c, b, hh: np.ascontiguousarray(
        x0[b].T[:, hh * Lh:(hh + 1) * Lh]))

    w_in_t = np.ascontiguousarray(np.asarray(inputs["L0_attn_W_in"],
                                             np.float32).T)
    conv_w = np.asarray(inputs["L0_attn_conv_w"], np.float32)
    conv_b = np.asarray(inputs["L0_attn_conv_b"], np.float32)
    W_xproj = np.asarray(inputs["L0_attn_W_xproj"], np.float32)
    W_dt = np.asarray(inputs["L0_attn_W_dt"], np.float32)
    b_dt = np.asarray(inputs["L0_attn_b_dt"], np.float32)
    A_log0 = np.asarray(inputs["L0_attn_A_log"], np.float32)
    onehot32 = np.zeros((32, 32 * 128), np.float32)
    for r in range(32):
        onehot32[r, r * 128:(r + 1) * 128] = 1.0
    h0z = np.zeros((D, D_STATE), np.float32)
    halo0 = np.zeros((D, 3), np.float32)
    w_out_t0 = np.ascontiguousarray(np.asarray(inputs["L0_attn_W_out"],
                                               np.float32).T)
    dskip_c = colpack(np.asarray(inputs["L0_attn_Dskip"], np.float32))

    U0 = np.asarray(inputs["L0_ffn_U"], np.float32)
    V0 = np.asarray(inputs["L0_ffn_V"], np.float32)
    sel8 = np.zeros((8, 4 * 128), np.float32)
    for cc in range(4):
        for pp in range(128):
            sel8[cc * 2 + pp // 64, cc * 128 + pp] = 1.0
    ident = np.eye(128, dtype=np.float32)
    iotaj = np.concatenate([np.arange(64), np.arange(64)]).astype(
        np.float32)[:, None]
    ssign = np.concatenate([-np.ones(64), np.ones(64)]).astype(
        np.float32)[:, None]
    maskut = np.triu(np.ones((128, 128), np.float32))
    ones128 = np.ones((128, 1), np.float32)
    z0s = np.zeros((2048, 1), np.float32)
    s0s = np.zeros((2048, 128), np.float32)

    common = {
        "halo": halo0, "conv_w": colpack(conv_w), "conv_b": colpack(conv_b),
        "w_xproj_t": np.ascontiguousarray(W_xproj.T),
        "w_dt_t": np.ascontiguousarray(W_dt.T),
        "b_dt": colpack(b_dt), "a_log": colpack(A_log0),
        "onehot32": onehot32, "h0": h0z,
        "dskip": dskip_c, "w_out_t": w_out_t0,
        "w1_t": np.ascontiguousarray(np.asarray(inputs["L0_ffn_W1"],
                                                np.float32)),
        "b1": colpack(np.asarray(inputs["L0_ffn_b1"], np.float32)),
        "w2_t": np.ascontiguousarray(np.asarray(inputs["L0_ffn_W2"],
                                                np.float32)),
        "b2": np.broadcast_to(np.asarray(inputs["L0_ffn_b2"], np.float32),
                              (128, 8)).copy(),
        "v_t": np.ascontiguousarray(V0.transpose(2, 0, 1).reshape(D, 512)),
        "u_t": np.ascontiguousarray(U0.transpose(0, 2, 1).reshape(512, D)),
        "sel8": sel8, "ident128": ident,
        "wq": np.asarray(inputs["L1_attn_Wq"], np.float32),
        "wk": np.asarray(inputs["L1_attn_Wk"], np.float32),
        "wv": np.asarray(inputs["L1_attn_Wv"], np.float32),
        "wg": np.asarray(inputs["L1_attn_Wg"], np.float32),
        "bg": np.asarray(inputs["L1_attn_bg"], np.float32).reshape(16, 1),
        "wo": np.asarray(inputs["L1_attn_Wo"], np.float32),
        "bo": colpack(np.asarray(inputs["L1_attn_bo"], np.float32)),
        "s0": s0s, "z0": z0s,
        "iotaj": iotaj, "ssign": ssign, "maskut": maskut,
        "ones128": ones128,
    }

    def trow_c(c):
        return (np.arange(Lh, dtype=np.float32) + (c % 2) * Lh)[None, :]

    r1 = _launch("P1", [dict(common, x=xs[c], w_in_t=w_in_t, trow=trow_c(c))
                        for c in range(8)])

    def im_p2(c):
        im = dict(common, x=xs[c], u_pre=r1[c]["u_pre"], z=r1[c]["z"],
                  u_silu=r1[c]["u_silu"], dt=r1[c]["dt"], bc=r1[c]["bc"],
                  trow=trow_c(c))
        if c % 2:
            im["halo"] = np.ascontiguousarray(
                r1[c - 1]["u_pre"][:, Lh - 3:Lh])
            im["h0"] = r1[c - 1]["h_end"]
            im["s0"] = r1[c - 1]["s_end"]
            im["z0"] = r1[c - 1]["z_end"]
        return im

    r2 = _launch("P2", [im_p2(c) for c in range(8)])
    am8 = np.array([[float(r2[c]["amax"][0, 0]) for c in range(8)]],
                   np.float32)

    w_fp8 = np.asarray(inputs["L1_ffn_weight"], np.float32)
    wscale = np.asarray(inputs["L1_ffn_weight_scale"],
                        np.float32).reshape(1, 1)
    bias1 = colpack(np.asarray(inputs["L1_ffn_bias"], np.float32))
    wq2 = np.ascontiguousarray(np.asarray(inputs["L2_attn_w_q"],
                                          np.float32).T)
    wk2 = np.ascontiguousarray(np.asarray(inputs["L2_attn_w_k"],
                                          np.float32).T)
    wv2 = np.ascontiguousarray(np.asarray(inputs["L2_attn_w_v"],
                                          np.float32).T)
    r3 = _launch("P3", [{"x3": r2[c]["x3"], "xn_ln": r2[c]["xn_ln"],
                         "amax8": am8,
                         "w_fp8_t": np.ascontiguousarray(w_fp8.T),
                         "wscale": wscale, "bias": bias1,
                         "wq": wq2, "wk": wk2, "wv": wv2,
                         "s0": s0s, "z0": z0s, "maskut": maskut,
                         "ones128": ones128, "identf": ident}
                        for c in range(8)])

    mems = np.asarray(inputs["L2_attn_memories"], np.float32)
    mnorms = np.asarray(inputs["L2_attn_memory_norms"], np.float32)
    gate2 = np.asarray(inputs["L2_attn_gate"], np.float32)
    diag_mh = np.zeros((64, 16), np.float32)
    for mm in range(4):
        for h in range(16):
            diag_mh[mm * 16 + h, h] = 1.0
    norm2 = np.ascontiguousarray(mnorms.transpose(2, 0, 1).reshape(128, 64))
    mem_t = np.ascontiguousarray(mems.reshape(4 * 16 * 128, 128))
    gate_bc = np.broadcast_to(gate2, (128, 16)).copy()
    wo_t2 = np.ascontiguousarray(np.asarray(inputs["L2_attn_w_o"],
                                            np.float32).T)
    H64 = _hadamard(64)
    H128 = _hadamard(128)
    h64c = np.vstack([H64, H64]).astype(ml_dtypes.bfloat16)
    h128c = H128.astype(ml_dtypes.bfloat16)
    w1f = np.asarray(inputs["L2_ffn_W1"], np.float32)
    w2f = np.asarray(inputs["L2_ffn_W2"], np.float32)
    w3f = np.asarray(inputs["L2_ffn_W3"], np.float32)

    w_in3 = np.asarray(inputs["L3_attn_w_in"], np.float32)
    ln1w = colpack(np.asarray(inputs["L3_attn_ln1_w"], np.float32))
    cw3 = np.asarray(inputs["L3_attn_conv_w"], np.float32)
    cb3 = np.asarray(inputs["L3_attn_conv_b"], np.float32)
    convw64 = np.zeros((64, 64), np.float32)
    convb64 = np.zeros((64, 16), np.float32)
    for gi in range(16):
        convw64[:, gi * 4:(gi + 1) * 4] = cw3[gi * 64:(gi + 1) * 64]
        convb64[:, gi] = cb3[gi * 64:(gi + 1) * 64]
    convw128 = np.zeros((128, 128), np.float32)
    convb128 = np.zeros((128, 32), np.float32)
    for ch in range(32):
        convw128[:, ch * 4:(ch + 1) * 4] = cw3[1024 + ch * 128:
                                               1024 + (ch + 1) * 128]
        convb128[:, ch] = cb3[1024 + ch * 128:1024 + (ch + 1) * 128]
    dtb3 = np.asarray(inputs["L3_attn_dt_bias"], np.float32).reshape(64, 1)
    alog3 = np.asarray(inputs["L3_attn_A_log"], np.float32).reshape(64, 1)
    drow3 = np.asarray(inputs["L3_attn_D"], np.float32).reshape(1, 64)
    cz = np.zeros((4096, 64), np.float32)
    halo3z = np.zeros((5120, 3), np.float32)
    conv3 = {"convw": convw128, "convb": convb128, "convw64": convw64,
             "convb64": convb64, "dtbias": dtb3, "alog": alog3,
             "drow": drow3, "maskut": maskut, "identf": ident}

    z0cz = np.zeros((128, 16), np.float32)

    def im_p45(c):
        peer = c + 1 if c % 2 == 0 else c - 1
        if c % 2 == 0:
            z0c = z0cz
        else:
            z0c = np.zeros((128, 16), np.float32)
            z0c[:, :] = r3[c - 1]["z_end"].reshape(16, 128).T
            z0c = np.ascontiguousarray(z0c)
        return dict(conv3, x4=r3[c]["x4"],
                    s0=s0s if c % 2 == 0 else r3[c - 1]["s_end"],
                    z0c=z0c,
                    sigma_q=r3[c]["sigma_q"], num_loc=r3[c]["num_loc"],
                    den_loc=r3[c]["den_loc"],
                    qbar_a=r3[c]["qbar"], qbar_b=r3[peer]["qbar"],
                    mem_t=mem_t, norm2=norm2, gate_bc=gate_bc,
                    diag_mh=diag_mh, wo_t=wo_t2,
                    w1_t=np.ascontiguousarray(w1f.T),
                    w3_t=np.ascontiguousarray(w3f.T),
                    w2_t=np.ascontiguousarray(w2f.T),
                    h64c=h64c, h128c=h128c,
                    w_in=w_in3, ln1w=ln1w, halo=halo3z, carry_in=cz)

    r5 = _launch("P45", [im_p45(c) for c in range(8)])

    im6c = dict(
                normw=colpack(np.asarray(inputs["L3_attn_norm_w"],
                                         np.float32)),
                ln2w=colpack(np.asarray(inputs["L3_attn_ln2_w"],
                                        np.float32)),
                w_out=np.ascontiguousarray(np.asarray(
                    inputs["L3_attn_w_out"], np.float32)),
                gate_w=np.ascontiguousarray(np.asarray(
                    inputs["L3_attn_gate_w"], np.float32)),
                up_w=np.ascontiguousarray(np.asarray(
                    inputs["L3_attn_up_w"], np.float32)),
                down_w=np.ascontiguousarray(np.asarray(
                    inputs["L3_attn_down_w"], np.float32)),
                w_od=np.ascontiguousarray(np.asarray(
                    inputs["L3_ffn_weight"], np.float32)),
                bias_c=colpack(np.asarray(inputs["L3_ffn_bias"],
                                          np.float32)))

    eh64 = np.zeros((64, 64 * 64), np.float32)
    for hh_ in range(64):
        eh64[hh_, hh_ * 64:(hh_ + 1) * 64] = 1.0

    def im_p6(c):
        im = dict(im6c, dtr=r5[c]["dtr"],
                  zz=r5[c]["zz"], hs=r5[c]["hs"], x6=r5[c]["x6"],
                  carry_in=cz, c_conv=r5[c]["c_conv"],
                  y_zero=r5[c]["y_ssm"], eh64=eh64,
                  dtbias=dtb3, alog=alog3, identf=ident)
        if c % 2:
            im["carry_in"] = r5[c - 1]["carry_out"]
        return im

    r6 = _launch("P6", [im_p6(c) for c in range(8)])

    out = np.empty((4, 2048, 2048), np.float32)
    for c in range(8):
        b, hh = c // 2, c % 2
        out[b, hh * Lh:(hh + 1) * Lh, :] = r6[c]["out"].T
    return out
